# version 06
# baseline (speedup 1.0000x reference)
import math

import numpy as np
import ml_dtypes

import concourse.bass as bass
import concourse.mybir as mybir
from concourse.bass_utils import run_bass_kernel_spmd
from concourse.bass_core_rust import IndirectOffsetOnAxis

F32 = mybir.dt.float32
BF16 = mybir.dt.bfloat16
I32 = mybir.dt.int32
bf16 = ml_dtypes.bfloat16
Act = mybir.ActivationFunctionType
Alu = mybir.AluOpType
AX = mybir.AxisListType

NCORE = 8
H = 2048
NH = 16
HD = 128
S = 8192
B = 2
TOK = 2048
NHT = 16
SC_SCORE = float(np.float32(1.0 / math.sqrt(HD)))
EXP_BIAS = -12.0
EPS = 1e-5

QD_BOUND_A = [0.25, 0.75, 1.25, 1.75]
QD_BOUND_B = [2.5, 3.5]
QD_BOUND_C = 5.0
QD_EMIN, QD_EMAX = -8, 5

GROUPS = [[0, 1, 2, 3], [4, 5, 6, 7]]


def _rope_tables():
    theta = 10000.0
    inv_freq = 1.0 / theta ** (np.arange(0, HD, 2, dtype=np.float32) / HD)
    t = np.arange(S, dtype=np.float32)
    freqs = np.einsum("i,j->ij", t, inv_freq)
    emb = np.concatenate((freqs, freqs), axis=-1)
    return np.cos(emb), np.sin(emb)


def _masks():
    ql = np.arange(512)[None, :]
    kl = np.arange(128)[:, None]
    m = np.concatenate(
        [(ql >= 128 * r + kl).astype(np.float32) for r in range(4)], axis=1
    )
    return m.astype(bf16)


class MK:

    def __init__(self, nc, es):
        self.nc = nc
        self.thunks = {e: [] for e in ("sync", "gpsimd", "tensor", "vector",
                                       "scalar")}
        self.events = {e: [] for e in self.thunks}
        self.pending_incs = []
        self.sems = {}
        self.val = {}
        self.waited = {}
        self.es = es

    def sem(self, name):
        self.sems[name] = self.es.enter_context(self.nc.semaphore(name))
        self.val[name] = 0

    def emit(self, eng, fn):
        self.flush_incs(eng)
        self.thunks[eng].append(fn)

    def wait(self, eng, name, value):
        if value <= 0:
            return
        key = (eng, name)
        if self.waited.get(key, 0) >= value:
            return
        self.waited[key] = value
        self.events[eng].append(("w", name, value))
        h = self.sems[name]
        self.emit(eng, lambda e, h=h, v=value: e.wait_ge(h, v))

    def inc(self, name, n):
        self.val[name] += n
        self._cur_eng_hint = None
        self.pending_incs.append((name, n))
        h = self.sems[name]
        return lambda instr: instr.then_inc(h, n)

    def flush_incs(self, eng):
        for name, n in self.pending_incs:
            self.events[eng].append(("i", name, n))
        self.pending_incs = []

    def check_deadlock(self):
        import collections
        pos = {e: 0 for e in self.events}
        cnt = collections.defaultdict(int)
        progressed = True
        while progressed:
            progressed = False
            for e, evs in self.events.items():
                while pos[e] < len(evs):
                    kind, name, v = evs[pos[e]]
                    if kind == "i":
                        cnt[name] += v
                        pos[e] += 1
                        progressed = True
                    else:
                        if cnt[name] >= v:
                            pos[e] += 1
                            progressed = True
                        else:
                            break
        stuck = {e: (pos[e], len(evs)) for e, evs in self.events.items()
                 if pos[e] < len(evs)}
        if stuck:
            print("DEADLOCK:")
            for e, (p, n) in stuck.items():
                kind, name, v = self.events[e][p]
                print(f"  {e}@{p}/{n}: waiting {name} >= {v} (has {cnt[name]})")
            return False
        print("schedule OK (no deadlock)")
        return True

    def mark(self, name):
        self.marks = getattr(self, "marks", {})
        self.marks[name] = {e: len(t) for e, t in self.thunks.items()}


def build_program():
    import contextlib
    es = contextlib.ExitStack()
    nc = bass.Bass("TRN2", target_bir_lowering=False, debug=False)
    mk = MK(nc, es)

    def din(name, shape, dt=F32):
        return nc.dram_tensor(name, shape, dt, kind="ExternalInput").ap()

    x_sl = din("x_sl", [TOK, H])
    wq_tl = din("wq_tl", [NHT, 128, H])
    wk_tl = din("wk_tl", [NHT, 128, H])
    wv_tl = din("wv_tl", [NHT, 128, H])
    wo_t = din("wo_t", [H, H])
    wffn_sl = din("wffn_sl", [H, 512])
    bias_b = din("bias_b", [128, H])
    cos_t = din("cos_t", [128, TOK])
    sin_t = din("sin_t", [128, TOK])
    masks_i = din("masks_i", [128, 2048], BF16)
    qsel_i = din("qsel_i", [128, 48], I32)
    aosel_i = din("aosel_i", [128, 64], I32)

    y_sl = nc.dram_tensor("y_sl", [TOK, H], F32, kind="ExternalOutput").ap()

    def dint(name, shape, dt=BF16):
        return nc.dram_tensor(name, shape, dt, kind="Internal").ap()

    ln_scr = dint("ln_scr", [TOK, H])
    wqd_scr = dint("wqd_scr", [H, 512])
    wt_in = dint("wt_in", [512, H])
    wt_out = dint("wt_out", [H, H])
    qkv_in = dint("qkv_in", [3 * H, TOK])
    qkv_out = dint("qkv_out", [4 * 3 * H, TOK])
    vsel = dint("vsel", [2 * 512, H])
    ao_in = dint("ao_in", [S, 512])
    ao_out = dint("ao_out", [4 * S, 512])
    aosel = dint("aosel", [2 * TOK, 512])
    a_scr = dint("a_scr", [TOK, H])

    POOL_COLS = 99840
    pool = es.enter_context(nc.sbuf_tensor("pool", [128, POOL_COLS], BF16))
    bias_sb = es.enter_context(nc.sbuf_tensor("bias_sb", [128, H], F32))
    stats = es.enter_context(nc.sbuf_tensor("stats", [128, 64], F32))
    cst_t = es.enter_context(nc.sbuf_tensor("cst_t", [128, 8], F32))
    qsel_sb = es.enter_context(nc.sbuf_tensor("qsel_sb", [128, 48], I32))
    aosel_sb = es.enter_context(nc.sbuf_tensor("aosel_sb", [128, 64], I32))
    psum = es.enter_context(nc.psum_tensor("ps", [128, 4096], F32))

    def bslice(off, n):
        return pool[:, off:off + n]

    def fslice(off, n):
        return pool[:, 2 * off:2 * (off + n)].bitcast(F32)

    R_LNT = 0
    R_LNSTG = 32768
    R_XSTG = 18432
    R_COS = 22528
    R_SIN = 24576
    R_MASK = 53248
    R_WSTG = 55296
    R_QRAW = 59392
    R_QSWP = 63488
    R_T1 = 33792
    R_T2 = 16384
    R_QOUT = 71680
    R_VSTG = 75776
    R_WQSTG = 39936
    R_WQTMP = 40960
    R_WQOUT = 87040
    R_WTT = 89088
    R_WQS = 46592
    R_Q = [0, 8192]
    R_K = [16384, 24576]
    R_V = [32768, 41984]
    R_PT = [51200, 51712]
    R_AOS = [52224, 52736]
    R_DEN = 27648
    R_VG3 = 55424
    R_AOT = 0
    R_AT = 8192
    R_WOS = 16384
    R_WTS = 24576
    R_ASTG = 32768
    R_R1P = [18432, 26624]
    R_LN2 = 34816
    R_TQ = 36864
    R_TA = 38912
    R_TB = 40960
    R_TS = 43008
    R_OUT = [45056, 47104]
    R_QDS = 49152
    R_AOG4 = 98816

    SC = SC_SCORE
    CONSTS = [EXP_BIAS, SC]
    for i, v in enumerate(CONSTS):
        nc.const_aps.register(F32, v, cst_t[:, i:i + 1])

    for s in ("CS", "PR", "X1", "SQ1", "LS", "LA", "LW", "LT",
              "WQL", "WQD", "QDA", "WQW", "WTT", "WTI", "AG1",
              "WS", "PQ", "EV", "SW", "RP", "QKW", "VW", "AG2",
              "HLQ", "VG", "VSW", "VTR", "STm", "EX", "MKm", "PF", "NR",
              "AO", "AG3",
              "AOG", "AOW", "AOT", "X2", "OPT", "OP", "R1", "ST2", "SQ2",
              "LN2A", "QDV", "AW", "AT", "WOL", "WFL", "FFT", "FF", "FO",
              "YW", "VSS"):
        mk.sem(s)
    _vss_n = [0]

    def vsync(v, ins):
        _vss_n[0] += 1
        ins.then_inc(mk.sems["VSS"], 1)
        v.wait_ge(mk.sems["VSS"], _vss_n[0])

    E, W, I = mk.emit, mk.wait, mk.inc

    def emit_qd(ncols, src_f32, dst_bf16, r_small, r_q, r_a, r_b, r_s,
                v_pre_waits, done_inc):
        nb = ncols // 32
        for (sname, v) in v_pre_waits:
            W("vector", sname, v)

        def _(v, src=src_f32, nb=nb):
            amax = fslice(r_small, nb)
            scale = fslice(r_small + nb, nb)
            src3 = src.rearrange("p (a b) -> p a b", b=32)
            vsync(v, v.tensor_reduce(amax.unsqueeze(2), src3, AX.X, Alu.max,
                                     apply_absolute_value=True))
            v.memset(scale, 2.0 ** (QD_EMIN - 3))
            es_ = list(range(QD_EMIN, QD_EMAX + 1))
            gbuf = [fslice(r_small + 3 * nb, nb), fslice(r_q, nb)]
            for i, e in enumerate(es_):
                v.tensor_scalar(gbuf[i % 2], amax, float(2.0 ** e),
                                float(2.0 ** (e - 3)), Alu.is_ge, Alu.mult)
                if i >= 1:
                    v.tensor_tensor(scale, scale, gbuf[(i - 1) % 2], Alu.add)
            vsync(v, v.tensor_tensor(scale, scale,
                                     gbuf[(len(es_) - 1) % 2], Alu.add))
            rscale = fslice(r_small + 2 * nb, nb)
            vsync(v, v.reciprocal(rscale, scale))
            q = fslice(r_q, ncols)
            rs_b = rscale.unsqueeze(2).broadcast_to([128, nb, 32])
            v.tensor_tensor(q.rearrange("p (a b) -> p a b", b=32), src3, rs_b,
                            Alu.mult)
        E("vector", _)
        qfn = I("QDA", 1)

        def _(v, fn=qfn):
            fn(v.engine_nop())
        E("vector", _)
        q_ready = mk.val["QDA"]
        W("scalar", "QDA", q_ready)
        afn = I("QDA", 1)

        def _(sc, fn=afn):
            q = fslice(r_q, ncols)
            sc.activation(fslice(r_a, ncols), q, Act.Abs)
            fn(sc.activation(fslice(r_s, ncols), q, Act.Sign))
        E("scalar", _)
        as_ready = mk.val["QDA"]
        W("vector", "QDA", as_ready)
        dfn = I(done_inc, 1)

        def _(v, nb=nb, fn=dfn):
            a = fslice(r_a, ncols)
            A = fslice(r_q, ncols)
            bb = fslice(r_b, ncols)
            sg = fslice(r_s, ncols)
            scale = fslice(r_small + nb, nb)
            sc_b = scale.unsqueeze(2).broadcast_to([128, nb, 32])
            v.tensor_single_scalar(A, a, QD_BOUND_A[0], Alu.is_ge)
            for bnd in QD_BOUND_A[1:]:
                v.scalar_tensor_tensor(A, a, bnd, A, Alu.is_ge, Alu.add)
            v.tensor_single_scalar(bb, a, QD_BOUND_B[0], Alu.is_ge)
            v.scalar_tensor_tensor(bb, a, QD_BOUND_B[1], bb, Alu.is_ge, Alu.add)
            v.scalar_tensor_tensor(bb, A, 0.5, bb, Alu.mult, Alu.add)
            v.tensor_scalar(a, a, QD_BOUND_C, 2.0, Alu.is_ge, Alu.mult)
            v.tensor_tensor(bb, bb, a, Alu.add)
            v.tensor_tensor(bb, bb, sg, Alu.mult)
            fn(v.tensor_tensor(dst_bf16.rearrange("p (a b) -> p a b", b=32),
                               bb.rearrange("p (a b) -> p a b", b=32), sc_b,
                               Alu.mult))
        E("vector", _)

    def _(v):
        for i, c in enumerate(CONSTS):
            v.memset(cst_t[:, i:i + 1], c)
    E("vector", _)
    csfn = I("CS", 1)

    def _(v, fn=csfn):
        fn(v.engine_nop())
    E("vector", _)

    prfn = [I("PR", 16) for _ in range(6)]

    def _(s, fns=prfn):
        fns[0](s.dma_start(out=fslice(R_COS, H), in_=cos_t))
        fns[1](s.dma_start(out=fslice(R_SIN, H), in_=sin_t))
        fns[2](s.dma_start(out=bslice(R_MASK, 2048), in_=masks_i))
        fns[3](s.dma_start(out=bias_sb[:], in_=bias_b))
        fns[4](s.dma_start(out=qsel_sb[:], in_=qsel_i))
        fns[5](s.dma_start(out=aosel_sb[:], in_=aosel_i))
    E("sync", _)
    PR_ALL = mk.val["PR"]

    mk.mark("prep")
    for i in range(16):
        bp = i % 2
        W("scalar", "WQD", i - 1)
        fn = I("WQL", 16)

        def _(s, i=i, bp=bp, fn=fn):
            fn(s.dma_start(out=fslice(R_WQSTG + bp * 512, 512),
                           in_=wffn_sl[128 * i:128 * (i + 1), :]))
        E("scalar", _)
        emit_qd(512,
                fslice(R_WQSTG + bp * 512, 512),
                bslice(R_WQOUT + bp * 512, 512),
                R_WQS,
                R_WQTMP, R_WQTMP + 512, R_WQTMP + 1024, R_WQTMP + 1536,
                v_pre_waits=[("CS", 1), ("WQL", 16 * (i + 1)),
                             ("WQW", 16 * (i - 1))],
                done_inc="WQD")
        if i >= 1:
            W("scalar", "WQD", i)
            fn = I("WQW", 16)

            def _(s, i=i, fn=fn):
                fn(s.dma_start(out=wqd_scr[128 * (i - 1):128 * i, :],
                               in_=bslice(R_WQOUT + ((i - 1) % 2) * 512, 512)))
            E("scalar", _)
    W("scalar", "WQD", 16)
    fn = I("WQW", 16)

    def _(s, fn=fn):
        fn(s.dma_start(out=wqd_scr[128 * 15:128 * 16, :],
                       in_=bslice(R_WQOUT + 512, 512)))
    E("scalar", _)
    W("sync", "WQW", 256)
    for k in range(4):
        W("sync", "WTI", 16 * (k - 1))
        fn = I("WTT", 16)

        def _(s, k=k, fn=fn):
            fn(s.dma_start_transpose(out=bslice(R_WTT + (k % 2) * 2048, 2048),
                                     in_=wqd_scr[:, 128 * k:128 * (k + 1)]))
        E("sync", _)
        W("sync", "WTT", 16 * (k + 1))
        fn = I("WTI", 16)

        def _(s, k=k, fn=fn):
            fn(s.dma_start(out=wt_in[128 * k:128 * (k + 1), :],
                           in_=bslice(R_WTT + (k % 2) * 2048, 2048)))
        E("sync", _)

    mk.mark("wqd")
    for t in range(16):
        bp = t % 2
        W("sync", "LA", t - 1)
        fn = I("X1", 16)

        def _(s, t=t, bp=bp, fn=fn):
            fn(s.dma_start(out=fslice(R_XSTG + bp * 2048, 2048),
                           in_=x_sl[128 * t:128 * (t + 1), :]))
        E("sync", _)
        if t >= 1:
            W("sync", "LA", t)
            fn = I("LW", 16)

            def _(s, t=t, fn=fn):
                fn(s.dma_start(out=ln_scr[128 * (t - 1):128 * t, :],
                               in_=bslice(R_LNSTG + ((t - 1) % 2) * 2048, 2048)))
            E("sync", _)
    W("sync", "LA", 16)
    fn = I("LW", 16)

    def _(s, fn=fn):
        fn(s.dma_start(out=ln_scr[128 * 15:128 * 16, :],
                       in_=bslice(R_LNSTG + 2048, 2048)))
    E("sync", _)
    for t in range(16):
        bp = t % 2
        W("scalar", "X1", 16 * (t + 1))
        W("scalar", "LS", 2 * t - 3)
        fn = I("SQ1", 1)

        def _(sc, bp=bp, fn=fn):
            fn(sc.activation(fslice(R_T1, 2048), fslice(R_XSTG + bp * 2048, 2048),
                             Act.Square,
                             accum_out=stats[:, 8 * bp + 1:8 * bp + 2]))
        E("scalar", _)
        sq1v = mk.val["SQ1"]
        W("vector", "X1", 16 * (t + 1))

        def _(v, bp=bp):
            vsync(v, v.reduce_sum(stats[:, 8 * bp:8 * bp + 1],
                                  fslice(R_XSTG + bp * 2048, 2048), AX.X))
        E("vector", _)
        W("vector", "SQ1", sq1v)
        W("vector", "SQ2", t - 1)
        fn = I("LS", 1)

        def _(v, bp=bp, fn=fn):
            st = stats[:, 8 * bp:8 * (bp + 1)]
            vsync(v, v.tensor_single_scalar(st[:, 0:1], st[:, 0:1], 1.0 / H,
                                            Alu.mult))
            vsync(v, v.tensor_single_scalar(st[:, 1:2], st[:, 1:2], 1.0 / H,
                                            Alu.mult))
            vsync(v, v.tensor_tensor(st[:, 2:3], st[:, 0:1], st[:, 0:1],
                                     Alu.mult))
            vsync(v, v.tensor_tensor(st[:, 2:3], st[:, 1:2], st[:, 2:3],
                                     Alu.subtract))
            fn(v.tensor_single_scalar(st[:, 3:4], st[:, 2:3], EPS, Alu.add))
        E("vector", _)
        ls1 = mk.val["LS"]
        W("scalar", "LS", ls1)
        fn = I("SQ2", 1)

        def _(sc, bp=bp, fn=fn):
            st = stats[:, 8 * bp:8 * (bp + 1)]
            fn(sc.activation(st[:, 4:5], st[:, 3:4], Act.Sqrt))
        E("scalar", _)
        sq2v = mk.val["SQ2"]
        W("vector", "SQ2", sq2v)
        W("vector", "LA", t - 1)
        fn = I("LS", 1)

        def _(v, bp=bp, fn=fn):
            st = stats[:, 8 * bp:8 * (bp + 1)]
            vsync(v, v.reciprocal(st[:, 5:6], st[:, 4:5]))
            vsync(v, v.tensor_tensor(st[:, 5:6], st[:, 3:4], st[:, 5:6],
                                     Alu.mult))
            vsync(v, v.tensor_tensor(st[:, 4:5], st[:, 4:5], st[:, 5:6],
                                     Alu.add))
            vsync(v, v.tensor_single_scalar(st[:, 4:5], st[:, 4:5], 0.5,
                                            Alu.mult))
            vsync(v, v.reciprocal(st[:, 5:6], st[:, 4:5]))
            vsync(v, v.tensor_tensor(st[:, 6:7], st[:, 0:1], st[:, 5:6],
                                     Alu.mult))
            fn(v.tensor_single_scalar(st[:, 6:7], st[:, 6:7], -1.0, Alu.mult))
        E("vector", _)
        ls2 = mk.val["LS"]
        W("scalar", "LS", ls2)
        W("scalar", "LW", 16 * (t - 1))
        fn = I("LA", 1)

        def _(sc, bp=bp, fn=fn):
            st = stats[:, 8 * bp:8 * (bp + 1)]
            fn(sc.activation(bslice(R_LNSTG + bp * 2048, 2048),
                             fslice(R_XSTG + bp * 2048, 2048), Act.Identity,
                             bias=st[:, 6:7], scale=st[:, 5:6]))
        E("scalar", _)
    W("sync", "LW", 256)
    for h in range(NHT):
        fn = I("LT", 16)

        def _(s, h=h, fn=fn):
            fn(s.dma_start_transpose(out=bslice(R_LNT + h * 2048, 2048),
                                     in_=ln_scr[:, 128 * h:128 * (h + 1)]))
        E("sync", _)

    mk.mark("ln")
    WTILED = {0: wq_tl, 1: wk_tl, 2: wv_tl}
    qk_idx = 0
    v_idx = 0
    pending_qk_write = None
    for d in range(NHT):
        for p in range(3):
            g = 3 * d + p
            gp = g % 2
            W("gpsimd", "PQ", g - 1)
            fn = I("WS", 16)

            def _(gg, d=d, p=p, gp=gp, fn=fn):
                fn(gg.dma_start(out=bslice(R_WSTG + gp * 2048, 2048),
                                in_=WTILED[p][d]))
            E("gpsimd", _)
            W("tensor", "LT", 256)
            W("tensor", "WS", 16 * (g + 1))
            W("tensor", "EV", g - 1)
            fn = I("PQ", 1)

            def _(te, gp=gp, fn=fn):
                reg = psum[:, 2048 * gp:2048 * (gp + 1)]
                wst = bslice(R_WSTG + gp * 2048, 2048)
                ins = None
                for ht in range(NHT):
                    lhsT = wst[:, 128 * ht:128 * (ht + 1)]
                    lnt = bslice(R_LNT + ht * 2048, 2048)
                    for ch in range(4):
                        ins = te.matmul(reg[:, 512 * ch:512 * (ch + 1)], lhsT,
                                        lnt[:, 512 * ch:512 * (ch + 1)],
                                        start=(ht == 0), stop=(ht == NHT - 1))
                fn(ins)
            E("tensor", _)
            W("scalar", "PQ", g + 1)
            if p < 2:
                qi = qk_idx
                W("scalar", "RP", qi - 1)
                fn = I("EV", 1)

                def _(sc, gp=gp, qi=qi, fn=fn):
                    fn(sc.activation(bslice(R_QRAW + (qi % 2) * 2048, 2048),
                                     psum[:, 2048 * gp:2048 * (gp + 1)],
                                     Act.Copy))
                E("scalar", _)
                W("sync", "EV", mk.val["EV"])
                fn1, fn2 = I("SW", 16), I("SW", 16)

                def _(s, qi=qi, fn1=fn1, fn2=fn2):
                    qr = bslice(R_QRAW + (qi % 2) * 2048, 2048)
                    qs = bslice(R_QSWP + (qi % 2) * 2048, 2048)
                    fn1(s.dma_start(out=qs[0:64, :], in_=qr[64:128, :]))
                    fn2(s.dma_start(out=qs[64:128, :], in_=qr[0:64, :]))
                E("sync", _)
                if pending_qk_write is not None:
                    pqi, prow0, prp = pending_qk_write
                    W("sync", "RP", prp)
                    fn = I("QKW", 16)

                    def _(s, pqi=pqi, prow0=prow0, fn=fn):
                        fn(s.dma_start(out=qkv_in[prow0:prow0 + 128, :],
                                       in_=bslice(R_QOUT + (pqi % 2) * 2048,
                                                  2048)))
                    E("sync", _)
                W("vector", "PR", PR_ALL)
                W("vector", "SW", 32 * (qi + 1))
                W("vector", "QKW", 16 * (qi - 1))
                fn = I("RP", 1)

                def _(v, qi=qi, fn=fn):
                    qr = bslice(R_QRAW + (qi % 2) * 2048, 2048)
                    qs = bslice(R_QSWP + (qi % 2) * 2048, 2048)
                    t1 = fslice(R_T1, 2048)
                    t2 = fslice(R_T2, 2048)
                    v.tensor_tensor(t1, qr, fslice(R_COS, 2048), Alu.mult)
                    v.tensor_tensor(t2, qs, fslice(R_SIN, 2048), Alu.mult)
                    fn(v.tensor_tensor(bslice(R_QOUT + (qi % 2) * 2048, 2048),
                                       t1, t2, Alu.add))
                E("vector", _)
                row0 = 2048 * p + 128 * d
                pending_qk_write = (qi, row0, mk.val["RP"])
                qk_idx += 1
            else:
                vi = v_idx
                W("scalar", "VW", 16 * (vi - 1))
                fn = I("EV", 1)

                def _(sc, gp=gp, vi=vi, fn=fn):
                    fn(sc.activation(bslice(R_VSTG + (vi % 2) * 2048, 2048),
                                     psum[:, 2048 * gp:2048 * (gp + 1)],
                                     Act.Copy))
                E("scalar", _)
                W("sync", "EV", mk.val["EV"])
                row0 = 4096 + 128 * d
                fn = I("VW", 16)

                def _(s, vi=vi, row0=row0, fn=fn):
                    fn(s.dma_start(out=qkv_in[row0:row0 + 128, :],
                                   in_=bslice(R_VSTG + (vi % 2) * 2048, 2048)))
                E("sync", _)
                v_idx += 1
    pqi, prow0, prp = pending_qk_write
    W("sync", "RP", prp)
    fn = I("QKW", 16)

    def _(s, pqi=pqi, prow0=prow0, fn=fn):
        fn(s.dma_start(out=qkv_in[prow0:prow0 + 128, :],
                       in_=bslice(R_QOUT + (pqi % 2) * 2048, 2048)))
    E("sync", _)
    assert mk.val["QKW"] == 16 * 32 and mk.val["VW"] == 16 * 16

    W("gpsimd", "WTI", 64)
    fn = I("AG1", 1)

    def _(gg, fn=fn):
        fn(gg.collective_compute("AllGather", Alu.bypass, replica_groups=GROUPS,
                                 ins=[wt_in], outs=[wt_out]))
    E("gpsimd", _)
    W("gpsimd", "QKW", 512)
    W("gpsimd", "VW", 256)
    fn = I("AG2", 1)

    def _(gg, fn=fn):
        fn(gg.collective_compute("AllGather", Alu.bypass, replica_groups=GROUPS,
                                 ins=[qkv_in], outs=[qkv_out]))
    E("gpsimd", _)

    mk.mark("proj")
    W("vector", "AG2", 1)
    onesdone = I("CS", 1)

    def _(v, fn=onesdone):
        for hp in range(2):
            for kt in range(64):
                v.memset(bslice(R_V[hp] + 144 * kt + 128, 1), 1.0)
        fn(v.engine_nop())
    E("vector", _)
    ONES_DONE = mk.val["CS"]

    KT_PER_HEAD = sum(4 * (qb + 1) for qb in range(16))
    st_cum = ex_cum = mkm_cum = pf_cum = nr_cum = ao_cum = 0
    vg_idx = 0

    for hi in range(4):
        hp = hi % 2
        W("gpsimd", "AG2", 1)
        W("gpsimd", "PR", PR_ALL)
        if hi >= 2:
            W("gpsimd", "STm", KT_PER_HEAD * (hi - 1))
            W("gpsimd", "PF", KT_PER_HEAD * (hi - 1))
            W("gpsimd", "CS", ONES_DONE)
        for r in range(4):
            fn1, fn2 = I("HLQ", 16), I("HLQ", 16)

            def _(gg, hi=hi, hp=hp, r=r, fn1=fn1, fn2=fn2):
                fn1(gg.indirect_dma_start(
                    out=bslice(R_Q[hp] + 2048 * r, 2048), out_offset=None,
                    in_=qkv_out,
                    in_offset=IndirectOffsetOnAxis(
                        qsel_sb[:, 12 * hi + r:12 * hi + r + 1], 0)))
                fn2(gg.indirect_dma_start(
                    out=bslice(R_K[hp] + 2048 * r, 2048), out_offset=None,
                    in_=qkv_out,
                    in_offset=IndirectOffsetOnAxis(
                        qsel_sb[:, 12 * hi + 4 + r:12 * hi + 4 + r + 1], 0)))
            E("gpsimd", _)
        for r in range(4):
            vg = vg_idx
            W("gpsimd", "VSW", 16 * (vg - 1))
            fn = I("VG", 16)

            def _(gg, hi=hi, r=r, vg=vg, fn=fn):
                fn(gg.indirect_dma_start(
                    out=bslice(R_VG3 + (vg % 2) * 2048, 2048), out_offset=None,
                    in_=qkv_out,
                    in_offset=IndirectOffsetOnAxis(
                        qsel_sb[:, 12 * hi + 8 + r:12 * hi + 8 + r + 1], 0)))
            E("gpsimd", _)
            W("gpsimd", "VG", 16 * (vg + 1))
            if hi >= 2:
                W("gpsimd", "VTR", 1024 * (hi - 1))
            fn = I("VSW", 16)

            def _(gg, hp=hp, r=r, vg=vg, fn=fn):
                fn(gg.dma_start(out=vsel[512 * hp + 128 * r:
                                         512 * hp + 128 * (r + 1), :],
                                in_=bslice(R_VG3 + (vg % 2) * 2048, 2048)))
            E("gpsimd", _)
            vg_idx += 1
        W("sync", "VSW", 64 * (hi + 1))
        W("sync", "CS", ONES_DONE)
        if hi >= 2:
            W("sync", "PF", KT_PER_HEAD * (hi - 1))
        for r in range(4):
            for j in range(16):
                fn = I("VTR", 16)

                def _(s, hp=hp, r=r, j=j, fn=fn):
                    kt = 16 * r + j
                    fn(s.dma_start_transpose(
                        out=bslice(R_V[hp] + 144 * kt, 128),
                        in_=vsel[512 * hp + 128 * r:512 * hp + 128 * (r + 1),
                                 128 * j:128 * (j + 1)]))
                E("sync", _)

        for qb in range(16):
            qbg = 16 * hi + qb
            nkt = 4 * (qb + 1)
            for kt in range(nkt):
                ktg = st_cum
                W("tensor", "HLQ", 128 * (hi + 1))
                W("tensor", "VTR", 1024 * (hi + 1))
                W("tensor", "EX", ktg - 1)
                fn = I("STm", 1)

                def _(te, hp=hp, qb=qb, kt=kt, ktg=ktg, fn=fn):
                    bank = psum[:, 512 * (ktg % 2):512 * (ktg % 2 + 1)]
                    fn(te.matmul(bank, bslice(R_K[hp] + 128 * kt, 128),
                                 bslice(R_Q[hp] + 512 * qb, 512),
                                 start=True, stop=True))
                E("tensor", _)
                st_cum += 1
                W("scalar", "STm", st_cum)
                W("scalar", "PF", ktg - 1)
                fn = I("EX", 1)

                def _(sc, ktg=ktg, fn=fn):
                    bank = psum[:, 512 * (ktg % 2):512 * (ktg % 2 + 1)]
                    fn(sc.activation(bslice(R_PT[ktg % 2], 512), bank, Act.Exp,
                                     bias=EXP_BIAS, scale=SC))
                E("scalar", _)
                ex_cum += 1
                diag = kt >= 4 * qb
                if diag:
                    rel = kt - 4 * qb
                    W("vector", "EX", ex_cum)
                    fn = I("MKm", 1)

                    def _(v, ktg=ktg, rel=rel, fn=fn):
                        pt = bslice(R_PT[ktg % 2], 512)
                        fn(v.tensor_tensor(pt, pt,
                                           bslice(R_MASK + 512 * rel, 512),
                                           Alu.mult))
                    E("vector", _)
                    mkm_cum += 1
                    W("tensor", "MKm", mkm_cum)
                else:
                    W("tensor", "EX", ex_cum)
                if kt == 0:
                    W("tensor", "NR", qbg - 1)
                fn = I("PF", 1)

                def _(te, hp=hp, kt=kt, ktg=ktg, nkt=nkt, qbg=qbg, fn=fn):
                    reg0 = 1024 + 1024 * (qbg % 2)
                    pt = bslice(R_PT[ktg % 2], 512)
                    vv = bslice(R_V[hp] + 144 * kt, 129)
                    ins = None
                    for j in range(4):
                        col = reg0 + 512 * (j // 2) + 256 * (j % 2)
                        ins = te.matmul(
                            psum[:, col:col + 129],
                            pt[:, 128 * j:128 * (j + 1)], vv,
                            start=(kt == 0 and j % 2 == 0),
                            stop=(kt == nkt - 1 and j % 2 == 1))
                    fn(ins)
                E("tensor", _)
                pf_cum += 1
            W("vector", "PF", pf_cum)
            W("vector", "AO", 64 * (qbg - 1))
            fn = I("NR", 1)

            def _(v, qbg=qbg, fn=fn):
                reg0 = 1024 + 1024 * (qbg % 2)
                den = fslice(R_DEN, 32)
                aos = bslice(R_AOS[qbg % 2], 512)
                ins = None
                for j in range(4):
                    col = reg0 + 512 * (j // 2) + 256 * (j % 2)
                    ins = v.reciprocal(den[:, j:j + 1],
                                       psum[:, col + 128:col + 129])
                vsync(v, ins)
                for j in range(4):
                    col = reg0 + 512 * (j // 2) + 256 * (j % 2)
                    ins = v.tensor_single_scalar(
                        aos[:, 128 * j:128 * (j + 1)],
                        psum[:, col:col + 128],
                        den[:, j:j + 1], Alu.mult)
                fn(ins)
            E("vector", _)
            nr_cum += 1
            W("sync", "NR", nr_cum)
            fns = [I("AO", 16) for _ in range(4)]

            def _(s, hi=hi, qb=qb, qbg=qbg, fns=fns):
                aos = bslice(R_AOS[qbg % 2], 512)
                for j in range(4):
                    fns[j](s.dma_start(
                        out=ao_in[512 * qb + 128 * j:512 * qb + 128 * (j + 1),
                                  128 * hi:128 * (hi + 1)],
                        in_=aos[:, 128 * j:128 * (j + 1)]))
            E("sync", _)
            ao_cum += 64

    W("gpsimd", "AO", ao_cum)
    fn = I("AG3", 1)

    def _(gg, fn=fn):
        fn(gg.collective_compute("AllGather", Alu.bypass, replica_groups=GROUPS,
                                 ins=[ao_in], outs=[ao_out]))
    E("gpsimd", _)

    mk.mark("attn")
    import os as _os2
    P4N = int(_os2.environ.get("MK_P4_N", "4"))
    pe_groups = []
    for st4 in range(P4N):
        pe_groups.append(("op", st4, 0))
        pe_groups.append(("op", st4, 1))
        if st4 >= 1:
            pe_groups.append(("ffn", st4 - 1, 0))
            pe_groups.append(("ffn", st4 - 1, 1))
    pe_groups.append(("ffn", P4N - 1, 0))
    pe_groups.append(("ffn", P4N - 1, 1))

    W("gpsimd", "AG3", 1)
    aog_idx = 0
    wo_steps = 0
    wt_steps = 0
    for kind, st4, pr in pe_groups:
        if kind == "op" and pr == 0:
            if st4 >= 2:
                W("gpsimd", "AOT", 256 * (st4 - 1))
            for r in range(4):
                for q4 in range(4):
                    ag = aog_idx
                    W("gpsimd", "AOW", 16 * (ag - 1))
                    fn = I("AOG", 16)

                    def _(gg, st4=st4, r=r, q4=q4, ag=ag, fn=fn):
                        fn(gg.indirect_dma_start(
                            out=bslice(R_AOG4 + (ag % 2) * 512, 512),
                            out_offset=None,
                            in_=ao_out,
                            in_offset=IndirectOffsetOnAxis(
                                aosel_sb[:, 16 * st4 + 4 * r + q4:
                                         16 * st4 + 4 * r + q4 + 1], 0)))
                    E("gpsimd", _)
                    W("gpsimd", "AOG", 16 * (ag + 1))
                    fn = I("AOW", 16)

                    def _(gg, st4=st4, r=r, q4=q4, ag=ag, fn=fn):
                        row0 = 2048 * (st4 % 2) + 512 * r + 128 * q4
                        fn(gg.dma_start(
                            out=aosel[row0:row0 + 128, :],
                            in_=bslice(R_AOG4 + (ag % 2) * 512, 512)))
                    E("gpsimd", _)
                    aog_idx += 1
            if st4 == 0:
                mk.mark("p4ga")
        for k in range(16):
            if kind == "op":
                W("gpsimd", "OPT", wo_steps - 3)
                slot = wo_steps % 4
                fn = I("WOL", 16)

                def _(gg, k=k, slot=slot, fn=fn):
                    fn(gg.dma_start(out=bslice(R_WOS + 2048 * slot, 2048),
                                    in_=wo_t[128 * k:128 * (k + 1), :]))
                E("gpsimd", _)
                wo_steps += 1
                if wo_steps == 16:
                    mk.mark("p4gb")
            else:
                W("gpsimd", "FFT", wt_steps - 3)
                slot = wt_steps % 4
                fn = I("WFL", 16)

                def _(gg, k=k, slot=slot, fn=fn):
                    fn(gg.dma_start(out=bslice(R_WTS + 2048 * slot, 2048),
                                    in_=wt_out[128 * k:128 * (k + 1), :]))
                E("gpsimd", _)
                wt_steps += 1
    mk.mark("p4gpsimd")

    W("sync", "AG3", 1)
    for st4 in range(P4N):
        for tt in range(4):
            W("sync", "FO", 4 * (st4 - 2) + tt + 1)
            fn = I("X2", 16)

            def _(s, st4=st4, tt=tt, fn=fn):
                fn(s.dma_start(out=fslice(R_R1P[st4 % 2] + tt * 2048, 2048),
                               in_=x_sl[512 * st4 + 128 * tt:
                                        512 * st4 + 128 * (tt + 1), :]))
            E("sync", _)
        W("sync", "OPT", 32 * st4)
        W("sync", "AOW", 16 * 16 * (st4 + 1))
        for r in range(4):
            for ht in range(4):
                fn = I("AOT", 16)

                def _(s, st4=st4, r=r, ht=ht, fn=fn):
                    slot = 4 * r + ht
                    base = 2048 * (st4 % 2) + 512 * r
                    fn(s.dma_start_transpose(
                        out=bslice(R_AOT + 512 * slot, 512),
                        in_=aosel[base:base + 512, 128 * ht:128 * (ht + 1)]))
                E("sync", _)
        for tt in range(4):
            g4 = 4 * st4 + tt
            W("sync", "QDV", g4 + 1)
            fn = I("AW", 16)

            def _(s, st4=st4, tt=tt, fn=fn):
                fn(s.dma_start(
                    out=a_scr[512 * st4 + 128 * tt:512 * st4 + 128 * (tt + 1), :],
                    in_=bslice(R_ASTG + (tt % 2) * 2048, 2048)))
            E("sync", _)
        W("sync", "AW", 64 * (st4 + 1))
        W("sync", "FFT", 32 * st4)
        for k in range(16):
            fn = I("AT", 16)

            def _(s, st4=st4, k=k, fn=fn):
                fn(s.dma_start_transpose(
                    out=bslice(R_AT + 512 * k, 512),
                    in_=a_scr[512 * st4:512 * (st4 + 1),
                              128 * k:128 * (k + 1)]))
            E("sync", _)
        if st4 >= 1:
            for tt in range(4):
                W("sync", "FO", 4 * (st4 - 1) + tt + 1)
                fn = I("YW", 16)

                def _(s, st4=st4, tt=tt, fn=fn):
                    fn(s.dma_start(
                        out=y_sl[512 * (st4 - 1) + 128 * tt:
                                 512 * (st4 - 1) + 128 * (tt + 1), :],
                        in_=fslice(R_OUT[tt % 2], 2048)))
                E("sync", _)
    for tt in range(4):
        W("sync", "FO", 4 * (P4N - 1) + tt + 1)
        fn = I("YW", 16)

        def _(s, tt=tt, fn=fn):
            fn(s.dma_start(out=y_sl[512 * (P4N - 1) + 128 * tt:
                                    512 * (P4N - 1) + 128 * (tt + 1), :],
                           in_=fslice(R_OUT[tt % 2], 2048)))
        E("sync", _)

    mk.mark("p4sync")
    group_done = []
    wo_pe = 0
    wt_pe = 0
    for gi, (kind, st4, pr) in enumerate(pe_groups):
        if gi >= 1:
            sname, sval = group_done[gi - 1]
            W("tensor", sname, sval)
        if kind == "op":
            W("tensor", "AOT", 256 * (st4 + 1))
            for k in range(16):
                W("tensor", "WOL", 16 * (wo_pe + 1))
                fn = I("OPT", 1)

                def _(te, k=k, pr=pr, slot=wo_pe % 4, fn=fn):
                    wst = bslice(R_WOS + 2048 * slot, 2048)
                    ins = None
                    for ti in range(2):
                        tt = 2 * pr + ti
                        reg = psum[:, 2048 * ti:2048 * (ti + 1)]
                        for ch in range(4):
                            ins = te.matmul(
                                reg[:, 512 * ch:512 * (ch + 1)],
                                bslice(R_AOT + 512 * k + 128 * tt, 128),
                                wst[:, 512 * ch:512 * (ch + 1)],
                                start=(k == 0), stop=(k == 15))
                    fn(ins)
                E("tensor", _)
                wo_pe += 1
            group_done.append(("R1", 4 * st4 + 2 * pr + 2))
        else:
            W("tensor", "AT", 256 * (st4 + 1))
            for k in range(16):
                W("tensor", "WFL", 16 * (wt_pe + 1))
                fn = I("FFT", 1)

                def _(te, k=k, pr=pr, slot=wt_pe % 4, fn=fn):
                    wst = bslice(R_WTS + 2048 * slot, 2048)
                    ins = None
                    for ti in range(2):
                        tt = 2 * pr + ti
                        reg = psum[:, 2048 * ti:2048 * (ti + 1)]
                        for ch in range(4):
                            ins = te.matmul(
                                reg[:, 512 * ch:512 * (ch + 1)],
                                bslice(R_AT + 512 * k + 128 * tt, 128),
                                wst[:, 512 * ch:512 * (ch + 1)],
                                start=(k == 0), stop=(k == 15))
                    fn(ins)
                E("tensor", _)
                wt_pe += 1
            group_done.append(("FO", 4 * st4 + 2 * pr + 2))

    op_done_at = {}
    ff_done_at = {}
    opv = ffv = 0
    for kind, st4, pr in pe_groups:
        if kind == "op":
            opv += 16
            op_done_at[(st4, 2 * pr)] = opv
            op_done_at[(st4, 2 * pr + 1)] = opv
        else:
            ffv += 16
            ff_done_at[(st4, 2 * pr)] = ffv
            ff_done_at[(st4, 2 * pr + 1)] = ffv

    mk.mark("p4pe")
    for st4 in range(P4N):
        for tt in range(4):
            g4 = 4 * st4 + tt
            bp = tt % 2
            ti = tt % 2
            W("vector", "OPT", op_done_at[(st4, tt)])
            W("vector", "X2", 16 * (g4 + 1))
            fn = I("R1", 1)

            def _(v, ti=ti, tt=tt, st4=st4, fn=fn):
                reg = psum[:, 2048 * ti:2048 * (ti + 1)]
                r1 = fslice(R_R1P[st4 % 2] + tt * 2048, 2048)
                fn(v.tensor_tensor(r1, reg, r1, Alu.add))
            E("vector", _)
            r1v = mk.val["R1"]

            def _(v, tt=tt, bp=bp, st4=st4):
                vsync(v, v.reduce_sum(stats[:, 32 + 8 * bp:32 + 8 * bp + 1],
                                      fslice(R_R1P[st4 % 2] + tt * 2048, 2048),
                                      AX.X))
            E("vector", _)
            W("scalar", "R1", r1v)
            W("scalar", "ST2", 2 * g4 - 3)
            W("scalar", "QDA", 2 * g4 + 31)
            fn = I("SQ1", 1)

            def _(sc, tt=tt, bp=bp, st4=st4, fn=fn):
                fn(sc.activation(fslice(R_LN2, 2048),
                                 fslice(R_R1P[st4 % 2] + tt * 2048, 2048),
                                 Act.Square,
                                 accum_out=stats[:, 32 + 8 * bp + 1:
                                                 32 + 8 * bp + 2]))
            E("scalar", _)
            sq1v = mk.val["SQ1"]
            W("vector", "SQ1", sq1v)
            W("vector", "SQ2", 16 + g4 - 1)
            fn = I("ST2", 1)

            def _(v, bp=bp, fn=fn):
                st = stats[:, 32 + 8 * bp:32 + 8 * (bp + 1)]
                vsync(v, v.tensor_single_scalar(st[:, 0:1], st[:, 0:1],
                                                1.0 / H, Alu.mult))
                vsync(v, v.tensor_single_scalar(st[:, 1:2], st[:, 1:2],
                                                1.0 / H, Alu.mult))
                vsync(v, v.tensor_tensor(st[:, 2:3], st[:, 0:1], st[:, 0:1],
                                         Alu.mult))
                vsync(v, v.tensor_tensor(st[:, 2:3], st[:, 1:2], st[:, 2:3],
                                         Alu.subtract))
                fn(v.tensor_single_scalar(st[:, 3:4], st[:, 2:3], EPS, Alu.add))
            E("vector", _)
            st2a = mk.val["ST2"]
            W("scalar", "ST2", st2a)
            fn = I("SQ2", 1)

            def _(sc, bp=bp, fn=fn):
                st = stats[:, 32 + 8 * bp:32 + 8 * (bp + 1)]
                fn(sc.activation(st[:, 4:5], st[:, 3:4], Act.Sqrt))
            E("scalar", _)
            sq2v = mk.val["SQ2"]
            W("vector", "SQ2", sq2v)
            W("vector", "LN2A", g4 - 1)
            fn = I("ST2", 1)

            def _(v, bp=bp, fn=fn):
                st = stats[:, 32 + 8 * bp:32 + 8 * (bp + 1)]
                vsync(v, v.reciprocal(st[:, 5:6], st[:, 4:5]))
                vsync(v, v.tensor_tensor(st[:, 5:6], st[:, 3:4], st[:, 5:6],
                                         Alu.mult))
                vsync(v, v.tensor_tensor(st[:, 4:5], st[:, 4:5], st[:, 5:6],
                                         Alu.add))
                vsync(v, v.tensor_single_scalar(st[:, 4:5], st[:, 4:5], 0.5,
                                                Alu.mult))
                vsync(v, v.reciprocal(st[:, 5:6], st[:, 4:5]))
                vsync(v, v.tensor_tensor(st[:, 6:7], st[:, 0:1], st[:, 5:6],
                                         Alu.mult))
                fn(v.tensor_single_scalar(st[:, 6:7], st[:, 6:7], -1.0,
                                          Alu.mult))
            E("vector", _)
            st2b = mk.val["ST2"]
            W("scalar", "ST2", st2b)
            fn = I("LN2A", 1)

            def _(sc, tt=tt, bp=bp, st4=st4, fn=fn):
                st = stats[:, 32 + 8 * bp:32 + 8 * (bp + 1)]
                fn(sc.activation(fslice(R_LN2, 2048),
                                 fslice(R_R1P[st4 % 2] + tt * 2048, 2048),
                                 Act.Identity,
                                 bias=st[:, 6:7], scale=st[:, 5:6]))
            E("scalar", _)
            ln2v = mk.val["LN2A"]
            emit_qd(2048,
                    fslice(R_LN2, 2048),
                    bslice(R_ASTG + bp * 2048, 2048),
                    R_QDS,
                    R_TQ, R_TA, R_TB, R_TS,
                    v_pre_waits=[("LN2A", ln2v), ("AW", 16 * (g4 - 1))],
                    done_inc="QDV")
        if st4 >= 1:
            for tt in range(4):
                g4p = 4 * (st4 - 1) + tt
                ti = tt % 2
                W("vector", "FFT", ff_done_at[(st4 - 1, tt)])
                W("vector", "YW", 16 * (g4p - 1))
                fn = I("FO", 1)

                def _(v, tt=tt, ti=ti, st4=st4, fn=fn):
                    reg = psum[:, 2048 * ti:2048 * (ti + 1)]
                    out = fslice(R_OUT[tt % 2], 2048)
                    v.tensor_tensor(out, reg, bias_sb[:], Alu.add)
                    fn(v.tensor_tensor(
                        out, out,
                        fslice(R_R1P[(st4 - 1) % 2] + tt * 2048, 2048),
                        Alu.add))
                E("vector", _)
    for tt in range(4):
        g4p = 4 * (P4N - 1) + tt
        ti = tt % 2
        W("vector", "FFT", ff_done_at[(P4N - 1, tt)])
        W("vector", "YW", 16 * (g4p - 1))
        fn = I("FO", 1)

        def _(v, tt=tt, ti=ti, fn=fn):
            reg = psum[:, 2048 * ti:2048 * (ti + 1)]
            out = fslice(R_OUT[tt % 2], 2048)
            v.tensor_tensor(out, reg, bias_sb[:], Alu.add)
            fn(v.tensor_tensor(out, out,
                               fslice(R_R1P[1] + tt * 2048, 2048), Alu.add))
        E("vector", _)

    W("gpsimd", "YW", 64 * P4N)
    mk.check_deadlock()
    mk.mark("full")
    import os as _os
    cut = _os.environ.get("MK_PHASE_CUT")
    if cut:
        m = mk.marks[cut]
        for e in mk.thunks:
            mk.thunks[e] = mk.thunks[e][:m[e]]
    with nc.Block() as block:
        block.sync(lambda e: [f(e) for f in mk.thunks["sync"]])
        block.gpsimd(lambda e: [f(e) for f in mk.thunks["gpsimd"]])
        block.tensor(lambda e: [f(e) for f in mk.thunks["tensor"]])
        block.vector(lambda e: [f(e) for f in mk.thunks["vector"]])
        block.scalar(lambda e: [f(e) for f in mk.thunks["scalar"]])
    es.close()
    return nc


_NC_CACHE = None


def kernel(L0_attn_hidden_states, L0_attn_Wq, L0_attn_Wk, L0_attn_Wv,
           L0_attn_Wo, L0_ffn_weight, L0_ffn_bias):
    global _NC_CACHE
    x = np.ascontiguousarray(np.asarray(L0_attn_hidden_states, dtype=np.float32)
                             .reshape(B * S, H))
    wq = np.asarray(L0_attn_Wq, dtype=np.float32)
    wk = np.asarray(L0_attn_Wk, dtype=np.float32)
    wv = np.asarray(L0_attn_Wv, dtype=np.float32)
    wo = np.asarray(L0_attn_Wo, dtype=np.float32)
    wf = np.asarray(L0_ffn_weight, dtype=np.float32)
    bias = np.asarray(L0_ffn_bias, dtype=np.float32)

    def tiled(w):
        wt = w.T.reshape(NHT, 128, NHT, 128)
        return np.ascontiguousarray(
            wt.transpose(2, 1, 0, 3).reshape(NHT, 128, H))

    wq_tl, wk_tl, wv_tl = tiled(wq), tiled(wk), tiled(wv)
    wo_t = np.ascontiguousarray(wo.T)
    bias_rep = np.ascontiguousarray(np.broadcast_to(bias[None, :], (128, H)))
    cos, sin = _rope_tables()
    masks = _masks()

    rows = np.arange(128, dtype=np.int32)
    in_maps = []
    for c in range(NCORE):
        gidx = c % 4
        pos0 = 2048 * gidx
        cos_sl = np.ascontiguousarray(cos[pos0:pos0 + TOK].T)
        sin_sl_t = sin[pos0:pos0 + TOK].T
        sin_sl = np.ascontiguousarray(
            np.concatenate([-sin_sl_t[:64], sin_sl_t[64:]], axis=0))
        qsel = np.zeros((128, 48), dtype=np.int32)
        for hi in range(4):
            for sec in range(3):
                for r in range(4):
                    qsel[:, 12 * hi + 4 * sec + r] = (
                        6144 * r + 2048 * sec + 128 * (4 * gidx + hi) + rows)
        aosel_ix = np.zeros((128, 64), dtype=np.int32)
        for st4 in range(4):
            for r in range(4):
                for q4 in range(4):
                    aosel_ix[:, 16 * st4 + 4 * r + q4] = (
                        8192 * r + 2048 * gidx + 512 * st4 + 128 * q4 + rows)
        in_maps.append({
            "x_sl": np.ascontiguousarray(x[2048 * c:2048 * (c + 1)]),
            "wq_tl": wq_tl, "wk_tl": wk_tl, "wv_tl": wv_tl,
            "wo_t": wo_t,
            "wffn_sl": np.ascontiguousarray(wf[:, 512 * gidx:512 * (gidx + 1)]),
            "bias_b": bias_rep,
            "cos_t": cos_sl, "sin_t": sin_sl,
            "masks_i": masks,
            "qsel_i": qsel, "aosel_i": aosel_ix,
        })

    if _NC_CACHE is None:
        _NC_CACHE = build_program()
    res = run_bass_kernel_spmd(_NC_CACHE, in_maps, core_ids=list(range(NCORE)))
    out = np.concatenate([res.results[c]["y_sl"] for c in range(NCORE)], axis=0)
    return out.reshape(B, S, H).astype(np.float32)


# version 07
# speedup vs baseline: 1.1021x; 1.1021x over previous
import math

import numpy as np
import ml_dtypes

import concourse.bass as bass
import concourse.mybir as mybir
from concourse.bass_utils import run_bass_kernel_spmd
from concourse.bass_core_rust import IndirectOffsetOnAxis

F32 = mybir.dt.float32
BF16 = mybir.dt.bfloat16
I32 = mybir.dt.int32
bf16 = ml_dtypes.bfloat16
Act = mybir.ActivationFunctionType
Alu = mybir.AluOpType
AX = mybir.AxisListType

NCORE = 8
H = 2048
NH = 16
HD = 128
S = 8192
B = 2
TOK = 2048
NHT = 16
SC_SCORE = float(np.float32(1.0 / math.sqrt(HD)))
EXP_BIAS = -12.0
EPS = 1e-5

QD_BOUND_A = [0.25, 0.75, 1.25, 1.75]
QD_BOUND_B = [2.5, 3.5]
QD_BOUND_C = 5.0
QD_EMIN, QD_EMAX = -8, 5

GROUPS = [[0, 1, 2, 3], [4, 5, 6, 7]]


def _rope_tables():
    theta = 10000.0
    inv_freq = 1.0 / theta ** (np.arange(0, HD, 2, dtype=np.float32) / HD)
    t = np.arange(S, dtype=np.float32)
    freqs = np.einsum("i,j->ij", t, inv_freq)
    emb = np.concatenate((freqs, freqs), axis=-1)
    return np.cos(emb), np.sin(emb)


def _masks():
    ql = np.arange(512)[None, :]
    kl = np.arange(128)[:, None]
    m = np.concatenate(
        [(ql >= 128 * r + kl).astype(np.float32) for r in range(4)], axis=1
    )
    return m.astype(bf16)


class MK:

    def __init__(self, nc, es):
        self.nc = nc
        self.thunks = {e: [] for e in ("sync", "gpsimd", "tensor", "vector",
                                       "scalar")}
        self.events = {e: [] for e in self.thunks}
        self.pending_incs = []
        self.sems = {}
        self.val = {}
        self.waited = {}
        self.es = es

    def sem(self, name):
        self.sems[name] = self.es.enter_context(self.nc.semaphore(name))
        self.val[name] = 0

    def emit(self, eng, fn):
        self.flush_incs(eng)
        self.thunks[eng].append(fn)

    def wait(self, eng, name, value):
        if value <= 0:
            return
        key = (eng, name)
        if self.waited.get(key, 0) >= value:
            return
        self.waited[key] = value
        self.events[eng].append(("w", name, value))
        h = self.sems[name]
        self.emit(eng, lambda e, h=h, v=value: e.wait_ge(h, v))

    def inc(self, name, n):
        self.val[name] += n
        self._cur_eng_hint = None
        self.pending_incs.append((name, n))
        h = self.sems[name]
        return lambda instr: instr.then_inc(h, n)

    def flush_incs(self, eng):
        for name, n in self.pending_incs:
            self.events[eng].append(("i", name, n))
        self.pending_incs = []

    def check_deadlock(self):
        import collections
        pos = {e: 0 for e in self.events}
        cnt = collections.defaultdict(int)
        progressed = True
        while progressed:
            progressed = False
            for e, evs in self.events.items():
                while pos[e] < len(evs):
                    kind, name, v = evs[pos[e]]
                    if kind == "i":
                        cnt[name] += v
                        pos[e] += 1
                        progressed = True
                    else:
                        if cnt[name] >= v:
                            pos[e] += 1
                            progressed = True
                        else:
                            break
        stuck = {e: (pos[e], len(evs)) for e, evs in self.events.items()
                 if pos[e] < len(evs)}
        if stuck:
            print("DEADLOCK:")
            for e, (p, n) in stuck.items():
                kind, name, v = self.events[e][p]
                print(f"  {e}@{p}/{n}: waiting {name} >= {v} (has {cnt[name]})")
            return False
        print("schedule OK (no deadlock)")
        return True

    def mark(self, name):
        self.marks = getattr(self, "marks", {})
        self.marks[name] = {e: len(t) for e, t in self.thunks.items()}


def build_program():
    import contextlib
    es = contextlib.ExitStack()
    nc = bass.Bass("TRN2", target_bir_lowering=False, debug=False)
    mk = MK(nc, es)

    def din(name, shape, dt=F32):
        return nc.dram_tensor(name, shape, dt, kind="ExternalInput").ap()

    x_sl = din("x_sl", [TOK, H])
    wq_tl = din("wq_tl", [NHT, 128, H])
    wk_tl = din("wk_tl", [NHT, 128, H])
    wv_tl = din("wv_tl", [NHT, 128, H])
    wo_t = din("wo_t", [H, H])
    wffn_sl = din("wffn_sl", [H, 512])
    bias_b = din("bias_b", [128, H])
    cos_t = din("cos_t", [128, TOK])
    sin_t = din("sin_t", [128, TOK])
    masks_i = din("masks_i", [128, 2048], BF16)
    qsel_i = din("qsel_i", [128, 48], I32)
    aosel_i = din("aosel_i", [128, 64], I32)

    y_sl = nc.dram_tensor("y_sl", [TOK, H], F32, kind="ExternalOutput").ap()

    def dint(name, shape, dt=BF16):
        return nc.dram_tensor(name, shape, dt, kind="Internal").ap()

    ln_scr = dint("ln_scr", [TOK, H])
    wqd_scr = dint("wqd_scr", [H, 512])
    wt_in = dint("wt_in", [512, H])
    wt_out = dint("wt_out", [H, H])
    qkv_in = dint("qkv_in", [3 * H, TOK])
    qkv_out = dint("qkv_out", [4 * 3 * H, TOK])
    vsel = dint("vsel", [2 * 512, H])
    ao_in = dint("ao_in", [S, 512])
    ao_out = dint("ao_out", [4 * S, 512])
    aosel = dint("aosel", [2 * TOK, 512])
    a_scr = dint("a_scr", [TOK, H])

    POOL_COLS = 99840
    pool = es.enter_context(nc.sbuf_tensor("pool", [128, POOL_COLS], BF16))
    bias_sb = es.enter_context(nc.sbuf_tensor("bias_sb", [128, H], F32))
    stats = es.enter_context(nc.sbuf_tensor("stats", [128, 64], F32))
    cst_t = es.enter_context(nc.sbuf_tensor("cst_t", [128, 8], F32))
    qsel_sb = es.enter_context(nc.sbuf_tensor("qsel_sb", [128, 48], I32))
    aosel_sb = es.enter_context(nc.sbuf_tensor("aosel_sb", [128, 64], I32))
    psum = es.enter_context(nc.psum_tensor("ps", [128, 4096], F32))

    def bslice(off, n):
        return pool[:, off:off + n]

    def fslice(off, n):
        return pool[:, 2 * off:2 * (off + n)].bitcast(F32)

    R_LNT = 0
    R_LNSTG = 32768
    R_XSTG = 18432
    R_COS = 22528
    R_SIN = 24576
    R_MASK = 53248
    R_WSTG = 55296
    R_QRAW = 59392
    R_QSWP = 63488
    R_T1 = 33792
    R_T2 = 16384
    R_QOUT = 71680
    R_VSTG = 75776
    R_WQSTG = 39936
    R_WQTMP = 40960
    R_WQOUT = 87040
    R_WTT = 89088
    R_WQS = 46592
    R_Q = [0, 8192]
    R_K = [16384, 24576]
    R_V = [32768, 41984]
    R_PT = [51200, 52224]
    R_AOS = [59520, 60032]
    R_DEN = 27648
    R_VG3 = 55424
    R_AOT = 0
    R_AT = 8192
    R_WOS = 16384
    R_WTS = 24576
    R_ASTG = 32768
    R_R1P = [18432, 26624]
    R_LN2 = 34816
    R_TQ = 36864
    R_TA = 38912
    R_TB = 40960
    R_TS = 43008
    R_OUT = [45056, 47104]
    R_QDS = 49152
    R_AOG4 = 98816

    SC = SC_SCORE
    CONSTS = [EXP_BIAS, SC]
    for i, v in enumerate(CONSTS):
        nc.const_aps.register(F32, v, cst_t[:, i:i + 1])

    for s in ("CS", "PR", "X1", "SQ1", "LS", "LA", "LW", "LT",
              "WQL", "WQD", "QDA", "WQW", "WTT", "WTI", "AG1",
              "WS", "PQ", "EV", "SW", "RP", "QKW", "VW", "AG2",
              "HLQ", "VG", "VSW", "VTR", "STm", "EX", "MKm", "PF", "NR",
              "AO", "AG3",
              "AOG", "AOW", "AOT", "X2", "OPT", "OP", "R1", "ST2", "SQ2",
              "LN2A", "QDV", "AW", "AT", "WOL", "WFL", "FFT", "FF", "FO",
              "YW", "VSS"):
        mk.sem(s)
    _vss_n = [0]

    def vsync(v, ins):
        _vss_n[0] += 1
        ins.then_inc(mk.sems["VSS"], 1)
        v.wait_ge(mk.sems["VSS"], _vss_n[0])

    E, W, I = mk.emit, mk.wait, mk.inc

    def emit_qd(ncols, src_f32, dst_bf16, r_small, r_q, r_a, r_b, r_s,
                v_pre_waits, done_inc):
        nb = ncols // 32
        for (sname, v) in v_pre_waits:
            W("vector", sname, v)

        def _(v, src=src_f32, nb=nb):
            amax = fslice(r_small, nb)
            scale = fslice(r_small + nb, nb)
            src3 = src.rearrange("p (a b) -> p a b", b=32)
            vsync(v, v.tensor_reduce(amax.unsqueeze(2), src3, AX.X, Alu.max,
                                     apply_absolute_value=True))
            v.memset(scale, 2.0 ** (QD_EMIN - 3))
            es_ = list(range(QD_EMIN, QD_EMAX + 1))
            gbuf = [fslice(r_small + 3 * nb, nb), fslice(r_q, nb)]
            for i, e in enumerate(es_):
                v.tensor_scalar(gbuf[i % 2], amax, float(2.0 ** e),
                                float(2.0 ** (e - 3)), Alu.is_ge, Alu.mult)
                if i >= 1:
                    v.tensor_tensor(scale, scale, gbuf[(i - 1) % 2], Alu.add)
            vsync(v, v.tensor_tensor(scale, scale,
                                     gbuf[(len(es_) - 1) % 2], Alu.add))
            rscale = fslice(r_small + 2 * nb, nb)
            vsync(v, v.reciprocal(rscale, scale))
            q = fslice(r_q, ncols)
            rs_b = rscale.unsqueeze(2).broadcast_to([128, nb, 32])
            v.tensor_tensor(q.rearrange("p (a b) -> p a b", b=32), src3, rs_b,
                            Alu.mult)
        E("vector", _)
        qfn = I("QDA", 1)

        def _(v, fn=qfn):
            fn(v.engine_nop())
        E("vector", _)
        q_ready = mk.val["QDA"]
        W("scalar", "QDA", q_ready)
        afn = I("QDA", 1)

        def _(sc, fn=afn):
            q = fslice(r_q, ncols)
            sc.activation(fslice(r_a, ncols), q, Act.Abs)
            fn(sc.activation(fslice(r_s, ncols), q, Act.Sign))
        E("scalar", _)
        as_ready = mk.val["QDA"]
        W("vector", "QDA", as_ready)
        dfn = I(done_inc, 1)

        def _(v, nb=nb, fn=dfn):
            a = fslice(r_a, ncols)
            A = fslice(r_q, ncols)
            bb = fslice(r_b, ncols)
            sg = fslice(r_s, ncols)
            scale = fslice(r_small + nb, nb)
            sc_b = scale.unsqueeze(2).broadcast_to([128, nb, 32])
            v.tensor_single_scalar(A, a, QD_BOUND_A[0], Alu.is_ge)
            for bnd in QD_BOUND_A[1:]:
                v.scalar_tensor_tensor(A, a, bnd, A, Alu.is_ge, Alu.add)
            v.tensor_single_scalar(bb, a, QD_BOUND_B[0], Alu.is_ge)
            v.scalar_tensor_tensor(bb, a, QD_BOUND_B[1], bb, Alu.is_ge, Alu.add)
            v.scalar_tensor_tensor(bb, A, 0.5, bb, Alu.mult, Alu.add)
            v.tensor_scalar(a, a, QD_BOUND_C, 2.0, Alu.is_ge, Alu.mult)
            v.tensor_tensor(bb, bb, a, Alu.add)
            v.tensor_tensor(bb, bb, sg, Alu.mult)
            fn(v.tensor_tensor(dst_bf16.rearrange("p (a b) -> p a b", b=32),
                               bb.rearrange("p (a b) -> p a b", b=32), sc_b,
                               Alu.mult))
        E("vector", _)

    def _(v):
        for i, c in enumerate(CONSTS):
            v.memset(cst_t[:, i:i + 1], c)
    E("vector", _)
    csfn = I("CS", 1)

    def _(v, fn=csfn):
        fn(v.engine_nop())
    E("vector", _)

    prfn = [I("PR", 16) for _ in range(6)]

    def _(s, fns=prfn):
        fns[0](s.dma_start(out=fslice(R_COS, H), in_=cos_t))
        fns[1](s.dma_start(out=fslice(R_SIN, H), in_=sin_t))
        fns[2](s.dma_start(out=bslice(R_MASK, 2048), in_=masks_i))
        fns[3](s.dma_start(out=bias_sb[:], in_=bias_b))
        fns[4](s.dma_start(out=qsel_sb[:], in_=qsel_i))
        fns[5](s.dma_start(out=aosel_sb[:], in_=aosel_i))
    E("sync", _)
    PR_ALL = mk.val["PR"]

    mk.mark("prep")
    for i in range(16):
        bp = i % 2
        W("scalar", "WQD", i - 1)
        fn = I("WQL", 16)

        def _(s, i=i, bp=bp, fn=fn):
            fn(s.dma_start(out=fslice(R_WQSTG + bp * 512, 512),
                           in_=wffn_sl[128 * i:128 * (i + 1), :]))
        E("scalar", _)
        emit_qd(512,
                fslice(R_WQSTG + bp * 512, 512),
                bslice(R_WQOUT + bp * 512, 512),
                R_WQS,
                R_WQTMP, R_WQTMP + 512, R_WQTMP + 1024, R_WQTMP + 1536,
                v_pre_waits=[("CS", 1), ("WQL", 16 * (i + 1)),
                             ("WQW", 16 * (i - 1))],
                done_inc="WQD")
        if i >= 1:
            W("scalar", "WQD", i)
            fn = I("WQW", 16)

            def _(s, i=i, fn=fn):
                fn(s.dma_start(out=wqd_scr[128 * (i - 1):128 * i, :],
                               in_=bslice(R_WQOUT + ((i - 1) % 2) * 512, 512)))
            E("scalar", _)
    W("scalar", "WQD", 16)
    fn = I("WQW", 16)

    def _(s, fn=fn):
        fn(s.dma_start(out=wqd_scr[128 * 15:128 * 16, :],
                       in_=bslice(R_WQOUT + 512, 512)))
    E("scalar", _)
    W("sync", "WQW", 256)
    for k in range(4):
        W("sync", "WTI", 16 * (k - 1))
        fn = I("WTT", 16)

        def _(s, k=k, fn=fn):
            fn(s.dma_start_transpose(out=bslice(R_WTT + (k % 2) * 2048, 2048),
                                     in_=wqd_scr[:, 128 * k:128 * (k + 1)]))
        E("sync", _)
        W("sync", "WTT", 16 * (k + 1))
        fn = I("WTI", 16)

        def _(s, k=k, fn=fn):
            fn(s.dma_start(out=wt_in[128 * k:128 * (k + 1), :],
                           in_=bslice(R_WTT + (k % 2) * 2048, 2048)))
        E("sync", _)

    mk.mark("wqd")
    for t in range(16):
        bp = t % 2
        W("sync", "LA", t - 1)
        fn = I("X1", 16)

        def _(s, t=t, bp=bp, fn=fn):
            fn(s.dma_start(out=fslice(R_XSTG + bp * 2048, 2048),
                           in_=x_sl[128 * t:128 * (t + 1), :]))
        E("sync", _)
        if t >= 1:
            W("sync", "LA", t)
            fn = I("LW", 16)

            def _(s, t=t, fn=fn):
                fn(s.dma_start(out=ln_scr[128 * (t - 1):128 * t, :],
                               in_=bslice(R_LNSTG + ((t - 1) % 2) * 2048, 2048)))
            E("sync", _)
    W("sync", "LA", 16)
    fn = I("LW", 16)

    def _(s, fn=fn):
        fn(s.dma_start(out=ln_scr[128 * 15:128 * 16, :],
                       in_=bslice(R_LNSTG + 2048, 2048)))
    E("sync", _)
    for t in range(16):
        bp = t % 2
        W("scalar", "X1", 16 * (t + 1))
        W("scalar", "LS", 2 * t - 3)
        fn = I("SQ1", 1)

        def _(sc, bp=bp, fn=fn):
            fn(sc.activation(fslice(R_T1, 2048), fslice(R_XSTG + bp * 2048, 2048),
                             Act.Square,
                             accum_out=stats[:, 8 * bp + 1:8 * bp + 2]))
        E("scalar", _)
        sq1v = mk.val["SQ1"]
        W("vector", "X1", 16 * (t + 1))

        def _(v, bp=bp):
            vsync(v, v.reduce_sum(stats[:, 8 * bp:8 * bp + 1],
                                  fslice(R_XSTG + bp * 2048, 2048), AX.X))
        E("vector", _)
        W("vector", "SQ1", sq1v)
        W("vector", "SQ2", t - 1)
        fn = I("LS", 1)

        def _(v, bp=bp, fn=fn):
            st = stats[:, 8 * bp:8 * (bp + 1)]
            vsync(v, v.tensor_single_scalar(st[:, 0:1], st[:, 0:1], 1.0 / H,
                                            Alu.mult))
            vsync(v, v.tensor_single_scalar(st[:, 1:2], st[:, 1:2], 1.0 / H,
                                            Alu.mult))
            vsync(v, v.tensor_tensor(st[:, 2:3], st[:, 0:1], st[:, 0:1],
                                     Alu.mult))
            vsync(v, v.tensor_tensor(st[:, 2:3], st[:, 1:2], st[:, 2:3],
                                     Alu.subtract))
            fn(v.tensor_single_scalar(st[:, 3:4], st[:, 2:3], EPS, Alu.add))
        E("vector", _)
        ls1 = mk.val["LS"]
        W("scalar", "LS", ls1)
        fn = I("SQ2", 1)

        def _(sc, bp=bp, fn=fn):
            st = stats[:, 8 * bp:8 * (bp + 1)]
            fn(sc.activation(st[:, 4:5], st[:, 3:4], Act.Sqrt))
        E("scalar", _)
        sq2v = mk.val["SQ2"]
        W("vector", "SQ2", sq2v)
        W("vector", "LA", t - 1)
        fn = I("LS", 1)

        def _(v, bp=bp, fn=fn):
            st = stats[:, 8 * bp:8 * (bp + 1)]
            vsync(v, v.reciprocal(st[:, 5:6], st[:, 4:5]))
            vsync(v, v.tensor_tensor(st[:, 5:6], st[:, 3:4], st[:, 5:6],
                                     Alu.mult))
            vsync(v, v.tensor_tensor(st[:, 4:5], st[:, 4:5], st[:, 5:6],
                                     Alu.add))
            vsync(v, v.tensor_single_scalar(st[:, 4:5], st[:, 4:5], 0.5,
                                            Alu.mult))
            vsync(v, v.reciprocal(st[:, 5:6], st[:, 4:5]))
            vsync(v, v.tensor_tensor(st[:, 6:7], st[:, 0:1], st[:, 5:6],
                                     Alu.mult))
            fn(v.tensor_single_scalar(st[:, 6:7], st[:, 6:7], -1.0, Alu.mult))
        E("vector", _)
        ls2 = mk.val["LS"]
        W("scalar", "LS", ls2)
        W("scalar", "LW", 16 * (t - 1))
        fn = I("LA", 1)

        def _(sc, bp=bp, fn=fn):
            st = stats[:, 8 * bp:8 * (bp + 1)]
            fn(sc.activation(bslice(R_LNSTG + bp * 2048, 2048),
                             fslice(R_XSTG + bp * 2048, 2048), Act.Identity,
                             bias=st[:, 6:7], scale=st[:, 5:6]))
        E("scalar", _)
    W("sync", "LW", 256)
    for h in range(NHT):
        fn = I("LT", 16)

        def _(s, h=h, fn=fn):
            fn(s.dma_start_transpose(out=bslice(R_LNT + h * 2048, 2048),
                                     in_=ln_scr[:, 128 * h:128 * (h + 1)]))
        E("sync", _)

    mk.mark("ln")
    WTILED = {0: wq_tl, 1: wk_tl, 2: wv_tl}
    qk_idx = 0
    v_idx = 0
    pending_qk_write = None
    for d in range(NHT):
        for p in range(3):
            g = 3 * d + p
            gp = g % 2
            W("gpsimd", "PQ", g - 1)
            fn = I("WS", 16)

            def _(gg, d=d, p=p, gp=gp, fn=fn):
                fn(gg.dma_start(out=bslice(R_WSTG + gp * 2048, 2048),
                                in_=WTILED[p][d]))
            E("gpsimd", _)
            W("tensor", "LT", 256)
            W("tensor", "WS", 16 * (g + 1))
            W("tensor", "EV", g - 1)
            fn = I("PQ", 1)

            def _(te, gp=gp, fn=fn):
                reg = psum[:, 2048 * gp:2048 * (gp + 1)]
                wst = bslice(R_WSTG + gp * 2048, 2048)
                ins = None
                for ht in range(NHT):
                    lhsT = wst[:, 128 * ht:128 * (ht + 1)]
                    lnt = bslice(R_LNT + ht * 2048, 2048)
                    for ch in range(4):
                        ins = te.matmul(reg[:, 512 * ch:512 * (ch + 1)], lhsT,
                                        lnt[:, 512 * ch:512 * (ch + 1)],
                                        start=(ht == 0), stop=(ht == NHT - 1))
                fn(ins)
            E("tensor", _)
            W("scalar", "PQ", g + 1)
            if p < 2:
                qi = qk_idx
                W("scalar", "RP", qi - 1)
                fn = I("EV", 1)

                def _(sc, gp=gp, qi=qi, fn=fn):
                    fn(sc.activation(bslice(R_QRAW + (qi % 2) * 2048, 2048),
                                     psum[:, 2048 * gp:2048 * (gp + 1)],
                                     Act.Copy))
                E("scalar", _)
                W("sync", "EV", mk.val["EV"])
                fn1, fn2 = I("SW", 16), I("SW", 16)

                def _(s, qi=qi, fn1=fn1, fn2=fn2):
                    qr = bslice(R_QRAW + (qi % 2) * 2048, 2048)
                    qs = bslice(R_QSWP + (qi % 2) * 2048, 2048)
                    fn1(s.dma_start(out=qs[0:64, :], in_=qr[64:128, :]))
                    fn2(s.dma_start(out=qs[64:128, :], in_=qr[0:64, :]))
                E("sync", _)
                if pending_qk_write is not None:
                    pqi, prow0, prp = pending_qk_write
                    W("sync", "RP", prp)
                    fn = I("QKW", 16)

                    def _(s, pqi=pqi, prow0=prow0, fn=fn):
                        fn(s.dma_start(out=qkv_in[prow0:prow0 + 128, :],
                                       in_=bslice(R_QOUT + (pqi % 2) * 2048,
                                                  2048)))
                    E("sync", _)
                W("vector", "PR", PR_ALL)
                W("vector", "SW", 32 * (qi + 1))
                W("vector", "QKW", 16 * (qi - 1))
                fn = I("RP", 1)

                def _(v, qi=qi, fn=fn):
                    qr = bslice(R_QRAW + (qi % 2) * 2048, 2048)
                    qs = bslice(R_QSWP + (qi % 2) * 2048, 2048)
                    t1 = fslice(R_T1, 2048)
                    t2 = fslice(R_T2, 2048)
                    v.tensor_tensor(t1, qr, fslice(R_COS, 2048), Alu.mult)
                    v.tensor_tensor(t2, qs, fslice(R_SIN, 2048), Alu.mult)
                    fn(v.tensor_tensor(bslice(R_QOUT + (qi % 2) * 2048, 2048),
                                       t1, t2, Alu.add))
                E("vector", _)
                row0 = 2048 * p + 128 * d
                pending_qk_write = (qi, row0, mk.val["RP"])
                qk_idx += 1
            else:
                vi = v_idx
                W("scalar", "VW", 16 * (vi - 1))
                fn = I("EV", 1)

                def _(sc, gp=gp, vi=vi, fn=fn):
                    fn(sc.activation(bslice(R_VSTG + (vi % 2) * 2048, 2048),
                                     psum[:, 2048 * gp:2048 * (gp + 1)],
                                     Act.Copy))
                E("scalar", _)
                W("sync", "EV", mk.val["EV"])
                row0 = 4096 + 128 * d
                fn = I("VW", 16)

                def _(s, vi=vi, row0=row0, fn=fn):
                    fn(s.dma_start(out=qkv_in[row0:row0 + 128, :],
                                   in_=bslice(R_VSTG + (vi % 2) * 2048, 2048)))
                E("sync", _)
                v_idx += 1
    pqi, prow0, prp = pending_qk_write
    W("sync", "RP", prp)
    fn = I("QKW", 16)

    def _(s, pqi=pqi, prow0=prow0, fn=fn):
        fn(s.dma_start(out=qkv_in[prow0:prow0 + 128, :],
                       in_=bslice(R_QOUT + (pqi % 2) * 2048, 2048)))
    E("sync", _)
    assert mk.val["QKW"] == 16 * 32 and mk.val["VW"] == 16 * 16

    W("gpsimd", "WTI", 64)
    fn = I("AG1", 1)

    def _(gg, fn=fn):
        fn(gg.collective_compute("AllGather", Alu.bypass, replica_groups=GROUPS,
                                 ins=[wt_in], outs=[wt_out]))
    E("gpsimd", _)
    W("gpsimd", "QKW", 512)
    W("gpsimd", "VW", 256)
    fn = I("AG2", 1)

    def _(gg, fn=fn):
        fn(gg.collective_compute("AllGather", Alu.bypass, replica_groups=GROUPS,
                                 ins=[qkv_in], outs=[qkv_out]))
    E("gpsimd", _)

    mk.mark("proj")
    W("vector", "AG2", 1)
    onesdone = I("CS", 1)

    def _(v, fn=onesdone):
        for hp in range(2):
            for kt in range(64):
                v.memset(bslice(R_V[hp] + 144 * kt + 128, 1), 1.0)
        fn(v.engine_nop())
    E("vector", _)
    ONES_DONE = mk.val["CS"]

    KT_PER_HEAD = sum(4 * (qb + 1) for qb in range(16))
    st_cum = ex_cum = mkm_cum = pf_cum = nr_cum = ao_cum = 0
    vg_idx = 0

    for hi in range(4):
        hp = hi % 2
        W("gpsimd", "AG2", 1)
        W("gpsimd", "PR", PR_ALL)
        if hi >= 2:
            W("gpsimd", "STm", KT_PER_HEAD * (hi - 1))
            W("gpsimd", "PF", KT_PER_HEAD // 2 * (hi - 1))
            W("gpsimd", "CS", ONES_DONE)
        for r in range(4):
            fn1, fn2 = I("HLQ", 16), I("HLQ", 16)

            def _(gg, hi=hi, hp=hp, r=r, fn1=fn1, fn2=fn2):
                fn1(gg.indirect_dma_start(
                    out=bslice(R_Q[hp] + 2048 * r, 2048), out_offset=None,
                    in_=qkv_out,
                    in_offset=IndirectOffsetOnAxis(
                        qsel_sb[:, 12 * hi + r:12 * hi + r + 1], 0)))
                fn2(gg.indirect_dma_start(
                    out=bslice(R_K[hp] + 2048 * r, 2048), out_offset=None,
                    in_=qkv_out,
                    in_offset=IndirectOffsetOnAxis(
                        qsel_sb[:, 12 * hi + 4 + r:12 * hi + 4 + r + 1], 0)))
            E("gpsimd", _)
        for r in range(4):
            vg = vg_idx
            W("gpsimd", "VSW", 16 * (vg - 1))
            fn = I("VG", 16)

            def _(gg, hi=hi, r=r, vg=vg, fn=fn):
                fn(gg.indirect_dma_start(
                    out=bslice(R_VG3 + (vg % 2) * 2048, 2048), out_offset=None,
                    in_=qkv_out,
                    in_offset=IndirectOffsetOnAxis(
                        qsel_sb[:, 12 * hi + 8 + r:12 * hi + 8 + r + 1], 0)))
            E("gpsimd", _)
            W("gpsimd", "VG", 16 * (vg + 1))
            if hi >= 2:
                W("gpsimd", "VTR", 1024 * (hi - 1))
            fn = I("VSW", 16)

            def _(gg, hp=hp, r=r, vg=vg, fn=fn):
                fn(gg.dma_start(out=vsel[512 * hp + 128 * r:
                                         512 * hp + 128 * (r + 1), :],
                                in_=bslice(R_VG3 + (vg % 2) * 2048, 2048)))
            E("gpsimd", _)
            vg_idx += 1
        W("sync", "VSW", 64 * (hi + 1))
        W("sync", "CS", ONES_DONE)
        if hi >= 2:
            W("sync", "PF", KT_PER_HEAD // 2 * (hi - 1))
        for r in range(4):
            for j in range(16):
                fn = I("VTR", 16)

                def _(s, hp=hp, r=r, j=j, fn=fn):
                    kt = 16 * r + j
                    fn(s.dma_start_transpose(
                        out=bslice(R_V[hp] + 144 * kt, 128),
                        in_=vsel[512 * hp + 128 * r:512 * hp + 128 * (r + 1),
                                 128 * j:128 * (j + 1)]))
                E("sync", _)

        for qb in range(16):
            qbg = 16 * hi + qb
            nkt = 4 * (qb + 1)
            for ktp in range(nkt // 2):
                pg = st_cum // 2
                pp = pg % 2
                kt0 = 2 * ktp
                W("tensor", "HLQ", 128 * (hi + 1))
                W("tensor", "VTR", 1024 * (hi + 1))
                W("tensor", "EX", pg - 1)
                if ktp == 0:
                    W("tensor", "NR", qbg - 1)
                fns = [I("STm", 1), I("STm", 1)]

                def _(te, hp=hp, qb=qb, kt0=kt0, pp=pp, fns=fns):
                    for i in range(2):
                        bank = psum[:, 1024 * pp + 512 * i:
                                    1024 * pp + 512 * (i + 1)]
                        fns[i](te.matmul(bank,
                                         bslice(R_K[hp] + 128 * (kt0 + i), 128),
                                         bslice(R_Q[hp] + 512 * qb, 512),
                                         start=True, stop=True))
                E("tensor", _)
                st_cum += 2
                W("scalar", "STm", st_cum)
                W("scalar", "PF", pg - 1)
                fn = I("EX", 1)

                def _(sc, pp=pp, fn=fn):
                    fn(sc.activation(bslice(R_PT[pp], 1024),
                                     psum[:, 1024 * pp:1024 * (pp + 1)],
                                     Act.Exp, bias=EXP_BIAS, scale=SC))
                E("scalar", _)
                ex_cum += 1
                any_diag = False
                for i in range(2):
                    kt = kt0 + i
                    if kt >= 4 * qb:
                        rel = kt - 4 * qb
                        W("vector", "EX", ex_cum)
                        fn = I("MKm", 1)

                        def _(v, pp=pp, i=i, rel=rel, fn=fn):
                            pt = bslice(R_PT[pp] + 512 * i, 512)
                            fn(v.tensor_tensor(pt, pt,
                                               bslice(R_MASK + 512 * rel, 512),
                                               Alu.mult))
                        E("vector", _)
                        mkm_cum += 1
                        any_diag = True
                if any_diag:
                    W("tensor", "MKm", mkm_cum)
                else:
                    W("tensor", "EX", ex_cum)
                fn = I("PF", 1)

                def _(te, hp=hp, kt0=kt0, pp=pp, nkt=nkt, qbg=qbg, fn=fn):
                    reg0 = 2048 + 1024 * (qbg % 2)
                    ins = None
                    for i in range(2):
                        kt = kt0 + i
                        pt = bslice(R_PT[pp] + 512 * i, 512)
                        vv = bslice(R_V[hp] + 144 * kt, 129)
                        for j in range(4):
                            col = reg0 + 512 * (j // 2) + 256 * (j % 2)
                            ins = te.matmul(
                                psum[:, col:col + 129],
                                pt[:, 128 * j:128 * (j + 1)], vv,
                                start=(kt == 0 and j % 2 == 0),
                                stop=(kt == nkt - 1 and j % 2 == 1))
                    fn(ins)
                E("tensor", _)
                pf_cum += 1
            W("vector", "PF", pf_cum)
            W("vector", "AO", 64 * (qbg - 1))
            fn = I("NR", 1)

            def _(v, qbg=qbg, fn=fn):
                reg0 = 2048 + 1024 * (qbg % 2)
                den = fslice(R_DEN, 32)
                aos = bslice(R_AOS[qbg % 2], 512)
                ins = None
                for j in range(4):
                    col = reg0 + 512 * (j // 2) + 256 * (j % 2)
                    ins = v.reciprocal(den[:, j:j + 1],
                                       psum[:, col + 128:col + 129])
                vsync(v, ins)
                for j in range(4):
                    col = reg0 + 512 * (j // 2) + 256 * (j % 2)
                    ins = v.tensor_single_scalar(
                        aos[:, 128 * j:128 * (j + 1)],
                        psum[:, col:col + 128],
                        den[:, j:j + 1], Alu.mult)
                fn(ins)
            E("vector", _)
            nr_cum += 1
            W("sync", "NR", nr_cum)
            fns = [I("AO", 16) for _ in range(4)]

            def _(s, hi=hi, qb=qb, qbg=qbg, fns=fns):
                aos = bslice(R_AOS[qbg % 2], 512)
                for j in range(4):
                    fns[j](s.dma_start(
                        out=ao_in[512 * qb + 128 * j:512 * qb + 128 * (j + 1),
                                  128 * hi:128 * (hi + 1)],
                        in_=aos[:, 128 * j:128 * (j + 1)]))
            E("sync", _)
            ao_cum += 64

    W("gpsimd", "AO", ao_cum)
    fn = I("AG3", 1)

    def _(gg, fn=fn):
        fn(gg.collective_compute("AllGather", Alu.bypass, replica_groups=GROUPS,
                                 ins=[ao_in], outs=[ao_out]))
    E("gpsimd", _)

    mk.mark("attn")
    import os as _os2
    P4N = int(_os2.environ.get("MK_P4_N", "4"))
    pe_groups = []
    for st4 in range(P4N):
        pe_groups.append(("op", st4, 0))
        pe_groups.append(("op", st4, 1))
        if st4 >= 1:
            pe_groups.append(("ffn", st4 - 1, 0))
            pe_groups.append(("ffn", st4 - 1, 1))
    pe_groups.append(("ffn", P4N - 1, 0))
    pe_groups.append(("ffn", P4N - 1, 1))

    W("gpsimd", "AG3", 1)
    aog_idx = 0
    wo_steps = 0
    wt_steps = 0
    for kind, st4, pr in pe_groups:
        if kind == "op" and pr == 0:
            if st4 >= 2:
                W("gpsimd", "AOT", 256 * (st4 - 1))
            for r in range(4):
                for q4 in range(4):
                    ag = aog_idx
                    W("gpsimd", "AOW", 16 * (ag - 1))
                    fn = I("AOG", 16)

                    def _(gg, st4=st4, r=r, q4=q4, ag=ag, fn=fn):
                        fn(gg.indirect_dma_start(
                            out=bslice(R_AOG4 + (ag % 2) * 512, 512),
                            out_offset=None,
                            in_=ao_out,
                            in_offset=IndirectOffsetOnAxis(
                                aosel_sb[:, 16 * st4 + 4 * r + q4:
                                         16 * st4 + 4 * r + q4 + 1], 0)))
                    E("gpsimd", _)
                    W("gpsimd", "AOG", 16 * (ag + 1))
                    fn = I("AOW", 16)

                    def _(gg, st4=st4, r=r, q4=q4, ag=ag, fn=fn):
                        row0 = 2048 * (st4 % 2) + 512 * r + 128 * q4
                        fn(gg.dma_start(
                            out=aosel[row0:row0 + 128, :],
                            in_=bslice(R_AOG4 + (ag % 2) * 512, 512)))
                    E("gpsimd", _)
                    aog_idx += 1
            if st4 == 0:
                mk.mark("p4ga")
        for k in range(16):
            if kind == "op":
                W("gpsimd", "OPT", wo_steps - 3)
                slot = wo_steps % 4
                fn = I("WOL", 16)

                def _(gg, k=k, slot=slot, fn=fn):
                    fn(gg.dma_start(out=bslice(R_WOS + 2048 * slot, 2048),
                                    in_=wo_t[128 * k:128 * (k + 1), :]))
                E("gpsimd", _)
                wo_steps += 1
                if wo_steps == 16:
                    mk.mark("p4gb")
            else:
                W("gpsimd", "FFT", wt_steps - 3)
                slot = wt_steps % 4
                fn = I("WFL", 16)

                def _(gg, k=k, slot=slot, fn=fn):
                    fn(gg.dma_start(out=bslice(R_WTS + 2048 * slot, 2048),
                                    in_=wt_out[128 * k:128 * (k + 1), :]))
                E("gpsimd", _)
                wt_steps += 1
    mk.mark("p4gpsimd")

    W("sync", "AG3", 1)
    for st4 in range(P4N):
        for tt in range(4):
            W("sync", "FO", 4 * (st4 - 2) + tt + 1)
            fn = I("X2", 16)

            def _(s, st4=st4, tt=tt, fn=fn):
                fn(s.dma_start(out=fslice(R_R1P[st4 % 2] + tt * 2048, 2048),
                               in_=x_sl[512 * st4 + 128 * tt:
                                        512 * st4 + 128 * (tt + 1), :]))
            E("sync", _)
        W("sync", "OPT", 32 * st4)
        W("sync", "AOW", 16 * 16 * (st4 + 1))
        for r in range(4):
            for ht in range(4):
                fn = I("AOT", 16)

                def _(s, st4=st4, r=r, ht=ht, fn=fn):
                    slot = 4 * r + ht
                    base = 2048 * (st4 % 2) + 512 * r
                    fn(s.dma_start_transpose(
                        out=bslice(R_AOT + 512 * slot, 512),
                        in_=aosel[base:base + 512, 128 * ht:128 * (ht + 1)]))
                E("sync", _)
        for tt in range(4):
            g4 = 4 * st4 + tt
            W("sync", "QDV", g4 + 1)
            fn = I("AW", 16)

            def _(s, st4=st4, tt=tt, fn=fn):
                fn(s.dma_start(
                    out=a_scr[512 * st4 + 128 * tt:512 * st4 + 128 * (tt + 1), :],
                    in_=bslice(R_ASTG + (tt % 2) * 2048, 2048)))
            E("sync", _)
        W("sync", "AW", 64 * (st4 + 1))
        W("sync", "FFT", 32 * st4)
        for k in range(16):
            fn = I("AT", 16)

            def _(s, st4=st4, k=k, fn=fn):
                fn(s.dma_start_transpose(
                    out=bslice(R_AT + 512 * k, 512),
                    in_=a_scr[512 * st4:512 * (st4 + 1),
                              128 * k:128 * (k + 1)]))
            E("sync", _)
        if st4 >= 1:
            for tt in range(4):
                W("sync", "FO", 4 * (st4 - 1) + tt + 1)
                fn = I("YW", 16)

                def _(s, st4=st4, tt=tt, fn=fn):
                    fn(s.dma_start(
                        out=y_sl[512 * (st4 - 1) + 128 * tt:
                                 512 * (st4 - 1) + 128 * (tt + 1), :],
                        in_=fslice(R_OUT[tt % 2], 2048)))
                E("sync", _)
    for tt in range(4):
        W("sync", "FO", 4 * (P4N - 1) + tt + 1)
        fn = I("YW", 16)

        def _(s, tt=tt, fn=fn):
            fn(s.dma_start(out=y_sl[512 * (P4N - 1) + 128 * tt:
                                    512 * (P4N - 1) + 128 * (tt + 1), :],
                           in_=fslice(R_OUT[tt % 2], 2048)))
        E("sync", _)

    mk.mark("p4sync")
    group_done = []
    wo_pe = 0
    wt_pe = 0
    for gi, (kind, st4, pr) in enumerate(pe_groups):
        if gi >= 1:
            sname, sval = group_done[gi - 1]
            W("tensor", sname, sval)
        if kind == "op":
            W("tensor", "AOT", 256 * (st4 + 1))
            for k in range(16):
                W("tensor", "WOL", 16 * (wo_pe + 1))
                fn = I("OPT", 1)

                def _(te, k=k, pr=pr, slot=wo_pe % 4, fn=fn):
                    wst = bslice(R_WOS + 2048 * slot, 2048)
                    ins = None
                    for ti in range(2):
                        tt = 2 * pr + ti
                        reg = psum[:, 2048 * ti:2048 * (ti + 1)]
                        for ch in range(4):
                            ins = te.matmul(
                                reg[:, 512 * ch:512 * (ch + 1)],
                                bslice(R_AOT + 512 * k + 128 * tt, 128),
                                wst[:, 512 * ch:512 * (ch + 1)],
                                start=(k == 0), stop=(k == 15))
                    fn(ins)
                E("tensor", _)
                wo_pe += 1
            group_done.append(("R1", 4 * st4 + 2 * pr + 2))
        else:
            W("tensor", "AT", 256 * (st4 + 1))
            for k in range(16):
                W("tensor", "WFL", 16 * (wt_pe + 1))
                fn = I("FFT", 1)

                def _(te, k=k, pr=pr, slot=wt_pe % 4, fn=fn):
                    wst = bslice(R_WTS + 2048 * slot, 2048)
                    ins = None
                    for ti in range(2):
                        tt = 2 * pr + ti
                        reg = psum[:, 2048 * ti:2048 * (ti + 1)]
                        for ch in range(4):
                            ins = te.matmul(
                                reg[:, 512 * ch:512 * (ch + 1)],
                                bslice(R_AT + 512 * k + 128 * tt, 128),
                                wst[:, 512 * ch:512 * (ch + 1)],
                                start=(k == 0), stop=(k == 15))
                    fn(ins)
                E("tensor", _)
                wt_pe += 1
            group_done.append(("FO", 4 * st4 + 2 * pr + 2))

    op_done_at = {}
    ff_done_at = {}
    opv = ffv = 0
    for kind, st4, pr in pe_groups:
        if kind == "op":
            opv += 16
            op_done_at[(st4, 2 * pr)] = opv
            op_done_at[(st4, 2 * pr + 1)] = opv
        else:
            ffv += 16
            ff_done_at[(st4, 2 * pr)] = ffv
            ff_done_at[(st4, 2 * pr + 1)] = ffv

    mk.mark("p4pe")
    for st4 in range(P4N):
        for tt in range(4):
            g4 = 4 * st4 + tt
            bp = tt % 2
            ti = tt % 2
            W("vector", "OPT", op_done_at[(st4, tt)])
            W("vector", "X2", 16 * (g4 + 1))
            fn = I("R1", 1)

            def _(v, ti=ti, tt=tt, st4=st4, fn=fn):
                reg = psum[:, 2048 * ti:2048 * (ti + 1)]
                r1 = fslice(R_R1P[st4 % 2] + tt * 2048, 2048)
                fn(v.tensor_tensor(r1, reg, r1, Alu.add))
            E("vector", _)
            r1v = mk.val["R1"]

            def _(v, tt=tt, bp=bp, st4=st4):
                vsync(v, v.reduce_sum(stats[:, 32 + 8 * bp:32 + 8 * bp + 1],
                                      fslice(R_R1P[st4 % 2] + tt * 2048, 2048),
                                      AX.X))
            E("vector", _)
            W("scalar", "R1", r1v)
            W("scalar", "ST2", 2 * g4 - 3)
            W("scalar", "QDA", 2 * g4 + 31)
            fn = I("SQ1", 1)

            def _(sc, tt=tt, bp=bp, st4=st4, fn=fn):
                fn(sc.activation(fslice(R_LN2, 2048),
                                 fslice(R_R1P[st4 % 2] + tt * 2048, 2048),
                                 Act.Square,
                                 accum_out=stats[:, 32 + 8 * bp + 1:
                                                 32 + 8 * bp + 2]))
            E("scalar", _)
            sq1v = mk.val["SQ1"]
            W("vector", "SQ1", sq1v)
            W("vector", "SQ2", 16 + g4 - 1)
            fn = I("ST2", 1)

            def _(v, bp=bp, fn=fn):
                st = stats[:, 32 + 8 * bp:32 + 8 * (bp + 1)]
                vsync(v, v.tensor_single_scalar(st[:, 0:1], st[:, 0:1],
                                                1.0 / H, Alu.mult))
                vsync(v, v.tensor_single_scalar(st[:, 1:2], st[:, 1:2],
                                                1.0 / H, Alu.mult))
                vsync(v, v.tensor_tensor(st[:, 2:3], st[:, 0:1], st[:, 0:1],
                                         Alu.mult))
                vsync(v, v.tensor_tensor(st[:, 2:3], st[:, 1:2], st[:, 2:3],
                                         Alu.subtract))
                fn(v.tensor_single_scalar(st[:, 3:4], st[:, 2:3], EPS, Alu.add))
            E("vector", _)
            st2a = mk.val["ST2"]
            W("scalar", "ST2", st2a)
            fn = I("SQ2", 1)

            def _(sc, bp=bp, fn=fn):
                st = stats[:, 32 + 8 * bp:32 + 8 * (bp + 1)]
                fn(sc.activation(st[:, 4:5], st[:, 3:4], Act.Sqrt))
            E("scalar", _)
            sq2v = mk.val["SQ2"]
            W("vector", "SQ2", sq2v)
            W("vector", "LN2A", g4 - 1)
            fn = I("ST2", 1)

            def _(v, bp=bp, fn=fn):
                st = stats[:, 32 + 8 * bp:32 + 8 * (bp + 1)]
                vsync(v, v.reciprocal(st[:, 5:6], st[:, 4:5]))
                vsync(v, v.tensor_tensor(st[:, 5:6], st[:, 3:4], st[:, 5:6],
                                         Alu.mult))
                vsync(v, v.tensor_tensor(st[:, 4:5], st[:, 4:5], st[:, 5:6],
                                         Alu.add))
                vsync(v, v.tensor_single_scalar(st[:, 4:5], st[:, 4:5], 0.5,
                                                Alu.mult))
                vsync(v, v.reciprocal(st[:, 5:6], st[:, 4:5]))
                vsync(v, v.tensor_tensor(st[:, 6:7], st[:, 0:1], st[:, 5:6],
                                         Alu.mult))
                fn(v.tensor_single_scalar(st[:, 6:7], st[:, 6:7], -1.0,
                                          Alu.mult))
            E("vector", _)
            st2b = mk.val["ST2"]
            W("scalar", "ST2", st2b)
            fn = I("LN2A", 1)

            def _(sc, tt=tt, bp=bp, st4=st4, fn=fn):
                st = stats[:, 32 + 8 * bp:32 + 8 * (bp + 1)]
                fn(sc.activation(fslice(R_LN2, 2048),
                                 fslice(R_R1P[st4 % 2] + tt * 2048, 2048),
                                 Act.Identity,
                                 bias=st[:, 6:7], scale=st[:, 5:6]))
            E("scalar", _)
            ln2v = mk.val["LN2A"]
            emit_qd(2048,
                    fslice(R_LN2, 2048),
                    bslice(R_ASTG + bp * 2048, 2048),
                    R_QDS,
                    R_TQ, R_TA, R_TB, R_TS,
                    v_pre_waits=[("LN2A", ln2v), ("AW", 16 * (g4 - 1))],
                    done_inc="QDV")
        if st4 >= 1:
            for tt in range(4):
                g4p = 4 * (st4 - 1) + tt
                ti = tt % 2
                W("vector", "FFT", ff_done_at[(st4 - 1, tt)])
                W("vector", "YW", 16 * (g4p - 1))
                fn = I("FO", 1)

                def _(v, tt=tt, ti=ti, st4=st4, fn=fn):
                    reg = psum[:, 2048 * ti:2048 * (ti + 1)]
                    out = fslice(R_OUT[tt % 2], 2048)
                    v.tensor_tensor(out, reg, bias_sb[:], Alu.add)
                    fn(v.tensor_tensor(
                        out, out,
                        fslice(R_R1P[(st4 - 1) % 2] + tt * 2048, 2048),
                        Alu.add))
                E("vector", _)
    for tt in range(4):
        g4p = 4 * (P4N - 1) + tt
        ti = tt % 2
        W("vector", "FFT", ff_done_at[(P4N - 1, tt)])
        W("vector", "YW", 16 * (g4p - 1))
        fn = I("FO", 1)

        def _(v, tt=tt, ti=ti, fn=fn):
            reg = psum[:, 2048 * ti:2048 * (ti + 1)]
            out = fslice(R_OUT[tt % 2], 2048)
            v.tensor_tensor(out, reg, bias_sb[:], Alu.add)
            fn(v.tensor_tensor(out, out,
                               fslice(R_R1P[1] + tt * 2048, 2048), Alu.add))
        E("vector", _)

    W("gpsimd", "YW", 64 * P4N)
    mk.check_deadlock()
    mk.mark("full")
    import os as _os
    cut = _os.environ.get("MK_PHASE_CUT")
    if cut:
        m = mk.marks[cut]
        for e in mk.thunks:
            mk.thunks[e] = mk.thunks[e][:m[e]]
    with nc.Block() as block:
        block.sync(lambda e: [f(e) for f in mk.thunks["sync"]])
        block.gpsimd(lambda e: [f(e) for f in mk.thunks["gpsimd"]])
        block.tensor(lambda e: [f(e) for f in mk.thunks["tensor"]])
        block.vector(lambda e: [f(e) for f in mk.thunks["vector"]])
        block.scalar(lambda e: [f(e) for f in mk.thunks["scalar"]])
    es.close()
    return nc


_NC_CACHE = None


def kernel(L0_attn_hidden_states, L0_attn_Wq, L0_attn_Wk, L0_attn_Wv,
           L0_attn_Wo, L0_ffn_weight, L0_ffn_bias):
    global _NC_CACHE
    x = np.ascontiguousarray(np.asarray(L0_attn_hidden_states, dtype=np.float32)
                             .reshape(B * S, H))
    wq = np.asarray(L0_attn_Wq, dtype=np.float32)
    wk = np.asarray(L0_attn_Wk, dtype=np.float32)
    wv = np.asarray(L0_attn_Wv, dtype=np.float32)
    wo = np.asarray(L0_attn_Wo, dtype=np.float32)
    wf = np.asarray(L0_ffn_weight, dtype=np.float32)
    bias = np.asarray(L0_ffn_bias, dtype=np.float32)

    def tiled(w):
        wt = w.T.reshape(NHT, 128, NHT, 128)
        return np.ascontiguousarray(
            wt.transpose(2, 1, 0, 3).reshape(NHT, 128, H))

    wq_tl, wk_tl, wv_tl = tiled(wq), tiled(wk), tiled(wv)
    wo_t = np.ascontiguousarray(wo.T)
    bias_rep = np.ascontiguousarray(np.broadcast_to(bias[None, :], (128, H)))
    cos, sin = _rope_tables()
    masks = _masks()

    rows = np.arange(128, dtype=np.int32)
    in_maps = []
    for c in range(NCORE):
        gidx = c % 4
        pos0 = 2048 * gidx
        cos_sl = np.ascontiguousarray(cos[pos0:pos0 + TOK].T)
        sin_sl_t = sin[pos0:pos0 + TOK].T
        sin_sl = np.ascontiguousarray(
            np.concatenate([-sin_sl_t[:64], sin_sl_t[64:]], axis=0))
        qsel = np.zeros((128, 48), dtype=np.int32)
        for hi in range(4):
            for sec in range(3):
                for r in range(4):
                    qsel[:, 12 * hi + 4 * sec + r] = (
                        6144 * r + 2048 * sec + 128 * (4 * gidx + hi) + rows)
        aosel_ix = np.zeros((128, 64), dtype=np.int32)
        for st4 in range(4):
            for r in range(4):
                for q4 in range(4):
                    aosel_ix[:, 16 * st4 + 4 * r + q4] = (
                        8192 * r + 2048 * gidx + 512 * st4 + 128 * q4 + rows)
        in_maps.append({
            "x_sl": np.ascontiguousarray(x[2048 * c:2048 * (c + 1)]),
            "wq_tl": wq_tl, "wk_tl": wk_tl, "wv_tl": wv_tl,
            "wo_t": wo_t,
            "wffn_sl": np.ascontiguousarray(wf[:, 512 * gidx:512 * (gidx + 1)]),
            "bias_b": bias_rep,
            "cos_t": cos_sl, "sin_t": sin_sl,
            "masks_i": masks,
            "qsel_i": qsel, "aosel_i": aosel_ix,
        })

    if _NC_CACHE is None:
        _NC_CACHE = build_program()
    res = run_bass_kernel_spmd(_NC_CACHE, in_maps, core_ids=list(range(NCORE)))
    out = np.concatenate([res.results[c]["y_sl"] for c in range(NCORE)], axis=0)
    return out.reshape(B, S, H).astype(np.float32)


# version 08
# speedup vs baseline: 1.4935x; 1.3551x over previous
import math

import numpy as np
import ml_dtypes

import concourse.bass as bass
import concourse.mybir as mybir
from concourse.bass_utils import run_bass_kernel_spmd
from concourse.bass_core_rust import IndirectOffsetOnAxis

F32 = mybir.dt.float32
BF16 = mybir.dt.bfloat16
I32 = mybir.dt.int32
bf16 = ml_dtypes.bfloat16
Act = mybir.ActivationFunctionType
Alu = mybir.AluOpType
AX = mybir.AxisListType

NCORE = 8
H = 2048
NH = 16
HD = 128
S = 8192
B = 2
TOK = 2048
NHT = 16
SC_SCORE = float(np.float32(1.0 / math.sqrt(HD)))
EXP_BIAS = -12.0
EPS = 1e-5

QD_BOUND_A = [0.25, 0.75, 1.25, 1.75]
QD_BOUND_B = [2.5, 3.5]
QD_BOUND_C = 5.0
QD_EMIN, QD_EMAX = -8, 5

GROUPS = [[0, 1, 2, 3], [4, 5, 6, 7]]


def _rope_tables():
    theta = 10000.0
    inv_freq = 1.0 / theta ** (np.arange(0, HD, 2, dtype=np.float32) / HD)
    t = np.arange(S, dtype=np.float32)
    freqs = np.einsum("i,j->ij", t, inv_freq)
    emb = np.concatenate((freqs, freqs), axis=-1)
    return np.cos(emb), np.sin(emb)


def _masks():
    ql = np.arange(512)[None, :]
    kl = np.arange(128)[:, None]
    m = np.concatenate(
        [(ql >= 128 * r + kl).astype(np.float32) for r in range(4)], axis=1
    )
    return m.astype(bf16)


class MK:

    def __init__(self, nc, es):
        self.nc = nc
        self.thunks = {e: [] for e in ("sync", "gpsimd", "tensor", "vector",
                                       "scalar")}
        self.events = {e: [] for e in self.thunks}
        self.pending_incs = []
        self.sems = {}
        self.val = {}
        self.waited = {}
        self.es = es

    def sem(self, name):
        self.sems[name] = self.es.enter_context(self.nc.semaphore(name))
        self.val[name] = 0

    def emit(self, eng, fn):
        self.flush_incs(eng)
        self.thunks[eng].append(fn)

    def wait(self, eng, name, value):
        if value <= 0:
            return
        key = (eng, name)
        if self.waited.get(key, 0) >= value:
            return
        self.waited[key] = value
        self.events[eng].append(("w", name, value))
        h = self.sems[name]
        self.emit(eng, lambda e, h=h, v=value: e.wait_ge(h, v))

    def inc(self, name, n):
        self.val[name] += n
        self._cur_eng_hint = None
        self.pending_incs.append((name, n))
        h = self.sems[name]
        return lambda instr: instr.then_inc(h, n)

    def flush_incs(self, eng):
        for name, n in self.pending_incs:
            self.events[eng].append(("i", name, n))
        self.pending_incs = []

    def check_deadlock(self):
        import collections
        pos = {e: 0 for e in self.events}
        cnt = collections.defaultdict(int)
        progressed = True
        while progressed:
            progressed = False
            for e, evs in self.events.items():
                while pos[e] < len(evs):
                    kind, name, v = evs[pos[e]]
                    if kind == "i":
                        cnt[name] += v
                        pos[e] += 1
                        progressed = True
                    else:
                        if cnt[name] >= v:
                            pos[e] += 1
                            progressed = True
                        else:
                            break
        stuck = {e: (pos[e], len(evs)) for e, evs in self.events.items()
                 if pos[e] < len(evs)}
        if stuck:
            print("DEADLOCK:")
            for e, (p, n) in stuck.items():
                kind, name, v = self.events[e][p]
                print(f"  {e}@{p}/{n}: waiting {name} >= {v} (has {cnt[name]})")
            return False
        print("schedule OK (no deadlock)")
        return True

    def mark(self, name):
        self.marks = getattr(self, "marks", {})
        self.marks[name] = {e: len(t) for e, t in self.thunks.items()}


def build_program():
    import contextlib
    es = contextlib.ExitStack()
    nc = bass.Bass("TRN2", target_bir_lowering=False, debug=False)
    mk = MK(nc, es)

    def din(name, shape, dt=F32):
        return nc.dram_tensor(name, shape, dt, kind="ExternalInput").ap()

    x_sl = din("x_sl", [TOK, H])
    wq_tl = din("wq_tl", [NHT, 128, H])
    wk_tl = din("wk_tl", [NHT, 128, H])
    wv_tl = din("wv_tl", [NHT, 128, H])
    wo_t = din("wo_t", [H, H])
    wffn_sl = din("wffn_sl", [H, 512])
    bias_b = din("bias_b", [128, H])
    cos_t = din("cos_t", [128, TOK])
    sin_t = din("sin_t", [128, TOK])
    masks_i = din("masks_i", [128, 2048], BF16)
    qsel_i = din("qsel_i", [128, 48], I32)
    aosel_i = din("aosel_i", [128, 64], I32)

    y_sl = nc.dram_tensor("y_sl", [TOK, H], F32, kind="ExternalOutput").ap()

    def dint(name, shape, dt=BF16):
        return nc.dram_tensor(name, shape, dt, kind="Internal").ap()

    ln_scr = dint("ln_scr", [TOK, H])
    wqd_scr = dint("wqd_scr", [H, 512])
    wt_in = dint("wt_in", [512, H])
    wt_out = dint("wt_out", [H, H])
    qkv_in = dint("qkv_in", [3 * H, TOK])
    qkv_out = dint("qkv_out", [4 * 3 * H, TOK])
    vsel = dint("vsel", [2 * 512, H])
    ao_in = dint("ao_in", [S, 512])
    ao_out = dint("ao_out", [4 * S, 512])
    aosel = dint("aosel", [2 * TOK, 512])
    a_scr = dint("a_scr", [TOK, H])

    POOL_COLS = 99840
    pool = es.enter_context(nc.sbuf_tensor("pool", [128, POOL_COLS], BF16))
    bias_sb = es.enter_context(nc.sbuf_tensor("bias_sb", [128, H], F32))
    stats = es.enter_context(nc.sbuf_tensor("stats", [128, 64], F32))
    cst_t = es.enter_context(nc.sbuf_tensor("cst_t", [128, 8], F32))
    qsel_sb = es.enter_context(nc.sbuf_tensor("qsel_sb", [128, 48], I32))
    aosel_sb = es.enter_context(nc.sbuf_tensor("aosel_sb", [128, 64], I32))
    psum = es.enter_context(nc.psum_tensor("ps", [128, 4096], F32))

    def bslice(off, n):
        return pool[:, off:off + n]

    def fslice(off, n):
        return pool[:, 2 * off:2 * (off + n)].bitcast(F32)

    R_LNT = 0
    R_LNSTG = 32768
    R_XSTG = 18432
    R_COS = 22528
    R_SIN = 24576
    R_MASK = 53248
    R_WSTG = 55296
    R_QRAW = 59392
    R_QSWP = 63488
    R_T1 = 33792
    R_T2 = 16384
    R_QOUT = 71680
    R_VSTG = 75776
    R_WQSTG = 39936
    R_WQTMP = 40960
    R_WQOUT = 87040
    R_WTT = 89088
    R_WQS = 46592
    R_Q = [0, 8192]
    R_K = [16384, 24576]
    R_V = [32768, 41984]
    R_PT = [51200, 52224]
    R_AOS = [59520, 60032]
    R_DEN = 27648
    R_VG3 = 55424
    R_AOT = 0
    R_AT = 8192
    R_WOS = 16384
    R_WTS = 24576
    R_ASTG = 32768
    R_R1P = [18432, 26624]
    R_LN2 = 34816
    R_TQ = 36864
    R_TA = 38912
    R_TB = 40960
    R_TS = 43008
    R_OUT = [45056, 47104]
    R_QDS = 49152
    R_AOG4 = 98816

    SC = SC_SCORE
    CONSTS = [EXP_BIAS, SC]
    for i, v in enumerate(CONSTS):
        nc.const_aps.register(F32, v, cst_t[:, i:i + 1])

    for s in ("CS", "PR", "X1", "SQ1", "LS", "LA", "LW", "LT",
              "WQL", "WQD", "QDA", "WQW", "WTT", "WTI", "AG1",
              "WS", "PQ", "EV", "SW", "RP", "QKW", "VW", "AG2",
              "HLQ", "VG", "VSW", "VTR", "STm", "EX", "MKm", "PF", "NR",
              "AO", "AG3",
              "AOG", "AOW", "AOT", "X2", "OPT", "OP", "R1", "ST2", "SQ2",
              "LN2A", "QDV", "AW", "AT", "WOL", "WFL", "FFT", "FF", "FO",
              "YW", "VSS"):
        mk.sem(s)
    _vss_n = [0]

    def vsync(v, ins):
        _vss_n[0] += 1
        ins.then_inc(mk.sems["VSS"], 1)
        v.wait_ge(mk.sems["VSS"], _vss_n[0])

    E, W, I = mk.emit, mk.wait, mk.inc

    def emit_qd(ncols, src_f32, dst_bf16, r_small, r_q, r_a, r_b, r_s,
                v_pre_waits, done_inc):
        nb = ncols // 32
        for (sname, v) in v_pre_waits:
            W("vector", sname, v)

        def _(v, src=src_f32, nb=nb):
            amax = fslice(r_small, nb)
            scale = fslice(r_small + nb, nb)
            src3 = src.rearrange("p (a b) -> p a b", b=32)
            vsync(v, v.tensor_reduce(amax.unsqueeze(2), src3, AX.X, Alu.max,
                                     apply_absolute_value=True))
            v.memset(scale, 2.0 ** (QD_EMIN - 3))
            es_ = list(range(QD_EMIN, QD_EMAX + 1))
            gbuf = [fslice(r_small + 3 * nb, nb), fslice(r_q, nb)]
            for i, e in enumerate(es_):
                v.tensor_scalar(gbuf[i % 2], amax, float(2.0 ** e),
                                float(2.0 ** (e - 3)), Alu.is_ge, Alu.mult)
                if i >= 1:
                    v.tensor_tensor(scale, scale, gbuf[(i - 1) % 2], Alu.add)
            vsync(v, v.tensor_tensor(scale, scale,
                                     gbuf[(len(es_) - 1) % 2], Alu.add))
            rscale = fslice(r_small + 2 * nb, nb)
            vsync(v, v.reciprocal(rscale, scale))
            q = fslice(r_q, ncols)
            rs_b = rscale.unsqueeze(2).broadcast_to([128, nb, 32])
            v.tensor_tensor(q.rearrange("p (a b) -> p a b", b=32), src3, rs_b,
                            Alu.mult)
        E("vector", _)
        qfn = I("QDA", 1)

        def _(v, fn=qfn):
            fn(v.engine_nop())
        E("vector", _)
        q_ready = mk.val["QDA"]
        W("scalar", "QDA", q_ready)
        afn = I("QDA", 1)

        def _(sc, fn=afn):
            q = fslice(r_q, ncols)
            sc.activation(fslice(r_a, ncols), q, Act.Abs)
            fn(sc.activation(fslice(r_s, ncols), q, Act.Sign))
        E("scalar", _)
        as_ready = mk.val["QDA"]
        W("vector", "QDA", as_ready)
        dfn = I(done_inc, 1)

        def _(v, nb=nb, fn=dfn):
            a = fslice(r_a, ncols)
            A = fslice(r_q, ncols)
            bb = fslice(r_b, ncols)
            sg = fslice(r_s, ncols)
            scale = fslice(r_small + nb, nb)
            sc_b = scale.unsqueeze(2).broadcast_to([128, nb, 32])
            v.tensor_single_scalar(A, a, QD_BOUND_A[0], Alu.is_ge)
            for bnd in QD_BOUND_A[1:]:
                v.scalar_tensor_tensor(A, a, bnd, A, Alu.is_ge, Alu.add)
            v.tensor_single_scalar(bb, a, QD_BOUND_B[0], Alu.is_ge)
            v.scalar_tensor_tensor(bb, a, QD_BOUND_B[1], bb, Alu.is_ge, Alu.add)
            v.scalar_tensor_tensor(bb, A, 0.5, bb, Alu.mult, Alu.add)
            v.tensor_scalar(a, a, QD_BOUND_C, 2.0, Alu.is_ge, Alu.mult)
            v.tensor_tensor(bb, bb, a, Alu.add)
            v.tensor_tensor(bb, bb, sg, Alu.mult)
            fn(v.tensor_tensor(dst_bf16.rearrange("p (a b) -> p a b", b=32),
                               bb.rearrange("p (a b) -> p a b", b=32), sc_b,
                               Alu.mult))
        E("vector", _)

    def _(v):
        for i, c in enumerate(CONSTS):
            v.memset(cst_t[:, i:i + 1], c)
    E("vector", _)
    csfn = I("CS", 1)

    def _(v, fn=csfn):
        fn(v.engine_nop())
    E("vector", _)

    prfn = [I("PR", 16) for _ in range(6)]

    def _(s, fns=prfn):
        fns[0](s.dma_start(out=fslice(R_COS, H), in_=cos_t))
        fns[1](s.dma_start(out=fslice(R_SIN, H), in_=sin_t))
        fns[2](s.dma_start(out=bslice(R_MASK, 2048), in_=masks_i))
        fns[3](s.dma_start(out=bias_sb[:], in_=bias_b))
        fns[4](s.dma_start(out=qsel_sb[:], in_=qsel_i))
        fns[5](s.dma_start(out=aosel_sb[:], in_=aosel_i))
    E("sync", _)
    PR_ALL = mk.val["PR"]

    mk.mark("prep")
    for i in range(16):
        bp = i % 2
        W("scalar", "WQD", i - 1)
        fn = I("WQL", 16)

        def _(s, i=i, bp=bp, fn=fn):
            fn(s.dma_start(out=fslice(R_WQSTG + bp * 512, 512),
                           in_=wffn_sl[128 * i:128 * (i + 1), :]))
        E("scalar", _)
        emit_qd(512,
                fslice(R_WQSTG + bp * 512, 512),
                bslice(R_WQOUT + bp * 512, 512),
                R_WQS,
                R_WQTMP, R_WQTMP + 512, R_WQTMP + 1024, R_WQTMP + 1536,
                v_pre_waits=[("CS", 1), ("WQL", 16 * (i + 1)),
                             ("WQW", 16 * (i - 1))],
                done_inc="WQD")
        if i >= 1:
            W("scalar", "WQD", i)
            fn = I("WQW", 16)

            def _(s, i=i, fn=fn):
                fn(s.dma_start(out=wqd_scr[128 * (i - 1):128 * i, :],
                               in_=bslice(R_WQOUT + ((i - 1) % 2) * 512, 512)))
            E("scalar", _)
    W("scalar", "WQD", 16)
    fn = I("WQW", 16)

    def _(s, fn=fn):
        fn(s.dma_start(out=wqd_scr[128 * 15:128 * 16, :],
                       in_=bslice(R_WQOUT + 512, 512)))
    E("scalar", _)
    W("sync", "WQW", 256)
    for k in range(4):
        W("sync", "WTI", 16 * (k - 1))
        fn = I("WTT", 16)

        def _(s, k=k, fn=fn):
            fn(s.dma_start_transpose(out=bslice(R_WTT + (k % 2) * 2048, 2048),
                                     in_=wqd_scr[:, 128 * k:128 * (k + 1)]))
        E("sync", _)
        W("sync", "WTT", 16 * (k + 1))
        fn = I("WTI", 16)

        def _(s, k=k, fn=fn):
            fn(s.dma_start(out=wt_in[128 * k:128 * (k + 1), :],
                           in_=bslice(R_WTT + (k % 2) * 2048, 2048)))
        E("sync", _)

    mk.mark("wqd")
    for t in range(16):
        bp = t % 2
        W("sync", "LA", t - 1)
        fn = I("X1", 16)

        def _(s, t=t, bp=bp, fn=fn):
            fn(s.dma_start(out=fslice(R_XSTG + bp * 2048, 2048),
                           in_=x_sl[128 * t:128 * (t + 1), :]))
        E("sync", _)
        if t >= 1:
            W("sync", "LA", t)
            fn = I("LW", 16)

            def _(s, t=t, fn=fn):
                fn(s.dma_start(out=ln_scr[128 * (t - 1):128 * t, :],
                               in_=bslice(R_LNSTG + ((t - 1) % 2) * 2048, 2048)))
            E("sync", _)
    W("sync", "LA", 16)
    fn = I("LW", 16)

    def _(s, fn=fn):
        fn(s.dma_start(out=ln_scr[128 * 15:128 * 16, :],
                       in_=bslice(R_LNSTG + 2048, 2048)))
    E("sync", _)
    for t in range(16):
        bp = t % 2
        W("scalar", "X1", 16 * (t + 1))
        W("scalar", "LS", 2 * t - 3)
        fn = I("SQ1", 1)

        def _(sc, bp=bp, fn=fn):
            fn(sc.activation(fslice(R_T1, 2048), fslice(R_XSTG + bp * 2048, 2048),
                             Act.Square,
                             accum_out=stats[:, 8 * bp + 1:8 * bp + 2]))
        E("scalar", _)
        sq1v = mk.val["SQ1"]
        W("vector", "X1", 16 * (t + 1))

        def _(v, bp=bp):
            vsync(v, v.reduce_sum(stats[:, 8 * bp:8 * bp + 1],
                                  fslice(R_XSTG + bp * 2048, 2048), AX.X))
        E("vector", _)
        W("vector", "SQ1", sq1v)
        W("vector", "SQ2", t - 1)
        fn = I("LS", 1)

        def _(v, bp=bp, fn=fn):
            st = stats[:, 8 * bp:8 * (bp + 1)]
            vsync(v, v.tensor_single_scalar(st[:, 0:1], st[:, 0:1], 1.0 / H,
                                            Alu.mult))
            vsync(v, v.tensor_single_scalar(st[:, 1:2], st[:, 1:2], 1.0 / H,
                                            Alu.mult))
            vsync(v, v.tensor_tensor(st[:, 2:3], st[:, 0:1], st[:, 0:1],
                                     Alu.mult))
            vsync(v, v.tensor_tensor(st[:, 2:3], st[:, 1:2], st[:, 2:3],
                                     Alu.subtract))
            fn(v.tensor_single_scalar(st[:, 3:4], st[:, 2:3], EPS, Alu.add))
        E("vector", _)
        ls1 = mk.val["LS"]
        W("scalar", "LS", ls1)
        fn = I("SQ2", 1)

        def _(sc, bp=bp, fn=fn):
            st = stats[:, 8 * bp:8 * (bp + 1)]
            fn(sc.activation(st[:, 4:5], st[:, 3:4], Act.Sqrt))
        E("scalar", _)
        sq2v = mk.val["SQ2"]
        W("vector", "SQ2", sq2v)
        W("vector", "LA", t - 1)
        fn = I("LS", 1)

        def _(v, bp=bp, fn=fn):
            st = stats[:, 8 * bp:8 * (bp + 1)]
            vsync(v, v.reciprocal(st[:, 5:6], st[:, 4:5]))
            vsync(v, v.tensor_tensor(st[:, 5:6], st[:, 3:4], st[:, 5:6],
                                     Alu.mult))
            vsync(v, v.tensor_tensor(st[:, 4:5], st[:, 4:5], st[:, 5:6],
                                     Alu.add))
            vsync(v, v.tensor_single_scalar(st[:, 4:5], st[:, 4:5], 0.5,
                                            Alu.mult))
            vsync(v, v.reciprocal(st[:, 5:6], st[:, 4:5]))
            vsync(v, v.tensor_tensor(st[:, 6:7], st[:, 0:1], st[:, 5:6],
                                     Alu.mult))
            fn(v.tensor_single_scalar(st[:, 6:7], st[:, 6:7], -1.0, Alu.mult))
        E("vector", _)
        ls2 = mk.val["LS"]
        W("scalar", "LS", ls2)
        W("scalar", "LW", 16 * (t - 1))
        fn = I("LA", 1)

        def _(sc, bp=bp, fn=fn):
            st = stats[:, 8 * bp:8 * (bp + 1)]
            fn(sc.activation(bslice(R_LNSTG + bp * 2048, 2048),
                             fslice(R_XSTG + bp * 2048, 2048), Act.Identity,
                             bias=st[:, 6:7], scale=st[:, 5:6]))
        E("scalar", _)
    W("sync", "LW", 256)
    for h in range(NHT):
        fn = I("LT", 16)

        def _(s, h=h, fn=fn):
            fn(s.dma_start_transpose(out=bslice(R_LNT + h * 2048, 2048),
                                     in_=ln_scr[:, 128 * h:128 * (h + 1)]))
        E("sync", _)

    mk.mark("ln")
    WTILED = {0: wq_tl, 1: wk_tl, 2: wv_tl}
    qk_idx = 0
    v_idx = 0
    pending_qk_write = None
    for d in range(NHT):
        for p in range(3):
            g = 3 * d + p
            gp = g % 2
            W("gpsimd", "PQ", g - 1)
            fn = I("WS", 16)

            def _(gg, d=d, p=p, gp=gp, fn=fn):
                fn(gg.dma_start(out=bslice(R_WSTG + gp * 2048, 2048),
                                in_=WTILED[p][d]))
            E("gpsimd", _)
            W("tensor", "LT", 256)
            W("tensor", "WS", 16 * (g + 1))
            W("tensor", "EV", g - 1)
            fn = I("PQ", 1)

            def _(te, gp=gp, fn=fn):
                reg = psum[:, 2048 * gp:2048 * (gp + 1)]
                wst = bslice(R_WSTG + gp * 2048, 2048)
                ins = None
                for ht in range(NHT):
                    lhsT = wst[:, 128 * ht:128 * (ht + 1)]
                    lnt = bslice(R_LNT + ht * 2048, 2048)
                    for ch in range(4):
                        ins = te.matmul(reg[:, 512 * ch:512 * (ch + 1)], lhsT,
                                        lnt[:, 512 * ch:512 * (ch + 1)],
                                        start=(ht == 0), stop=(ht == NHT - 1))
                fn(ins)
            E("tensor", _)
            W("scalar", "PQ", g + 1)
            if p < 2:
                qi = qk_idx
                W("scalar", "RP", qi - 1)
                fn = I("EV", 1)

                def _(sc, gp=gp, qi=qi, fn=fn):
                    fn(sc.activation(bslice(R_QRAW + (qi % 2) * 2048, 2048),
                                     psum[:, 2048 * gp:2048 * (gp + 1)],
                                     Act.Copy))
                E("scalar", _)
                W("sync", "EV", mk.val["EV"])
                fn1, fn2 = I("SW", 16), I("SW", 16)

                def _(s, qi=qi, fn1=fn1, fn2=fn2):
                    qr = bslice(R_QRAW + (qi % 2) * 2048, 2048)
                    qs = bslice(R_QSWP + (qi % 2) * 2048, 2048)
                    fn1(s.dma_start(out=qs[0:64, :], in_=qr[64:128, :]))
                    fn2(s.dma_start(out=qs[64:128, :], in_=qr[0:64, :]))
                E("sync", _)
                if pending_qk_write is not None:
                    pqi, prow0, prp = pending_qk_write
                    W("sync", "RP", prp)
                    fn = I("QKW", 16)

                    def _(s, pqi=pqi, prow0=prow0, fn=fn):
                        fn(s.dma_start(out=qkv_in[prow0:prow0 + 128, :],
                                       in_=bslice(R_QOUT + (pqi % 2) * 2048,
                                                  2048)))
                    E("sync", _)
                W("vector", "PR", PR_ALL)
                W("vector", "SW", 32 * (qi + 1))
                W("vector", "QKW", 16 * (qi - 1))
                fn = I("RP", 1)

                def _(v, qi=qi, fn=fn):
                    qr = bslice(R_QRAW + (qi % 2) * 2048, 2048)
                    qs = bslice(R_QSWP + (qi % 2) * 2048, 2048)
                    t1 = fslice(R_T1, 2048)
                    t2 = fslice(R_T2, 2048)
                    v.tensor_tensor(t1, qr, fslice(R_COS, 2048), Alu.mult)
                    v.tensor_tensor(t2, qs, fslice(R_SIN, 2048), Alu.mult)
                    fn(v.tensor_tensor(bslice(R_QOUT + (qi % 2) * 2048, 2048),
                                       t1, t2, Alu.add))
                E("vector", _)
                row0 = 2048 * p + 128 * d
                pending_qk_write = (qi, row0, mk.val["RP"])
                qk_idx += 1
            else:
                vi = v_idx
                W("scalar", "VW", 16 * (vi - 1))
                fn = I("EV", 1)

                def _(sc, gp=gp, vi=vi, fn=fn):
                    fn(sc.activation(bslice(R_VSTG + (vi % 2) * 2048, 2048),
                                     psum[:, 2048 * gp:2048 * (gp + 1)],
                                     Act.Copy))
                E("scalar", _)
                W("sync", "EV", mk.val["EV"])
                row0 = 4096 + 128 * d
                fn = I("VW", 16)

                def _(s, vi=vi, row0=row0, fn=fn):
                    fn(s.dma_start(out=qkv_in[row0:row0 + 128, :],
                                   in_=bslice(R_VSTG + (vi % 2) * 2048, 2048)))
                E("sync", _)
                v_idx += 1
    pqi, prow0, prp = pending_qk_write
    W("sync", "RP", prp)
    fn = I("QKW", 16)

    def _(s, pqi=pqi, prow0=prow0, fn=fn):
        fn(s.dma_start(out=qkv_in[prow0:prow0 + 128, :],
                       in_=bslice(R_QOUT + (pqi % 2) * 2048, 2048)))
    E("sync", _)
    assert mk.val["QKW"] == 16 * 32 and mk.val["VW"] == 16 * 16

    W("gpsimd", "WTI", 64)
    fn = I("AG1", 1)

    def _(gg, fn=fn):
        fn(gg.collective_compute("AllGather", Alu.bypass, replica_groups=GROUPS,
                                 ins=[wt_in], outs=[wt_out]))
    E("gpsimd", _)
    W("gpsimd", "QKW", 512)
    W("gpsimd", "VW", 256)
    fn = I("AG2", 1)

    def _(gg, fn=fn):
        fn(gg.collective_compute("AllGather", Alu.bypass, replica_groups=GROUPS,
                                 ins=[qkv_in], outs=[qkv_out]))
    E("gpsimd", _)

    mk.mark("proj")
    W("vector", "AG2", 1)
    onesdone = I("CS", 1)

    def _(v, fn=onesdone):
        for hp in range(2):
            for kt in range(64):
                v.memset(bslice(R_V[hp] + 144 * kt + 128, 1), 1.0)
        fn(v.engine_nop())
    E("vector", _)
    ONES_DONE = mk.val["CS"]

    KT_PER_HEAD = sum(4 * (qb + 1) for qb in range(16))
    st_cum = ex_cum = mkm_cum = pf_cum = nr_cum = ao_cum = 0
    vg_idx = 0

    for hi in range(4):
        hp = hi % 2
        W("gpsimd", "AG2", 1)
        W("gpsimd", "PR", PR_ALL)
        if hi >= 2:
            W("gpsimd", "STm", KT_PER_HEAD * (hi - 1))
            W("gpsimd", "PF", KT_PER_HEAD // 2 * (hi - 1))
            W("gpsimd", "CS", ONES_DONE)
        for r in range(4):
            fn1, fn2 = I("HLQ", 16), I("HLQ", 16)

            def _(gg, hi=hi, hp=hp, r=r, fn1=fn1, fn2=fn2):
                fn1(gg.indirect_dma_start(
                    out=bslice(R_Q[hp] + 2048 * r, 2048), out_offset=None,
                    in_=qkv_out,
                    in_offset=IndirectOffsetOnAxis(
                        qsel_sb[:, 12 * hi + r:12 * hi + r + 1], 0)))
                fn2(gg.indirect_dma_start(
                    out=bslice(R_K[hp] + 2048 * r, 2048), out_offset=None,
                    in_=qkv_out,
                    in_offset=IndirectOffsetOnAxis(
                        qsel_sb[:, 12 * hi + 4 + r:12 * hi + 4 + r + 1], 0)))
            E("gpsimd", _)
        for r in range(4):
            vg = vg_idx
            W("gpsimd", "VSW", 16 * (vg - 1))
            fn = I("VG", 16)

            def _(gg, hi=hi, r=r, vg=vg, fn=fn):
                fn(gg.indirect_dma_start(
                    out=bslice(R_VG3 + (vg % 2) * 2048, 2048), out_offset=None,
                    in_=qkv_out,
                    in_offset=IndirectOffsetOnAxis(
                        qsel_sb[:, 12 * hi + 8 + r:12 * hi + 8 + r + 1], 0)))
            E("gpsimd", _)
            W("gpsimd", "VG", 16 * (vg + 1))
            if hi >= 2:
                W("gpsimd", "VTR", 1024 * (hi - 1))
            fn = I("VSW", 16)

            def _(gg, hp=hp, r=r, vg=vg, fn=fn):
                fn(gg.dma_start(out=vsel[512 * hp + 128 * r:
                                         512 * hp + 128 * (r + 1), :],
                                in_=bslice(R_VG3 + (vg % 2) * 2048, 2048)))
            E("gpsimd", _)
            vg_idx += 1
        W("sync", "VSW", 64 * (hi + 1))
        W("sync", "CS", ONES_DONE)
        if hi >= 2:
            W("sync", "PF", KT_PER_HEAD // 2 * (hi - 1))
        for r in range(4):
            for j in range(16):
                fn = I("VTR", 16)

                def _(s, hp=hp, r=r, j=j, fn=fn):
                    kt = 16 * r + j
                    fn(s.dma_start_transpose(
                        out=bslice(R_V[hp] + 144 * kt, 128),
                        in_=vsel[512 * hp + 128 * r:512 * hp + 128 * (r + 1),
                                 128 * j:128 * (j + 1)]))
                E("sync", _)

        if hi == 0:
            UNITS = []
            for h2 in range(4):
                for qb2 in range(16):
                    for ktp2 in range(2 * (qb2 + 1)):
                        UNITS.append((h2, qb2, ktp2))

            def emit_st(g):
                h2, qb2, ktp2 = UNITS[g]
                hp2 = h2 % 2
                kt0 = 2 * ktp2
                pp = g % 2
                W("tensor", "HLQ", 128 * (h2 + 1))
                W("tensor", "EX", g - 1)
                fns = [I("STm", 1), I("STm", 1)]

                def _(te, hp2=hp2, qb2=qb2, kt0=kt0, pp=pp, fns=fns):
                    for i in range(2):
                        bank = psum[:, 1024 * pp + 512 * i:
                                    1024 * pp + 512 * (i + 1)]
                        fns[i](te.matmul(bank,
                                         bslice(R_K[hp2] + 128 * (kt0 + i), 128),
                                         bslice(R_Q[hp2] + 512 * qb2, 512),
                                         start=True, stop=True))
                E("tensor", _)

            emit_st(0)
            gctr = 0

        for qb in range(16):
            qbg = 16 * hi + qb
            nkt = 4 * (qb + 1)
            for ktp in range(nkt // 2):
                g = gctr
                pp = g % 2
                kt0 = 2 * ktp
                if g + 1 < len(UNITS):
                    emit_st(g + 1)
                W("scalar", "STm", 2 * (g + 1))
                W("scalar", "PF", g - 1)
                fn = I("EX", 1)

                def _(sc, pp=pp, fn=fn):
                    fn(sc.activation(bslice(R_PT[pp], 1024),
                                     psum[:, 1024 * pp:1024 * (pp + 1)],
                                     Act.Exp, bias=EXP_BIAS, scale=SC))
                E("scalar", _)
                any_diag = False
                for i in range(2):
                    kt = kt0 + i
                    if kt >= 4 * qb:
                        rel = kt - 4 * qb
                        W("vector", "EX", g + 1)
                        fn = I("MKm", 1)

                        def _(v, pp=pp, i=i, rel=rel, fn=fn):
                            pt = bslice(R_PT[pp] + 512 * i, 512)
                            fn(v.tensor_tensor(pt, pt,
                                               bslice(R_MASK + 512 * rel, 512),
                                               Alu.mult))
                        E("vector", _)
                        mkm_cum += 1
                        any_diag = True
                if any_diag:
                    W("tensor", "MKm", mkm_cum)
                else:
                    W("tensor", "EX", g + 1)
                if ktp == 0:
                    W("tensor", "NR", qbg - 1)
                if qb == 0 and ktp == 0:
                    W("tensor", "VTR", 1024 * (hi + 1))
                fn = I("PF", 1)

                def _(te, hp=hp, kt0=kt0, pp=pp, nkt=nkt, qbg=qbg, fn=fn):
                    reg0 = 2048 + 1024 * (qbg % 2)
                    ins = None
                    for i in range(2):
                        kt = kt0 + i
                        pt = bslice(R_PT[pp] + 512 * i, 512)
                        vv = bslice(R_V[hp] + 144 * kt, 129)
                        for j in range(4):
                            col = reg0 + 512 * (j // 2) + 256 * (j % 2)
                            ins = te.matmul(
                                psum[:, col:col + 129],
                                pt[:, 128 * j:128 * (j + 1)], vv,
                                start=(kt == 0 and j % 2 == 0),
                                stop=(kt == nkt - 1 and j % 2 == 1))
                    fn(ins)
                E("tensor", _)
                gctr += 1
            W("vector", "PF", gctr)
            W("vector", "AO", 64 * (qbg - 1))
            fn = I("NR", 1)

            def _(v, qbg=qbg, fn=fn):
                reg0 = 2048 + 1024 * (qbg % 2)
                den = fslice(R_DEN, 32)
                aos = bslice(R_AOS[qbg % 2], 512)
                ins = None
                for j in range(4):
                    col = reg0 + 512 * (j // 2) + 256 * (j % 2)
                    ins = v.reciprocal(den[:, j:j + 1],
                                       psum[:, col + 128:col + 129])
                vsync(v, ins)
                for j in range(4):
                    col = reg0 + 512 * (j // 2) + 256 * (j % 2)
                    ins = v.tensor_single_scalar(
                        aos[:, 128 * j:128 * (j + 1)],
                        psum[:, col:col + 128],
                        den[:, j:j + 1], Alu.mult)
                fn(ins)
            E("vector", _)
            nr_cum += 1
            W("sync", "NR", nr_cum)
            fns = [I("AO", 16) for _ in range(4)]

            def _(s, hi=hi, qb=qb, qbg=qbg, fns=fns):
                aos = bslice(R_AOS[qbg % 2], 512)
                for j in range(4):
                    fns[j](s.dma_start(
                        out=ao_in[512 * qb + 128 * j:512 * qb + 128 * (j + 1),
                                  128 * hi:128 * (hi + 1)],
                        in_=aos[:, 128 * j:128 * (j + 1)]))
            E("sync", _)
            ao_cum += 64

    W("gpsimd", "AO", ao_cum)
    fn = I("AG3", 1)

    def _(gg, fn=fn):
        fn(gg.collective_compute("AllGather", Alu.bypass, replica_groups=GROUPS,
                                 ins=[ao_in], outs=[ao_out]))
    E("gpsimd", _)

    mk.mark("attn")
    import os as _os2
    P4N = int(_os2.environ.get("MK_P4_N", "4"))
    pe_groups = []
    for st4 in range(P4N):
        pe_groups.append(("op", st4, 0))
        pe_groups.append(("op", st4, 1))
        if st4 >= 1:
            pe_groups.append(("ffn", st4 - 1, 0))
            pe_groups.append(("ffn", st4 - 1, 1))
    pe_groups.append(("ffn", P4N - 1, 0))
    pe_groups.append(("ffn", P4N - 1, 1))

    W("gpsimd", "AG3", 1)
    aog_idx = 0
    wo_steps = 0
    wt_steps = 0
    for kind, st4, pr in pe_groups:
        if kind == "op" and pr == 0:
            if st4 >= 2:
                W("gpsimd", "AOT", 256 * (st4 - 1))
            for r in range(4):
                for q4 in range(4):
                    ag = aog_idx
                    W("gpsimd", "AOW", 16 * (ag - 1))
                    fn = I("AOG", 16)

                    def _(gg, st4=st4, r=r, q4=q4, ag=ag, fn=fn):
                        fn(gg.indirect_dma_start(
                            out=bslice(R_AOG4 + (ag % 2) * 512, 512),
                            out_offset=None,
                            in_=ao_out,
                            in_offset=IndirectOffsetOnAxis(
                                aosel_sb[:, 16 * st4 + 4 * r + q4:
                                         16 * st4 + 4 * r + q4 + 1], 0)))
                    E("gpsimd", _)
                    W("gpsimd", "AOG", 16 * (ag + 1))
                    fn = I("AOW", 16)

                    def _(gg, st4=st4, r=r, q4=q4, ag=ag, fn=fn):
                        row0 = 2048 * (st4 % 2) + 512 * r + 128 * q4
                        fn(gg.dma_start(
                            out=aosel[row0:row0 + 128, :],
                            in_=bslice(R_AOG4 + (ag % 2) * 512, 512)))
                    E("gpsimd", _)
                    aog_idx += 1
            if st4 == 0:
                mk.mark("p4ga")
        for k in range(16):
            if kind == "op":
                W("gpsimd", "OPT", wo_steps - 3)
                slot = wo_steps % 4
                fn = I("WOL", 16)

                def _(gg, k=k, slot=slot, fn=fn):
                    fn(gg.dma_start(out=bslice(R_WOS + 2048 * slot, 2048),
                                    in_=wo_t[128 * k:128 * (k + 1), :]))
                E("gpsimd", _)
                wo_steps += 1
                if wo_steps == 16:
                    mk.mark("p4gb")
            else:
                W("gpsimd", "FFT", wt_steps - 3)
                slot = wt_steps % 4
                fn = I("WFL", 16)

                def _(gg, k=k, slot=slot, fn=fn):
                    fn(gg.dma_start(out=bslice(R_WTS + 2048 * slot, 2048),
                                    in_=wt_out[128 * k:128 * (k + 1), :]))
                E("gpsimd", _)
                wt_steps += 1
    mk.mark("p4gpsimd")

    W("sync", "AG3", 1)
    for st4 in range(P4N):
        for tt in range(4):
            W("sync", "FO", 4 * (st4 - 2) + tt + 1)
            fn = I("X2", 16)

            def _(s, st4=st4, tt=tt, fn=fn):
                fn(s.dma_start(out=fslice(R_R1P[st4 % 2] + tt * 2048, 2048),
                               in_=x_sl[512 * st4 + 128 * tt:
                                        512 * st4 + 128 * (tt + 1), :]))
            E("sync", _)
        W("sync", "OPT", 32 * st4)
        W("sync", "AOW", 16 * 16 * (st4 + 1))
        for r in range(4):
            for ht in range(4):
                fn = I("AOT", 16)

                def _(s, st4=st4, r=r, ht=ht, fn=fn):
                    slot = 4 * r + ht
                    base = 2048 * (st4 % 2) + 512 * r
                    fn(s.dma_start_transpose(
                        out=bslice(R_AOT + 512 * slot, 512),
                        in_=aosel[base:base + 512, 128 * ht:128 * (ht + 1)]))
                E("sync", _)
        for tt in range(4):
            g4 = 4 * st4 + tt
            W("sync", "QDV", g4 + 1)
            fn = I("AW", 16)

            def _(s, st4=st4, tt=tt, fn=fn):
                fn(s.dma_start(
                    out=a_scr[512 * st4 + 128 * tt:512 * st4 + 128 * (tt + 1), :],
                    in_=bslice(R_ASTG + (tt % 2) * 2048, 2048)))
            E("sync", _)
        W("sync", "AW", 64 * (st4 + 1))
        W("sync", "FFT", 32 * st4)
        for k in range(16):
            fn = I("AT", 16)

            def _(s, st4=st4, k=k, fn=fn):
                fn(s.dma_start_transpose(
                    out=bslice(R_AT + 512 * k, 512),
                    in_=a_scr[512 * st4:512 * (st4 + 1),
                              128 * k:128 * (k + 1)]))
            E("sync", _)
        if st4 >= 1:
            for tt in range(4):
                W("sync", "FO", 4 * (st4 - 1) + tt + 1)
                fn = I("YW", 16)

                def _(s, st4=st4, tt=tt, fn=fn):
                    fn(s.dma_start(
                        out=y_sl[512 * (st4 - 1) + 128 * tt:
                                 512 * (st4 - 1) + 128 * (tt + 1), :],
                        in_=fslice(R_OUT[tt % 2], 2048)))
                E("sync", _)
    for tt in range(4):
        W("sync", "FO", 4 * (P4N - 1) + tt + 1)
        fn = I("YW", 16)

        def _(s, tt=tt, fn=fn):
            fn(s.dma_start(out=y_sl[512 * (P4N - 1) + 128 * tt:
                                    512 * (P4N - 1) + 128 * (tt + 1), :],
                           in_=fslice(R_OUT[tt % 2], 2048)))
        E("sync", _)

    mk.mark("p4sync")
    group_done = []
    wo_pe = 0
    wt_pe = 0
    for gi, (kind, st4, pr) in enumerate(pe_groups):
        if gi >= 1:
            sname, sval = group_done[gi - 1]
            W("tensor", sname, sval)
        if kind == "op":
            W("tensor", "AOT", 256 * (st4 + 1))
            for k in range(16):
                W("tensor", "WOL", 16 * (wo_pe + 1))
                fn = I("OPT", 1)

                def _(te, k=k, pr=pr, slot=wo_pe % 4, fn=fn):
                    wst = bslice(R_WOS + 2048 * slot, 2048)
                    ins = None
                    for ti in range(2):
                        tt = 2 * pr + ti
                        reg = psum[:, 2048 * ti:2048 * (ti + 1)]
                        for ch in range(4):
                            ins = te.matmul(
                                reg[:, 512 * ch:512 * (ch + 1)],
                                bslice(R_AOT + 512 * k + 128 * tt, 128),
                                wst[:, 512 * ch:512 * (ch + 1)],
                                start=(k == 0), stop=(k == 15))
                    fn(ins)
                E("tensor", _)
                wo_pe += 1
            group_done.append(("R1", 4 * st4 + 2 * pr + 2))
        else:
            W("tensor", "AT", 256 * (st4 + 1))
            for k in range(16):
                W("tensor", "WFL", 16 * (wt_pe + 1))
                fn = I("FFT", 1)

                def _(te, k=k, pr=pr, slot=wt_pe % 4, fn=fn):
                    wst = bslice(R_WTS + 2048 * slot, 2048)
                    ins = None
                    for ti in range(2):
                        tt = 2 * pr + ti
                        reg = psum[:, 2048 * ti:2048 * (ti + 1)]
                        for ch in range(4):
                            ins = te.matmul(
                                reg[:, 512 * ch:512 * (ch + 1)],
                                bslice(R_AT + 512 * k + 128 * tt, 128),
                                wst[:, 512 * ch:512 * (ch + 1)],
                                start=(k == 0), stop=(k == 15))
                    fn(ins)
                E("tensor", _)
                wt_pe += 1
            group_done.append(("FO", 4 * st4 + 2 * pr + 2))

    op_done_at = {}
    ff_done_at = {}
    opv = ffv = 0
    for kind, st4, pr in pe_groups:
        if kind == "op":
            opv += 16
            op_done_at[(st4, 2 * pr)] = opv
            op_done_at[(st4, 2 * pr + 1)] = opv
        else:
            ffv += 16
            ff_done_at[(st4, 2 * pr)] = ffv
            ff_done_at[(st4, 2 * pr + 1)] = ffv

    mk.mark("p4pe")
    for st4 in range(P4N):
        for tt in range(4):
            g4 = 4 * st4 + tt
            bp = tt % 2
            ti = tt % 2
            W("vector", "OPT", op_done_at[(st4, tt)])
            W("vector", "X2", 16 * (g4 + 1))
            fn = I("R1", 1)

            def _(v, ti=ti, tt=tt, st4=st4, fn=fn):
                reg = psum[:, 2048 * ti:2048 * (ti + 1)]
                r1 = fslice(R_R1P[st4 % 2] + tt * 2048, 2048)
                fn(v.tensor_tensor(r1, reg, r1, Alu.add))
            E("vector", _)
            r1v = mk.val["R1"]

            def _(v, tt=tt, bp=bp, st4=st4):
                vsync(v, v.reduce_sum(stats[:, 32 + 8 * bp:32 + 8 * bp + 1],
                                      fslice(R_R1P[st4 % 2] + tt * 2048, 2048),
                                      AX.X))
            E("vector", _)
            W("scalar", "R1", r1v)
            W("scalar", "ST2", 2 * g4 - 3)
            W("scalar", "QDA", 2 * g4 + 31)
            fn = I("SQ1", 1)

            def _(sc, tt=tt, bp=bp, st4=st4, fn=fn):
                fn(sc.activation(fslice(R_LN2, 2048),
                                 fslice(R_R1P[st4 % 2] + tt * 2048, 2048),
                                 Act.Square,
                                 accum_out=stats[:, 32 + 8 * bp + 1:
                                                 32 + 8 * bp + 2]))
            E("scalar", _)
            sq1v = mk.val["SQ1"]
            W("vector", "SQ1", sq1v)
            W("vector", "SQ2", 16 + g4 - 1)
            fn = I("ST2", 1)

            def _(v, bp=bp, fn=fn):
                st = stats[:, 32 + 8 * bp:32 + 8 * (bp + 1)]
                vsync(v, v.tensor_single_scalar(st[:, 0:1], st[:, 0:1],
                                                1.0 / H, Alu.mult))
                vsync(v, v.tensor_single_scalar(st[:, 1:2], st[:, 1:2],
                                                1.0 / H, Alu.mult))
                vsync(v, v.tensor_tensor(st[:, 2:3], st[:, 0:1], st[:, 0:1],
                                         Alu.mult))
                vsync(v, v.tensor_tensor(st[:, 2:3], st[:, 1:2], st[:, 2:3],
                                         Alu.subtract))
                fn(v.tensor_single_scalar(st[:, 3:4], st[:, 2:3], EPS, Alu.add))
            E("vector", _)
            st2a = mk.val["ST2"]
            W("scalar", "ST2", st2a)
            fn = I("SQ2", 1)

            def _(sc, bp=bp, fn=fn):
                st = stats[:, 32 + 8 * bp:32 + 8 * (bp + 1)]
                fn(sc.activation(st[:, 4:5], st[:, 3:4], Act.Sqrt))
            E("scalar", _)
            sq2v = mk.val["SQ2"]
            W("vector", "SQ2", sq2v)
            W("vector", "LN2A", g4 - 1)
            fn = I("ST2", 1)

            def _(v, bp=bp, fn=fn):
                st = stats[:, 32 + 8 * bp:32 + 8 * (bp + 1)]
                vsync(v, v.reciprocal(st[:, 5:6], st[:, 4:5]))
                vsync(v, v.tensor_tensor(st[:, 5:6], st[:, 3:4], st[:, 5:6],
                                         Alu.mult))
                vsync(v, v.tensor_tensor(st[:, 4:5], st[:, 4:5], st[:, 5:6],
                                         Alu.add))
                vsync(v, v.tensor_single_scalar(st[:, 4:5], st[:, 4:5], 0.5,
                                                Alu.mult))
                vsync(v, v.reciprocal(st[:, 5:6], st[:, 4:5]))
                vsync(v, v.tensor_tensor(st[:, 6:7], st[:, 0:1], st[:, 5:6],
                                         Alu.mult))
                fn(v.tensor_single_scalar(st[:, 6:7], st[:, 6:7], -1.0,
                                          Alu.mult))
            E("vector", _)
            st2b = mk.val["ST2"]
            W("scalar", "ST2", st2b)
            fn = I("LN2A", 1)

            def _(sc, tt=tt, bp=bp, st4=st4, fn=fn):
                st = stats[:, 32 + 8 * bp:32 + 8 * (bp + 1)]
                fn(sc.activation(fslice(R_LN2, 2048),
                                 fslice(R_R1P[st4 % 2] + tt * 2048, 2048),
                                 Act.Identity,
                                 bias=st[:, 6:7], scale=st[:, 5:6]))
            E("scalar", _)
            ln2v = mk.val["LN2A"]
            emit_qd(2048,
                    fslice(R_LN2, 2048),
                    bslice(R_ASTG + bp * 2048, 2048),
                    R_QDS,
                    R_TQ, R_TA, R_TB, R_TS,
                    v_pre_waits=[("LN2A", ln2v), ("AW", 16 * (g4 - 1))],
                    done_inc="QDV")
        if st4 >= 1:
            for tt in range(4):
                g4p = 4 * (st4 - 1) + tt
                ti = tt % 2
                W("vector", "FFT", ff_done_at[(st4 - 1, tt)])
                W("vector", "YW", 16 * (g4p - 1))
                fn = I("FO", 1)

                def _(v, tt=tt, ti=ti, st4=st4, fn=fn):
                    reg = psum[:, 2048 * ti:2048 * (ti + 1)]
                    out = fslice(R_OUT[tt % 2], 2048)
                    v.tensor_tensor(out, reg, bias_sb[:], Alu.add)
                    fn(v.tensor_tensor(
                        out, out,
                        fslice(R_R1P[(st4 - 1) % 2] + tt * 2048, 2048),
                        Alu.add))
                E("vector", _)
    for tt in range(4):
        g4p = 4 * (P4N - 1) + tt
        ti = tt % 2
        W("vector", "FFT", ff_done_at[(P4N - 1, tt)])
        W("vector", "YW", 16 * (g4p - 1))
        fn = I("FO", 1)

        def _(v, tt=tt, ti=ti, fn=fn):
            reg = psum[:, 2048 * ti:2048 * (ti + 1)]
            out = fslice(R_OUT[tt % 2], 2048)
            v.tensor_tensor(out, reg, bias_sb[:], Alu.add)
            fn(v.tensor_tensor(out, out,
                               fslice(R_R1P[1] + tt * 2048, 2048), Alu.add))
        E("vector", _)

    W("gpsimd", "YW", 64 * P4N)
    mk.check_deadlock()
    mk.mark("full")
    import os as _os
    cut = _os.environ.get("MK_PHASE_CUT")
    if cut:
        m = mk.marks[cut]
        for e in mk.thunks:
            mk.thunks[e] = mk.thunks[e][:m[e]]
    with nc.Block() as block:
        block.sync(lambda e: [f(e) for f in mk.thunks["sync"]])
        block.gpsimd(lambda e: [f(e) for f in mk.thunks["gpsimd"]])
        block.tensor(lambda e: [f(e) for f in mk.thunks["tensor"]])
        block.vector(lambda e: [f(e) for f in mk.thunks["vector"]])
        block.scalar(lambda e: [f(e) for f in mk.thunks["scalar"]])
    es.close()
    return nc


_NC_CACHE = None


def kernel(L0_attn_hidden_states, L0_attn_Wq, L0_attn_Wk, L0_attn_Wv,
           L0_attn_Wo, L0_ffn_weight, L0_ffn_bias):
    global _NC_CACHE
    x = np.ascontiguousarray(np.asarray(L0_attn_hidden_states, dtype=np.float32)
                             .reshape(B * S, H))
    wq = np.asarray(L0_attn_Wq, dtype=np.float32)
    wk = np.asarray(L0_attn_Wk, dtype=np.float32)
    wv = np.asarray(L0_attn_Wv, dtype=np.float32)
    wo = np.asarray(L0_attn_Wo, dtype=np.float32)
    wf = np.asarray(L0_ffn_weight, dtype=np.float32)
    bias = np.asarray(L0_ffn_bias, dtype=np.float32)

    def tiled(w):
        wt = w.T.reshape(NHT, 128, NHT, 128)
        return np.ascontiguousarray(
            wt.transpose(2, 1, 0, 3).reshape(NHT, 128, H))

    wq_tl, wk_tl, wv_tl = tiled(wq), tiled(wk), tiled(wv)
    wo_t = np.ascontiguousarray(wo.T)
    bias_rep = np.ascontiguousarray(np.broadcast_to(bias[None, :], (128, H)))
    cos, sin = _rope_tables()
    masks = _masks()

    rows = np.arange(128, dtype=np.int32)
    in_maps = []
    for c in range(NCORE):
        gidx = c % 4
        pos0 = 2048 * gidx
        cos_sl = np.ascontiguousarray(cos[pos0:pos0 + TOK].T)
        sin_sl_t = sin[pos0:pos0 + TOK].T
        sin_sl = np.ascontiguousarray(
            np.concatenate([-sin_sl_t[:64], sin_sl_t[64:]], axis=0))
        qsel = np.zeros((128, 48), dtype=np.int32)
        for hi in range(4):
            for sec in range(3):
                for r in range(4):
                    qsel[:, 12 * hi + 4 * sec + r] = (
                        6144 * r + 2048 * sec + 128 * (4 * gidx + hi) + rows)
        aosel_ix = np.zeros((128, 64), dtype=np.int32)
        for st4 in range(4):
            for r in range(4):
                for q4 in range(4):
                    aosel_ix[:, 16 * st4 + 4 * r + q4] = (
                        8192 * r + 2048 * gidx + 512 * st4 + 128 * q4 + rows)
        in_maps.append({
            "x_sl": np.ascontiguousarray(x[2048 * c:2048 * (c + 1)]),
            "wq_tl": wq_tl, "wk_tl": wk_tl, "wv_tl": wv_tl,
            "wo_t": wo_t,
            "wffn_sl": np.ascontiguousarray(wf[:, 512 * gidx:512 * (gidx + 1)]),
            "bias_b": bias_rep,
            "cos_t": cos_sl, "sin_t": sin_sl,
            "masks_i": masks,
            "qsel_i": qsel, "aosel_i": aosel_ix,
        })

    if _NC_CACHE is None:
        _NC_CACHE = build_program()
    res = run_bass_kernel_spmd(_NC_CACHE, in_maps, core_ids=list(range(NCORE)))
    out = np.concatenate([res.results[c]["y_sl"] for c in range(NCORE)], axis=0)
    return out.reshape(B, S, H).astype(np.float32)


# version 09
# speedup vs baseline: 1.5455x; 1.0348x over previous
import math

import numpy as np
import ml_dtypes

import concourse.bass as bass
import concourse.mybir as mybir
from concourse.bass_utils import run_bass_kernel_spmd
from concourse.bass_core_rust import IndirectOffsetOnAxis

F32 = mybir.dt.float32
BF16 = mybir.dt.bfloat16
I32 = mybir.dt.int32
bf16 = ml_dtypes.bfloat16
Act = mybir.ActivationFunctionType
Alu = mybir.AluOpType
AX = mybir.AxisListType

NCORE = 8
H = 2048
NH = 16
HD = 128
S = 8192
B = 2
TOK = 2048
NHT = 16
SC_SCORE = float(np.float32(1.0 / math.sqrt(HD)))
EXP_BIAS = -12.0
EPS = 1e-5

QD_BOUND_A = [0.25, 0.75, 1.25, 1.75]
QD_BOUND_B = [2.5, 3.5]
QD_BOUND_C = 5.0
QD_EMIN, QD_EMAX = -8, 5

GROUPS = [[0, 1, 2, 3], [4, 5, 6, 7]]


def _rope_tables():
    theta = 10000.0
    inv_freq = 1.0 / theta ** (np.arange(0, HD, 2, dtype=np.float32) / HD)
    t = np.arange(S, dtype=np.float32)
    freqs = np.einsum("i,j->ij", t, inv_freq)
    emb = np.concatenate((freqs, freqs), axis=-1)
    return np.cos(emb), np.sin(emb)


def _masks():
    ql = np.arange(512)[None, :]
    kl = np.arange(128)[:, None]
    m = np.concatenate(
        [(ql >= 128 * r + kl).astype(np.float32) for r in range(4)], axis=1
    )
    return m.astype(bf16)


class MK:

    def __init__(self, nc, es):
        self.nc = nc
        self.thunks = {e: [] for e in ("sync", "gpsimd", "tensor", "vector",
                                       "scalar")}
        self.events = {e: [] for e in self.thunks}
        self.pending_incs = []
        self.sems = {}
        self.val = {}
        self.waited = {}
        self.es = es

    def sem(self, name):
        self.sems[name] = self.es.enter_context(self.nc.semaphore(name))
        self.val[name] = 0

    def emit(self, eng, fn):
        self.flush_incs(eng)
        self.thunks[eng].append(fn)

    def wait(self, eng, name, value):
        if value <= 0:
            return
        key = (eng, name)
        if self.waited.get(key, 0) >= value:
            return
        self.waited[key] = value
        self.events[eng].append(("w", name, value))
        h = self.sems[name]
        self.emit(eng, lambda e, h=h, v=value: e.wait_ge(h, v))

    def inc(self, name, n):
        self.val[name] += n
        self._cur_eng_hint = None
        self.pending_incs.append((name, n))
        h = self.sems[name]
        return lambda instr: instr.then_inc(h, n)

    def flush_incs(self, eng):
        for name, n in self.pending_incs:
            self.events[eng].append(("i", name, n))
        self.pending_incs = []

    def check_deadlock(self):
        import collections
        pos = {e: 0 for e in self.events}
        cnt = collections.defaultdict(int)
        progressed = True
        while progressed:
            progressed = False
            for e, evs in self.events.items():
                while pos[e] < len(evs):
                    kind, name, v = evs[pos[e]]
                    if kind == "i":
                        cnt[name] += v
                        pos[e] += 1
                        progressed = True
                    else:
                        if cnt[name] >= v:
                            pos[e] += 1
                            progressed = True
                        else:
                            break
        stuck = {e: (pos[e], len(evs)) for e, evs in self.events.items()
                 if pos[e] < len(evs)}
        if stuck:
            print("DEADLOCK:")
            for e, (p, n) in stuck.items():
                kind, name, v = self.events[e][p]
                print(f"  {e}@{p}/{n}: waiting {name} >= {v} (has {cnt[name]})")
            return False
        print("schedule OK (no deadlock)")
        return True

    def mark(self, name):
        self.marks = getattr(self, "marks", {})
        self.marks[name] = {e: len(t) for e, t in self.thunks.items()}


def build_program():
    import contextlib
    es = contextlib.ExitStack()
    nc = bass.Bass("TRN2", target_bir_lowering=False, debug=False)
    mk = MK(nc, es)

    def din(name, shape, dt=F32):
        return nc.dram_tensor(name, shape, dt, kind="ExternalInput").ap()

    x_sl = din("x_sl", [TOK, H])
    wq_tl = din("wq_tl", [NHT, 128, H])
    wk_tl = din("wk_tl", [NHT, 128, H])
    wv_tl = din("wv_tl", [NHT, 128, H])
    wo_t = din("wo_t", [H, H])
    wffn_sl = din("wffn_sl", [H, 512])
    bias_b = din("bias_b", [128, H])
    cos_t = din("cos_t", [128, TOK])
    sin_t = din("sin_t", [128, TOK])
    masks_i = din("masks_i", [128, 2048], BF16)
    qsel_i = din("qsel_i", [128, 48], I32)
    aosel_i = din("aosel_i", [128, 64], I32)

    y_sl = nc.dram_tensor("y_sl", [TOK, H], F32, kind="ExternalOutput").ap()

    def dint(name, shape, dt=BF16):
        return nc.dram_tensor(name, shape, dt, kind="Internal").ap()

    ln_scr = dint("ln_scr", [TOK, H])
    wqd_scr = dint("wqd_scr", [H, 512])
    wt_in = dint("wt_in", [512, H])
    wt_out = dint("wt_out", [H, H])
    qkv_in = dint("qkv_in", [3 * H, TOK])
    qkv_out = dint("qkv_out", [4 * 3 * H, TOK])
    vsel = dint("vsel", [2 * 512, H])
    ao_in = dint("ao_in", [S, 512])
    ao_out = dint("ao_out", [4 * S, 512])
    aosel = dint("aosel", [2 * TOK, 512])
    a_scr = dint("a_scr", [TOK, H])

    POOL_COLS = 99840
    pool = es.enter_context(nc.sbuf_tensor("pool", [128, POOL_COLS], BF16))
    bias_sb = es.enter_context(nc.sbuf_tensor("bias_sb", [128, H], F32))
    stats = es.enter_context(nc.sbuf_tensor("stats", [128, 64], F32))
    cst_t = es.enter_context(nc.sbuf_tensor("cst_t", [128, 8], F32))
    qsel_sb = es.enter_context(nc.sbuf_tensor("qsel_sb", [128, 48], I32))
    aosel_sb = es.enter_context(nc.sbuf_tensor("aosel_sb", [128, 64], I32))
    psum = es.enter_context(nc.psum_tensor("ps", [128, 4096], F32))

    def bslice(off, n):
        return pool[:, off:off + n]

    def fslice(off, n):
        return pool[:, 2 * off:2 * (off + n)].bitcast(F32)

    R_LNT = 0
    R_LNSTG = 32768
    R_XSTG = 18432
    R_COS = 22528
    R_SIN = 24576
    R_MASK = 53248
    R_WSTG = 55296
    R_QRAW = 59392
    R_QSWP = 63488
    R_T1 = 33792
    R_T2 = 16384
    R_QOUT = 71680
    R_VSTG = 75776
    R_WQSTG = 39936
    R_WQTMP = 40960
    R_WQOUT = 87040
    R_WTT = 89088
    R_WQS = 46592
    R_Q = [0, 8192]
    R_K = [16384, 24576]
    R_V = [32768, 41984]
    R_PT = [51200, 52224]
    R_AOS = [59520, 60032]
    R_DEN = 27648
    R_VG3 = 55424
    R_AOT = 0
    R_AT = 8192
    R_WOS = 16384
    R_WTS = 24576
    R_ASTG = 32768
    R_R1P = [18432, 26624]
    R_LN2 = 34816
    R_TQ = 36864
    R_TA = 38912
    R_TB = 40960
    R_TS = 43008
    R_OUT = [45056, 47104]
    R_QDS = 49152
    R_AOG4 = 98816

    SC = SC_SCORE
    CONSTS = [EXP_BIAS, SC]
    for i, v in enumerate(CONSTS):
        nc.const_aps.register(F32, v, cst_t[:, i:i + 1])

    for s in ("CS", "PR", "X1", "SQ1", "LS", "LA", "LW", "LT",
              "WQL", "WQD", "QDA", "WQW", "WTT", "WTI", "AG1",
              "WS", "PQ", "EV", "SW", "RP", "QKW", "VW", "AG2",
              "HLQ", "VG", "VSW", "VTR", "STm", "EX", "MKm", "PF", "NR",
              "AO", "AG3",
              "AOG", "AOW", "AOT", "X2", "OPT", "OP", "R1", "ST2", "SQ2",
              "LN2A", "QDV", "AW", "AT", "WOL", "WFL", "FFT", "FF", "FO",
              "YW", "VSS"):
        mk.sem(s)
    _vss_n = [0]

    def vsync(v, ins):
        _vss_n[0] += 1
        ins.then_inc(mk.sems["VSS"], 1)
        v.wait_ge(mk.sems["VSS"], _vss_n[0])

    E, W, I = mk.emit, mk.wait, mk.inc

    def emit_qd(ncols, src_f32, dst_bf16, r_small, r_q, r_a, r_b, r_s,
                v_pre_waits, done_inc):
        nb = ncols // 32
        for (sname, v) in v_pre_waits:
            W("vector", sname, v)

        def _(v, src=src_f32, nb=nb):
            amax = fslice(r_small, nb)
            scale = fslice(r_small + nb, nb)
            src3 = src.rearrange("p (a b) -> p a b", b=32)
            vsync(v, v.tensor_reduce(amax.unsqueeze(2), src3, AX.X, Alu.max,
                                     apply_absolute_value=True))
            v.memset(scale, 2.0 ** (QD_EMIN - 3))
            es_ = list(range(QD_EMIN, QD_EMAX + 1))
            gbuf = [fslice(r_small + 3 * nb, nb), fslice(r_q, nb)]
            for i, e in enumerate(es_):
                v.tensor_scalar(gbuf[i % 2], amax, float(2.0 ** e),
                                float(2.0 ** (e - 3)), Alu.is_ge, Alu.mult)
                if i >= 1:
                    v.tensor_tensor(scale, scale, gbuf[(i - 1) % 2], Alu.add)
            vsync(v, v.tensor_tensor(scale, scale,
                                     gbuf[(len(es_) - 1) % 2], Alu.add))
            rscale = fslice(r_small + 2 * nb, nb)
            vsync(v, v.reciprocal(rscale, scale))
            q = fslice(r_q, ncols)
            rs_b = rscale.unsqueeze(2).broadcast_to([128, nb, 32])
            v.tensor_tensor(q.rearrange("p (a b) -> p a b", b=32), src3, rs_b,
                            Alu.mult)
        E("vector", _)
        qfn = I("QDA", 1)

        def _(v, fn=qfn):
            fn(v.engine_nop())
        E("vector", _)
        q_ready = mk.val["QDA"]
        W("scalar", "QDA", q_ready)
        afn = I("QDA", 1)

        def _(sc, fn=afn):
            q = fslice(r_q, ncols)
            sc.activation(fslice(r_a, ncols), q, Act.Abs)
            fn(sc.activation(fslice(r_s, ncols), q, Act.Sign))
        E("scalar", _)
        as_ready = mk.val["QDA"]
        W("vector", "QDA", as_ready)
        dfn = I(done_inc, 1)

        def _(v, nb=nb, fn=dfn):
            a = fslice(r_a, ncols)
            A = fslice(r_q, ncols)
            bb = fslice(r_b, ncols)
            sg = fslice(r_s, ncols)
            scale = fslice(r_small + nb, nb)
            sc_b = scale.unsqueeze(2).broadcast_to([128, nb, 32])
            v.tensor_single_scalar(A, a, QD_BOUND_A[0], Alu.is_ge)
            for bnd in QD_BOUND_A[1:]:
                v.scalar_tensor_tensor(A, a, bnd, A, Alu.is_ge, Alu.add)
            v.tensor_single_scalar(bb, a, QD_BOUND_B[0], Alu.is_ge)
            v.scalar_tensor_tensor(bb, a, QD_BOUND_B[1], bb, Alu.is_ge, Alu.add)
            v.scalar_tensor_tensor(bb, A, 0.5, bb, Alu.mult, Alu.add)
            v.tensor_scalar(a, a, QD_BOUND_C, 2.0, Alu.is_ge, Alu.mult)
            v.tensor_tensor(bb, bb, a, Alu.add)
            v.tensor_tensor(bb, bb, sg, Alu.mult)
            fn(v.tensor_tensor(dst_bf16.rearrange("p (a b) -> p a b", b=32),
                               bb.rearrange("p (a b) -> p a b", b=32), sc_b,
                               Alu.mult))
        E("vector", _)

    def _(v):
        for i, c in enumerate(CONSTS):
            v.memset(cst_t[:, i:i + 1], c)
    E("vector", _)
    csfn = I("CS", 1)

    def _(v, fn=csfn):
        fn(v.engine_nop())
    E("vector", _)

    prfn = [I("PR", 16) for _ in range(6)]

    def _(s, fns=prfn):
        fns[0](s.dma_start(out=fslice(R_COS, H), in_=cos_t))
        fns[1](s.dma_start(out=fslice(R_SIN, H), in_=sin_t))
        fns[2](s.dma_start(out=bslice(R_MASK, 2048), in_=masks_i))
        fns[3](s.dma_start(out=bias_sb[:], in_=bias_b))
        fns[4](s.dma_start(out=qsel_sb[:], in_=qsel_i))
        fns[5](s.dma_start(out=aosel_sb[:], in_=aosel_i))
    E("sync", _)
    PR_ALL = mk.val["PR"]

    mk.mark("prep")
    for i in range(16):
        bp = i % 2
        W("scalar", "WQD", i - 1)
        fn = I("WQL", 16)

        def _(s, i=i, bp=bp, fn=fn):
            fn(s.dma_start(out=fslice(R_WQSTG + bp * 512, 512),
                           in_=wffn_sl[128 * i:128 * (i + 1), :]))
        E("scalar", _)
        emit_qd(512,
                fslice(R_WQSTG + bp * 512, 512),
                bslice(R_WQOUT + bp * 512, 512),
                R_WQS,
                R_WQTMP, R_WQTMP + 512, R_WQTMP + 1024, R_WQTMP + 1536,
                v_pre_waits=[("CS", 1), ("WQL", 16 * (i + 1)),
                             ("WQW", 16 * (i - 1))],
                done_inc="WQD")
        if i >= 1:
            W("scalar", "WQD", i)
            fn = I("WQW", 16)

            def _(s, i=i, fn=fn):
                fn(s.dma_start(out=wqd_scr[128 * (i - 1):128 * i, :],
                               in_=bslice(R_WQOUT + ((i - 1) % 2) * 512, 512)))
            E("scalar", _)
    W("scalar", "WQD", 16)
    fn = I("WQW", 16)

    def _(s, fn=fn):
        fn(s.dma_start(out=wqd_scr[128 * 15:128 * 16, :],
                       in_=bslice(R_WQOUT + 512, 512)))
    E("scalar", _)
    W("sync", "WQW", 256)
    for k in range(4):
        W("sync", "WTI", 16 * (k - 1))
        fn = I("WTT", 16)

        def _(s, k=k, fn=fn):
            fn(s.dma_start_transpose(out=bslice(R_WTT + (k % 2) * 2048, 2048),
                                     in_=wqd_scr[:, 128 * k:128 * (k + 1)]))
        E("sync", _)
        W("sync", "WTT", 16 * (k + 1))
        fn = I("WTI", 16)

        def _(s, k=k, fn=fn):
            fn(s.dma_start(out=wt_in[128 * k:128 * (k + 1), :],
                           in_=bslice(R_WTT + (k % 2) * 2048, 2048)))
        E("sync", _)

    mk.mark("wqd")
    for t in range(16):
        bp = t % 2
        W("sync", "LA", t - 1)
        fn = I("X1", 16)

        def _(s, t=t, bp=bp, fn=fn):
            fn(s.dma_start(out=fslice(R_XSTG + bp * 2048, 2048),
                           in_=x_sl[128 * t:128 * (t + 1), :]))
        E("sync", _)
        if t >= 1:
            W("sync", "LA", t)
            fn = I("LW", 16)

            def _(s, t=t, fn=fn):
                fn(s.dma_start(out=ln_scr[128 * (t - 1):128 * t, :],
                               in_=bslice(R_LNSTG + ((t - 1) % 2) * 2048, 2048)))
            E("sync", _)
    W("sync", "LA", 16)
    fn = I("LW", 16)

    def _(s, fn=fn):
        fn(s.dma_start(out=ln_scr[128 * 15:128 * 16, :],
                       in_=bslice(R_LNSTG + 2048, 2048)))
    E("sync", _)
    for t in range(16):
        bp = t % 2
        W("scalar", "X1", 16 * (t + 1))
        W("scalar", "LS", 2 * t - 3)
        fn = I("SQ1", 1)

        def _(sc, bp=bp, fn=fn):
            fn(sc.activation(fslice(R_T1, 2048), fslice(R_XSTG + bp * 2048, 2048),
                             Act.Square,
                             accum_out=stats[:, 8 * bp + 1:8 * bp + 2]))
        E("scalar", _)
        sq1v = mk.val["SQ1"]
        W("vector", "X1", 16 * (t + 1))

        def _(v, bp=bp):
            vsync(v, v.reduce_sum(stats[:, 8 * bp:8 * bp + 1],
                                  fslice(R_XSTG + bp * 2048, 2048), AX.X))
        E("vector", _)
        W("vector", "SQ1", sq1v)
        W("vector", "SQ2", t - 1)
        fn = I("LS", 1)

        def _(v, bp=bp, fn=fn):
            st = stats[:, 8 * bp:8 * (bp + 1)]
            vsync(v, v.tensor_single_scalar(st[:, 0:1], st[:, 0:1], 1.0 / H,
                                            Alu.mult))
            vsync(v, v.tensor_single_scalar(st[:, 1:2], st[:, 1:2], 1.0 / H,
                                            Alu.mult))
            vsync(v, v.tensor_tensor(st[:, 2:3], st[:, 0:1], st[:, 0:1],
                                     Alu.mult))
            vsync(v, v.tensor_tensor(st[:, 2:3], st[:, 1:2], st[:, 2:3],
                                     Alu.subtract))
            fn(v.tensor_single_scalar(st[:, 3:4], st[:, 2:3], EPS, Alu.add))
        E("vector", _)
        ls1 = mk.val["LS"]
        W("scalar", "LS", ls1)
        fn = I("SQ2", 1)

        def _(sc, bp=bp, fn=fn):
            st = stats[:, 8 * bp:8 * (bp + 1)]
            fn(sc.activation(st[:, 4:5], st[:, 3:4], Act.Sqrt))
        E("scalar", _)
        sq2v = mk.val["SQ2"]
        W("vector", "SQ2", sq2v)
        W("vector", "LA", t - 1)
        fn = I("LS", 1)

        def _(v, bp=bp, fn=fn):
            st = stats[:, 8 * bp:8 * (bp + 1)]
            vsync(v, v.reciprocal(st[:, 5:6], st[:, 4:5]))
            vsync(v, v.tensor_tensor(st[:, 5:6], st[:, 3:4], st[:, 5:6],
                                     Alu.mult))
            vsync(v, v.tensor_tensor(st[:, 4:5], st[:, 4:5], st[:, 5:6],
                                     Alu.add))
            vsync(v, v.tensor_single_scalar(st[:, 4:5], st[:, 4:5], 0.5,
                                            Alu.mult))
            vsync(v, v.reciprocal(st[:, 5:6], st[:, 4:5]))
            vsync(v, v.tensor_tensor(st[:, 6:7], st[:, 0:1], st[:, 5:6],
                                     Alu.mult))
            fn(v.tensor_single_scalar(st[:, 6:7], st[:, 6:7], -1.0, Alu.mult))
        E("vector", _)
        ls2 = mk.val["LS"]
        W("scalar", "LS", ls2)
        W("scalar", "LW", 16 * (t - 1))
        fn = I("LA", 1)

        def _(sc, bp=bp, fn=fn):
            st = stats[:, 8 * bp:8 * (bp + 1)]
            fn(sc.activation(bslice(R_LNSTG + bp * 2048, 2048),
                             fslice(R_XSTG + bp * 2048, 2048), Act.Identity,
                             bias=st[:, 6:7], scale=st[:, 5:6]))
        E("scalar", _)
    W("sync", "LW", 256)
    for h in range(NHT):
        fn = I("LT", 16)

        def _(s, h=h, fn=fn):
            fn(s.dma_start_transpose(out=bslice(R_LNT + h * 2048, 2048),
                                     in_=ln_scr[:, 128 * h:128 * (h + 1)]))
        E("sync", _)

    mk.mark("ln")
    WTILED = {0: wq_tl, 1: wk_tl, 2: wv_tl}
    qk_idx = 0
    v_idx = 0
    pending_qk_write = None
    for d in range(NHT):
        for p in range(3):
            g = 3 * d + p
            gp = g % 2
            W("gpsimd", "PQ", g - 1)
            fn = I("WS", 16)

            def _(gg, d=d, p=p, gp=gp, fn=fn):
                fn(gg.dma_start(out=bslice(R_WSTG + gp * 2048, 2048),
                                in_=WTILED[p][d]))
            E("gpsimd", _)
            W("tensor", "LT", 256)
            W("tensor", "WS", 16 * (g + 1))
            W("tensor", "EV", g - 1)
            fn = I("PQ", 1)

            def _(te, gp=gp, fn=fn):
                reg = psum[:, 2048 * gp:2048 * (gp + 1)]
                wst = bslice(R_WSTG + gp * 2048, 2048)
                ins = None
                for ht in range(NHT):
                    lhsT = wst[:, 128 * ht:128 * (ht + 1)]
                    lnt = bslice(R_LNT + ht * 2048, 2048)
                    for ch in range(4):
                        ins = te.matmul(reg[:, 512 * ch:512 * (ch + 1)], lhsT,
                                        lnt[:, 512 * ch:512 * (ch + 1)],
                                        start=(ht == 0), stop=(ht == NHT - 1))
                fn(ins)
            E("tensor", _)
            W("scalar", "PQ", g + 1)
            if p < 2:
                qi = qk_idx
                W("scalar", "RP", qi - 1)
                fn = I("EV", 1)

                def _(sc, gp=gp, qi=qi, fn=fn):
                    fn(sc.activation(bslice(R_QRAW + (qi % 2) * 2048, 2048),
                                     psum[:, 2048 * gp:2048 * (gp + 1)],
                                     Act.Copy))
                E("scalar", _)
                W("sync", "EV", mk.val["EV"])
                fn1, fn2 = I("SW", 16), I("SW", 16)

                def _(s, qi=qi, fn1=fn1, fn2=fn2):
                    qr = bslice(R_QRAW + (qi % 2) * 2048, 2048)
                    qs = bslice(R_QSWP + (qi % 2) * 2048, 2048)
                    fn1(s.dma_start(out=qs[0:64, :], in_=qr[64:128, :]))
                    fn2(s.dma_start(out=qs[64:128, :], in_=qr[0:64, :]))
                E("sync", _)
                if pending_qk_write is not None:
                    pqi, prow0, prp = pending_qk_write
                    W("sync", "RP", prp)
                    fn = I("QKW", 16)

                    def _(s, pqi=pqi, prow0=prow0, fn=fn):
                        fn(s.dma_start(out=qkv_in[prow0:prow0 + 128, :],
                                       in_=bslice(R_QOUT + (pqi % 2) * 2048,
                                                  2048)))
                    E("sync", _)
                W("vector", "PR", PR_ALL)
                W("vector", "SW", 32 * (qi + 1))
                W("vector", "QKW", 16 * (qi - 1))
                fn = I("RP", 1)

                def _(v, qi=qi, fn=fn):
                    qr = bslice(R_QRAW + (qi % 2) * 2048, 2048)
                    qs = bslice(R_QSWP + (qi % 2) * 2048, 2048)
                    t1 = fslice(R_T1, 2048)
                    t2 = fslice(R_T2, 2048)
                    v.tensor_tensor(t1, qr, fslice(R_COS, 2048), Alu.mult)
                    v.tensor_tensor(t2, qs, fslice(R_SIN, 2048), Alu.mult)
                    fn(v.tensor_tensor(bslice(R_QOUT + (qi % 2) * 2048, 2048),
                                       t1, t2, Alu.add))
                E("vector", _)
                row0 = 2048 * p + 128 * d
                pending_qk_write = (qi, row0, mk.val["RP"])
                qk_idx += 1
            else:
                vi = v_idx
                W("scalar", "VW", 16 * (vi - 1))
                fn = I("EV", 1)

                def _(sc, gp=gp, vi=vi, fn=fn):
                    fn(sc.activation(bslice(R_VSTG + (vi % 2) * 2048, 2048),
                                     psum[:, 2048 * gp:2048 * (gp + 1)],
                                     Act.Copy))
                E("scalar", _)
                W("sync", "EV", mk.val["EV"])
                row0 = 4096 + 128 * d
                fn = I("VW", 16)

                def _(s, vi=vi, row0=row0, fn=fn):
                    fn(s.dma_start(out=qkv_in[row0:row0 + 128, :],
                                   in_=bslice(R_VSTG + (vi % 2) * 2048, 2048)))
                E("sync", _)
                v_idx += 1
    pqi, prow0, prp = pending_qk_write
    W("sync", "RP", prp)
    fn = I("QKW", 16)

    def _(s, pqi=pqi, prow0=prow0, fn=fn):
        fn(s.dma_start(out=qkv_in[prow0:prow0 + 128, :],
                       in_=bslice(R_QOUT + (pqi % 2) * 2048, 2048)))
    E("sync", _)
    assert mk.val["QKW"] == 16 * 32 and mk.val["VW"] == 16 * 16

    W("gpsimd", "WTI", 64)
    fn = I("AG1", 1)

    def _(gg, fn=fn):
        fn(gg.collective_compute("AllGather", Alu.bypass, replica_groups=GROUPS,
                                 ins=[wt_in], outs=[wt_out]))
    E("gpsimd", _)
    W("gpsimd", "QKW", 512)
    W("gpsimd", "VW", 256)
    fn = I("AG2", 1)

    def _(gg, fn=fn):
        fn(gg.collective_compute("AllGather", Alu.bypass, replica_groups=GROUPS,
                                 ins=[qkv_in], outs=[qkv_out]))
    E("gpsimd", _)

    mk.mark("proj")
    W("vector", "AG2", 1)
    onesdone = I("CS", 1)

    def _(v, fn=onesdone):
        for hp in range(2):
            for kt in range(64):
                v.memset(bslice(R_V[hp] + 144 * kt + 128, 1), 1.0)
        fn(v.engine_nop())
    E("vector", _)
    ONES_DONE = mk.val["CS"]

    KT_PER_HEAD = sum(4 * (qb + 1) for qb in range(16))
    st_cum = ex_cum = mkm_cum = pf_cum = nr_cum = ao_cum = 0
    vg_idx = 0

    for hi in range(4):
        hp = hi % 2
        W("gpsimd", "AG2", 1)
        W("gpsimd", "PR", PR_ALL)
        if hi >= 2:
            W("gpsimd", "STm", KT_PER_HEAD * (hi - 1))
            W("gpsimd", "PF", KT_PER_HEAD // 2 * (hi - 1))
            W("gpsimd", "CS", ONES_DONE)
        for r in range(4):
            fn1, fn2 = I("HLQ", 16), I("HLQ", 16)

            def _(gg, hi=hi, hp=hp, r=r, fn1=fn1, fn2=fn2):
                fn1(gg.indirect_dma_start(
                    out=bslice(R_Q[hp] + 2048 * r, 2048), out_offset=None,
                    in_=qkv_out,
                    in_offset=IndirectOffsetOnAxis(
                        qsel_sb[:, 12 * hi + r:12 * hi + r + 1], 0)))
                fn2(gg.indirect_dma_start(
                    out=bslice(R_K[hp] + 2048 * r, 2048), out_offset=None,
                    in_=qkv_out,
                    in_offset=IndirectOffsetOnAxis(
                        qsel_sb[:, 12 * hi + 4 + r:12 * hi + 4 + r + 1], 0)))
            E("gpsimd", _)
        for r in range(4):
            vg = vg_idx
            W("gpsimd", "VSW", 16 * (vg - 1))
            fn = I("VG", 16)

            def _(gg, hi=hi, r=r, vg=vg, fn=fn):
                fn(gg.indirect_dma_start(
                    out=bslice(R_VG3 + (vg % 2) * 2048, 2048), out_offset=None,
                    in_=qkv_out,
                    in_offset=IndirectOffsetOnAxis(
                        qsel_sb[:, 12 * hi + 8 + r:12 * hi + 8 + r + 1], 0)))
            E("gpsimd", _)
            W("gpsimd", "VG", 16 * (vg + 1))
            if hi >= 2:
                W("gpsimd", "VTR", 1024 * (hi - 1))
            fn = I("VSW", 16)

            def _(gg, hp=hp, r=r, vg=vg, fn=fn):
                fn(gg.dma_start(out=vsel[512 * hp + 128 * r:
                                         512 * hp + 128 * (r + 1), :],
                                in_=bslice(R_VG3 + (vg % 2) * 2048, 2048)))
            E("gpsimd", _)
            vg_idx += 1
        def emit_vtr(H, tix):
            hpH = H % 2
            r = tix // 16
            W("sync", "VSW", 64 * H + 16 * (r + 1))
            W("sync", "CS", ONES_DONE)
            if H >= 2:
                W("sync", "PF", KT_PER_HEAD // 2 * (H - 1))
            fn = I("VTR", 16)

            def _(s, hpH=hpH, r=r, j=tix % 16, fn=fn):
                kt = 16 * r + j
                fn(s.dma_start_transpose(
                    out=bslice(R_V[hpH] + 144 * kt, 128),
                    in_=vsel[512 * hpH + 128 * r:512 * hpH + 128 * (r + 1),
                             128 * j:128 * (j + 1)]))
            E("sync", _)

        if hi < 2:
            for tix in range(64):
                emit_vtr(hi, tix)

        if hi == 0:
            UNITS = []
            for h2 in range(4):
                for qb2 in range(16):
                    for ktp2 in range(2 * (qb2 + 1)):
                        UNITS.append((h2, qb2, ktp2))

            def emit_st(g):
                h2, qb2, ktp2 = UNITS[g]
                hp2 = h2 % 2
                kt0 = 2 * ktp2
                pp = g % 2
                W("tensor", "HLQ", 128 * (h2 + 1))
                W("tensor", "EX", g - 1)
                fns = [I("STm", 1), I("STm", 1)]

                def _(te, hp2=hp2, qb2=qb2, kt0=kt0, pp=pp, fns=fns):
                    for i in range(2):
                        bank = psum[:, 1024 * pp + 512 * i:
                                    1024 * pp + 512 * (i + 1)]
                        fns[i](te.matmul(bank,
                                         bslice(R_K[hp2] + 128 * (kt0 + i), 128),
                                         bslice(R_Q[hp2] + 512 * qb2, 512),
                                         start=True, stop=True))
                E("tensor", _)

            emit_st(0)
            gctr = 0

        for qb in range(16):
            qbg = 16 * hi + qb
            nkt = 4 * (qb + 1)
            for ktp in range(nkt // 2):
                g = gctr
                pp = g % 2
                kt0 = 2 * ktp
                if g + 1 < len(UNITS):
                    emit_st(g + 1)
                W("scalar", "STm", 2 * (g + 1))
                W("scalar", "PF", g - 1)
                fn = I("EX", 1)

                def _(sc, pp=pp, fn=fn):
                    fn(sc.activation(bslice(R_PT[pp], 1024),
                                     psum[:, 1024 * pp:1024 * (pp + 1)],
                                     Act.Exp, bias=EXP_BIAS, scale=SC))
                E("scalar", _)
                any_diag = False
                for i in range(2):
                    kt = kt0 + i
                    if kt >= 4 * qb:
                        rel = kt - 4 * qb
                        W("vector", "EX", g + 1)
                        fn = I("MKm", 1)

                        def _(v, pp=pp, i=i, rel=rel, fn=fn):
                            pt = bslice(R_PT[pp] + 512 * i, 512)
                            fn(v.tensor_tensor(pt, pt,
                                               bslice(R_MASK + 512 * rel, 512),
                                               Alu.mult))
                        E("vector", _)
                        mkm_cum += 1
                        any_diag = True
                if any_diag:
                    W("tensor", "MKm", mkm_cum)
                else:
                    W("tensor", "EX", g + 1)
                if ktp == 0:
                    W("tensor", "NR", qbg - 1)
                if qb == 0 and ktp == 0:
                    W("tensor", "VTR", 1024 * (hi + 1))
                fn = I("PF", 1)

                def _(te, hp=hp, kt0=kt0, pp=pp, nkt=nkt, qbg=qbg, fn=fn):
                    reg0 = 2048 + 1024 * (qbg % 2)
                    ins = None
                    for i in range(2):
                        kt = kt0 + i
                        pt = bslice(R_PT[pp] + 512 * i, 512)
                        vv = bslice(R_V[hp] + 144 * kt, 129)
                        for j in range(4):
                            col = reg0 + 512 * (j // 2) + 256 * (j % 2)
                            ins = te.matmul(
                                psum[:, col:col + 129],
                                pt[:, 128 * j:128 * (j + 1)], vv,
                                start=(kt == 0 and j % 2 == 0),
                                stop=(kt == nkt - 1 and j % 2 == 1))
                    fn(ins)
                E("tensor", _)
                gctr += 1
            W("vector", "PF", gctr)
            W("vector", "AO", 64 * (qbg - 1))
            fn = I("NR", 1)

            def _(v, qbg=qbg, fn=fn):
                reg0 = 2048 + 1024 * (qbg % 2)
                den = fslice(R_DEN, 32)
                aos = bslice(R_AOS[qbg % 2], 512)
                ins = None
                for j in range(4):
                    col = reg0 + 512 * (j // 2) + 256 * (j % 2)
                    ins = v.reciprocal(den[:, j:j + 1],
                                       psum[:, col + 128:col + 129])
                vsync(v, ins)
                for j in range(4):
                    col = reg0 + 512 * (j // 2) + 256 * (j % 2)
                    ins = v.tensor_single_scalar(
                        aos[:, 128 * j:128 * (j + 1)],
                        psum[:, col:col + 128],
                        den[:, j:j + 1], Alu.mult)
                fn(ins)
            E("vector", _)
            nr_cum += 1
            W("sync", "NR", nr_cum)
            fns = [I("AO", 16) for _ in range(4)]

            def _(s, hi=hi, qb=qb, qbg=qbg, fns=fns):
                aos = bslice(R_AOS[qbg % 2], 512)
                for j in range(4):
                    fns[j](s.dma_start(
                        out=ao_in[512 * qb + 128 * j:512 * qb + 128 * (j + 1),
                                  128 * hi:128 * (hi + 1)],
                        in_=aos[:, 128 * j:128 * (j + 1)]))
            E("sync", _)
            ao_cum += 64
            if 2 <= hi + 1 <= 3:
                for tix in range(4 * qb, 4 * qb + 4):
                    emit_vtr(hi + 1, tix)

    W("gpsimd", "AO", ao_cum)
    fn = I("AG3", 1)

    def _(gg, fn=fn):
        fn(gg.collective_compute("AllGather", Alu.bypass, replica_groups=GROUPS,
                                 ins=[ao_in], outs=[ao_out]))
    E("gpsimd", _)

    mk.mark("attn")
    import os as _os2
    P4N = int(_os2.environ.get("MK_P4_N", "4"))
    pe_groups = []
    for st4 in range(P4N):
        pe_groups.append(("op", st4, 0))
        pe_groups.append(("op", st4, 1))
        if st4 >= 1:
            pe_groups.append(("ffn", st4 - 1, 0))
            pe_groups.append(("ffn", st4 - 1, 1))
    pe_groups.append(("ffn", P4N - 1, 0))
    pe_groups.append(("ffn", P4N - 1, 1))

    W("gpsimd", "AG3", 1)
    aog_idx = 0
    wo_steps = 0
    wt_steps = 0
    for kind, st4, pr in pe_groups:
        if kind == "op" and pr == 0:
            if st4 >= 2:
                W("gpsimd", "AOT", 256 * (st4 - 1))
            for r in range(4):
                for q4 in range(4):
                    ag = aog_idx
                    W("gpsimd", "AOW", 16 * (ag - 1))
                    fn = I("AOG", 16)

                    def _(gg, st4=st4, r=r, q4=q4, ag=ag, fn=fn):
                        fn(gg.indirect_dma_start(
                            out=bslice(R_AOG4 + (ag % 2) * 512, 512),
                            out_offset=None,
                            in_=ao_out,
                            in_offset=IndirectOffsetOnAxis(
                                aosel_sb[:, 16 * st4 + 4 * r + q4:
                                         16 * st4 + 4 * r + q4 + 1], 0)))
                    E("gpsimd", _)
                    W("gpsimd", "AOG", 16 * (ag + 1))
                    fn = I("AOW", 16)

                    def _(gg, st4=st4, r=r, q4=q4, ag=ag, fn=fn):
                        row0 = 2048 * (st4 % 2) + 512 * r + 128 * q4
                        fn(gg.dma_start(
                            out=aosel[row0:row0 + 128, :],
                            in_=bslice(R_AOG4 + (ag % 2) * 512, 512)))
                    E("gpsimd", _)
                    aog_idx += 1
            if st4 == 0:
                mk.mark("p4ga")
        for k in range(16):
            if kind == "op":
                W("gpsimd", "OPT", wo_steps - 3)
                slot = wo_steps % 4
                fn = I("WOL", 16)

                def _(gg, k=k, slot=slot, fn=fn):
                    fn(gg.dma_start(out=bslice(R_WOS + 2048 * slot, 2048),
                                    in_=wo_t[128 * k:128 * (k + 1), :]))
                E("gpsimd", _)
                wo_steps += 1
                if wo_steps == 16:
                    mk.mark("p4gb")
            else:
                W("gpsimd", "FFT", wt_steps - 3)
                slot = wt_steps % 4
                fn = I("WFL", 16)

                def _(gg, k=k, slot=slot, fn=fn):
                    fn(gg.dma_start(out=bslice(R_WTS + 2048 * slot, 2048),
                                    in_=wt_out[128 * k:128 * (k + 1), :]))
                E("gpsimd", _)
                wt_steps += 1
    mk.mark("p4gpsimd")

    W("sync", "AG3", 1)
    for st4 in range(P4N):
        for tt in range(4):
            W("sync", "FO", 4 * (st4 - 2) + tt + 1)
            fn = I("X2", 16)

            def _(s, st4=st4, tt=tt, fn=fn):
                fn(s.dma_start(out=fslice(R_R1P[st4 % 2] + tt * 2048, 2048),
                               in_=x_sl[512 * st4 + 128 * tt:
                                        512 * st4 + 128 * (tt + 1), :]))
            E("sync", _)
        W("sync", "OPT", 32 * st4)
        W("sync", "AOW", 16 * 16 * (st4 + 1))
        for r in range(4):
            for ht in range(4):
                fn = I("AOT", 16)

                def _(s, st4=st4, r=r, ht=ht, fn=fn):
                    slot = 4 * r + ht
                    base = 2048 * (st4 % 2) + 512 * r
                    fn(s.dma_start_transpose(
                        out=bslice(R_AOT + 512 * slot, 512),
                        in_=aosel[base:base + 512, 128 * ht:128 * (ht + 1)]))
                E("sync", _)
        for tt in range(4):
            g4 = 4 * st4 + tt
            W("sync", "QDV", g4 + 1)
            fn = I("AW", 16)

            def _(s, st4=st4, tt=tt, fn=fn):
                fn(s.dma_start(
                    out=a_scr[512 * st4 + 128 * tt:512 * st4 + 128 * (tt + 1), :],
                    in_=bslice(R_ASTG + (tt % 2) * 2048, 2048)))
            E("sync", _)
        W("sync", "AW", 64 * (st4 + 1))
        W("sync", "FFT", 32 * st4)
        for k in range(16):
            fn = I("AT", 16)

            def _(s, st4=st4, k=k, fn=fn):
                fn(s.dma_start_transpose(
                    out=bslice(R_AT + 512 * k, 512),
                    in_=a_scr[512 * st4:512 * (st4 + 1),
                              128 * k:128 * (k + 1)]))
            E("sync", _)
        if st4 >= 1:
            for tt in range(4):
                W("sync", "FO", 4 * (st4 - 1) + tt + 1)
                fn = I("YW", 16)

                def _(s, st4=st4, tt=tt, fn=fn):
                    fn(s.dma_start(
                        out=y_sl[512 * (st4 - 1) + 128 * tt:
                                 512 * (st4 - 1) + 128 * (tt + 1), :],
                        in_=fslice(R_OUT[tt % 2], 2048)))
                E("sync", _)
    for tt in range(4):
        W("sync", "FO", 4 * (P4N - 1) + tt + 1)
        fn = I("YW", 16)

        def _(s, tt=tt, fn=fn):
            fn(s.dma_start(out=y_sl[512 * (P4N - 1) + 128 * tt:
                                    512 * (P4N - 1) + 128 * (tt + 1), :],
                           in_=fslice(R_OUT[tt % 2], 2048)))
        E("sync", _)

    mk.mark("p4sync")
    group_done = []
    wo_pe = 0
    wt_pe = 0
    for gi, (kind, st4, pr) in enumerate(pe_groups):
        if gi >= 1:
            sname, sval = group_done[gi - 1]
            W("tensor", sname, sval)
        if kind == "op":
            W("tensor", "AOT", 256 * (st4 + 1))
            for k in range(16):
                W("tensor", "WOL", 16 * (wo_pe + 1))
                fn = I("OPT", 1)

                def _(te, k=k, pr=pr, slot=wo_pe % 4, fn=fn):
                    wst = bslice(R_WOS + 2048 * slot, 2048)
                    ins = None
                    for ti in range(2):
                        tt = 2 * pr + ti
                        reg = psum[:, 2048 * ti:2048 * (ti + 1)]
                        for ch in range(4):
                            ins = te.matmul(
                                reg[:, 512 * ch:512 * (ch + 1)],
                                bslice(R_AOT + 512 * k + 128 * tt, 128),
                                wst[:, 512 * ch:512 * (ch + 1)],
                                start=(k == 0), stop=(k == 15))
                    fn(ins)
                E("tensor", _)
                wo_pe += 1
            group_done.append(("R1", 4 * st4 + 2 * pr + 2))
        else:
            W("tensor", "AT", 256 * (st4 + 1))
            for k in range(16):
                W("tensor", "WFL", 16 * (wt_pe + 1))
                fn = I("FFT", 1)

                def _(te, k=k, pr=pr, slot=wt_pe % 4, fn=fn):
                    wst = bslice(R_WTS + 2048 * slot, 2048)
                    ins = None
                    for ti in range(2):
                        tt = 2 * pr + ti
                        reg = psum[:, 2048 * ti:2048 * (ti + 1)]
                        for ch in range(4):
                            ins = te.matmul(
                                reg[:, 512 * ch:512 * (ch + 1)],
                                bslice(R_AT + 512 * k + 128 * tt, 128),
                                wst[:, 512 * ch:512 * (ch + 1)],
                                start=(k == 0), stop=(k == 15))
                    fn(ins)
                E("tensor", _)
                wt_pe += 1
            group_done.append(("FO", 4 * st4 + 2 * pr + 2))

    op_done_at = {}
    ff_done_at = {}
    opv = ffv = 0
    for kind, st4, pr in pe_groups:
        if kind == "op":
            opv += 16
            op_done_at[(st4, 2 * pr)] = opv
            op_done_at[(st4, 2 * pr + 1)] = opv
        else:
            ffv += 16
            ff_done_at[(st4, 2 * pr)] = ffv
            ff_done_at[(st4, 2 * pr + 1)] = ffv

    mk.mark("p4pe")
    for st4 in range(P4N):
        for tt in range(4):
            g4 = 4 * st4 + tt
            bp = tt % 2
            ti = tt % 2
            W("vector", "OPT", op_done_at[(st4, tt)])
            W("vector", "X2", 16 * (g4 + 1))
            fn = I("R1", 1)

            def _(v, ti=ti, tt=tt, st4=st4, fn=fn):
                reg = psum[:, 2048 * ti:2048 * (ti + 1)]
                r1 = fslice(R_R1P[st4 % 2] + tt * 2048, 2048)
                fn(v.tensor_tensor(r1, reg, r1, Alu.add))
            E("vector", _)
            r1v = mk.val["R1"]

            def _(v, tt=tt, bp=bp, st4=st4):
                vsync(v, v.reduce_sum(stats[:, 32 + 8 * bp:32 + 8 * bp + 1],
                                      fslice(R_R1P[st4 % 2] + tt * 2048, 2048),
                                      AX.X))
            E("vector", _)
            W("scalar", "R1", r1v)
            W("scalar", "ST2", 2 * g4 - 3)
            W("scalar", "QDA", 2 * g4 + 31)
            fn = I("SQ1", 1)

            def _(sc, tt=tt, bp=bp, st4=st4, fn=fn):
                fn(sc.activation(fslice(R_LN2, 2048),
                                 fslice(R_R1P[st4 % 2] + tt * 2048, 2048),
                                 Act.Square,
                                 accum_out=stats[:, 32 + 8 * bp + 1:
                                                 32 + 8 * bp + 2]))
            E("scalar", _)
            sq1v = mk.val["SQ1"]
            W("vector", "SQ1", sq1v)
            W("vector", "SQ2", 16 + g4 - 1)
            fn = I("ST2", 1)

            def _(v, bp=bp, fn=fn):
                st = stats[:, 32 + 8 * bp:32 + 8 * (bp + 1)]
                vsync(v, v.tensor_single_scalar(st[:, 0:1], st[:, 0:1],
                                                1.0 / H, Alu.mult))
                vsync(v, v.tensor_single_scalar(st[:, 1:2], st[:, 1:2],
                                                1.0 / H, Alu.mult))
                vsync(v, v.tensor_tensor(st[:, 2:3], st[:, 0:1], st[:, 0:1],
                                         Alu.mult))
                vsync(v, v.tensor_tensor(st[:, 2:3], st[:, 1:2], st[:, 2:3],
                                         Alu.subtract))
                fn(v.tensor_single_scalar(st[:, 3:4], st[:, 2:3], EPS, Alu.add))
            E("vector", _)
            st2a = mk.val["ST2"]
            W("scalar", "ST2", st2a)
            fn = I("SQ2", 1)

            def _(sc, bp=bp, fn=fn):
                st = stats[:, 32 + 8 * bp:32 + 8 * (bp + 1)]
                fn(sc.activation(st[:, 4:5], st[:, 3:4], Act.Sqrt))
            E("scalar", _)
            sq2v = mk.val["SQ2"]
            W("vector", "SQ2", sq2v)
            W("vector", "LN2A", g4 - 1)
            fn = I("ST2", 1)

            def _(v, bp=bp, fn=fn):
                st = stats[:, 32 + 8 * bp:32 + 8 * (bp + 1)]
                vsync(v, v.reciprocal(st[:, 5:6], st[:, 4:5]))
                vsync(v, v.tensor_tensor(st[:, 5:6], st[:, 3:4], st[:, 5:6],
                                         Alu.mult))
                vsync(v, v.tensor_tensor(st[:, 4:5], st[:, 4:5], st[:, 5:6],
                                         Alu.add))
                vsync(v, v.tensor_single_scalar(st[:, 4:5], st[:, 4:5], 0.5,
                                                Alu.mult))
                vsync(v, v.reciprocal(st[:, 5:6], st[:, 4:5]))
                vsync(v, v.tensor_tensor(st[:, 6:7], st[:, 0:1], st[:, 5:6],
                                         Alu.mult))
                fn(v.tensor_single_scalar(st[:, 6:7], st[:, 6:7], -1.0,
                                          Alu.mult))
            E("vector", _)
            st2b = mk.val["ST2"]
            W("scalar", "ST2", st2b)
            fn = I("LN2A", 1)

            def _(sc, tt=tt, bp=bp, st4=st4, fn=fn):
                st = stats[:, 32 + 8 * bp:32 + 8 * (bp + 1)]
                fn(sc.activation(fslice(R_LN2, 2048),
                                 fslice(R_R1P[st4 % 2] + tt * 2048, 2048),
                                 Act.Identity,
                                 bias=st[:, 6:7], scale=st[:, 5:6]))
            E("scalar", _)
            ln2v = mk.val["LN2A"]
            emit_qd(2048,
                    fslice(R_LN2, 2048),
                    bslice(R_ASTG + bp * 2048, 2048),
                    R_QDS,
                    R_TQ, R_TA, R_TB, R_TS,
                    v_pre_waits=[("LN2A", ln2v), ("AW", 16 * (g4 - 1))],
                    done_inc="QDV")
        if st4 >= 1:
            for tt in range(4):
                g4p = 4 * (st4 - 1) + tt
                ti = tt % 2
                W("vector", "FFT", ff_done_at[(st4 - 1, tt)])
                W("vector", "YW", 16 * (g4p - 1))
                fn = I("FO", 1)

                def _(v, tt=tt, ti=ti, st4=st4, fn=fn):
                    reg = psum[:, 2048 * ti:2048 * (ti + 1)]
                    out = fslice(R_OUT[tt % 2], 2048)
                    v.tensor_tensor(out, reg, bias_sb[:], Alu.add)
                    fn(v.tensor_tensor(
                        out, out,
                        fslice(R_R1P[(st4 - 1) % 2] + tt * 2048, 2048),
                        Alu.add))
                E("vector", _)
    for tt in range(4):
        g4p = 4 * (P4N - 1) + tt
        ti = tt % 2
        W("vector", "FFT", ff_done_at[(P4N - 1, tt)])
        W("vector", "YW", 16 * (g4p - 1))
        fn = I("FO", 1)

        def _(v, tt=tt, ti=ti, fn=fn):
            reg = psum[:, 2048 * ti:2048 * (ti + 1)]
            out = fslice(R_OUT[tt % 2], 2048)
            v.tensor_tensor(out, reg, bias_sb[:], Alu.add)
            fn(v.tensor_tensor(out, out,
                               fslice(R_R1P[1] + tt * 2048, 2048), Alu.add))
        E("vector", _)

    W("gpsimd", "YW", 64 * P4N)
    mk.check_deadlock()
    mk.mark("full")
    import os as _os
    cut = _os.environ.get("MK_PHASE_CUT")
    if cut:
        m = mk.marks[cut]
        for e in mk.thunks:
            mk.thunks[e] = mk.thunks[e][:m[e]]
    with nc.Block() as block:
        block.sync(lambda e: [f(e) for f in mk.thunks["sync"]])
        block.gpsimd(lambda e: [f(e) for f in mk.thunks["gpsimd"]])
        block.tensor(lambda e: [f(e) for f in mk.thunks["tensor"]])
        block.vector(lambda e: [f(e) for f in mk.thunks["vector"]])
        block.scalar(lambda e: [f(e) for f in mk.thunks["scalar"]])
    es.close()
    return nc


_NC_CACHE = None


def kernel(L0_attn_hidden_states, L0_attn_Wq, L0_attn_Wk, L0_attn_Wv,
           L0_attn_Wo, L0_ffn_weight, L0_ffn_bias):
    global _NC_CACHE
    x = np.ascontiguousarray(np.asarray(L0_attn_hidden_states, dtype=np.float32)
                             .reshape(B * S, H))
    wq = np.asarray(L0_attn_Wq, dtype=np.float32)
    wk = np.asarray(L0_attn_Wk, dtype=np.float32)
    wv = np.asarray(L0_attn_Wv, dtype=np.float32)
    wo = np.asarray(L0_attn_Wo, dtype=np.float32)
    wf = np.asarray(L0_ffn_weight, dtype=np.float32)
    bias = np.asarray(L0_ffn_bias, dtype=np.float32)

    def tiled(w):
        wt = w.T.reshape(NHT, 128, NHT, 128)
        return np.ascontiguousarray(
            wt.transpose(2, 1, 0, 3).reshape(NHT, 128, H))

    wq_tl, wk_tl, wv_tl = tiled(wq), tiled(wk), tiled(wv)
    wo_t = np.ascontiguousarray(wo.T)
    bias_rep = np.ascontiguousarray(np.broadcast_to(bias[None, :], (128, H)))
    cos, sin = _rope_tables()
    masks = _masks()

    rows = np.arange(128, dtype=np.int32)
    in_maps = []
    for c in range(NCORE):
        gidx = c % 4
        pos0 = 2048 * gidx
        cos_sl = np.ascontiguousarray(cos[pos0:pos0 + TOK].T)
        sin_sl_t = sin[pos0:pos0 + TOK].T
        sin_sl = np.ascontiguousarray(
            np.concatenate([-sin_sl_t[:64], sin_sl_t[64:]], axis=0))
        qsel = np.zeros((128, 48), dtype=np.int32)
        for hi in range(4):
            for sec in range(3):
                for r in range(4):
                    qsel[:, 12 * hi + 4 * sec + r] = (
                        6144 * r + 2048 * sec + 128 * (4 * gidx + hi) + rows)
        aosel_ix = np.zeros((128, 64), dtype=np.int32)
        for st4 in range(4):
            for r in range(4):
                for q4 in range(4):
                    aosel_ix[:, 16 * st4 + 4 * r + q4] = (
                        8192 * r + 2048 * gidx + 512 * st4 + 128 * q4 + rows)
        in_maps.append({
            "x_sl": np.ascontiguousarray(x[2048 * c:2048 * (c + 1)]),
            "wq_tl": wq_tl, "wk_tl": wk_tl, "wv_tl": wv_tl,
            "wo_t": wo_t,
            "wffn_sl": np.ascontiguousarray(wf[:, 512 * gidx:512 * (gidx + 1)]),
            "bias_b": bias_rep,
            "cos_t": cos_sl, "sin_t": sin_sl,
            "masks_i": masks,
            "qsel_i": qsel, "aosel_i": aosel_ix,
        })

    if _NC_CACHE is None:
        _NC_CACHE = build_program()
    res = run_bass_kernel_spmd(_NC_CACHE, in_maps, core_ids=list(range(NCORE)))
    out = np.concatenate([res.results[c]["y_sl"] for c in range(NCORE)], axis=0)
    return out.reshape(B, S, H).astype(np.float32)
